# version 06
# baseline (speedup 1.0000x reference)
import numpy as np
from contextlib import ExitStack

import concourse.bass as bass
import concourse.mybir as mybir
from concourse.bass_utils import run_bass_kernel_spmd
import ml_dtypes

F32 = mybir.dt.float32
BF16 = mybir.dt.bfloat16
AF = mybir.ActivationFunctionType
ALU = mybir.AluOpType
AX = mybir.AxisListType

NCORES = 8
S = 4096
D = 1024
NT = S // 128
NG = S // 512
NH, HD = 16, 64
FF = 2048
HID = 4096
EPS = 1e-5

ENGINES = ("sync", "tensor", "vector", "scalar", "gpsimd")


class Prog:

    def __init__(self, nc):
        self.nc = nc
        self.thunks = {e: [] for e in ENGINES}
        self.counts = {}
        self.deferred = {e: [] for e in ENGINES}
        self.ops_since_barrier = {e: 0 for e in ENGINES}
        self.engine_sem = {"tensor": "g_pe", "vector": "g_dve", "scalar": "g_act",
                          "gpsimd": "g_pool", "sync": "g_sy"}
        for s in self.engine_sem.values():
            self.sem(s)
        self.dma_sems = set()
        self.dmy = None

    _SEM_MAP = [
        ("_qlda", "lpA0"), ("_qldb", "lpA1"), ("_qldg", "s_ldg"),
        ("_qla", "lpA0"), ("_qlb", "lpA1"), ("_kla", "lpB0"), ("_klb", "lpB1"),
        ("_vla", "lpC0"), ("_vlb", "lpC1"), ("_khl", "lpB0"), ("_khm", "lpB1"),
        ("_vhl", "lpC0"), ("_vhm", "lpC1"),
        ("_lha", "lpA0"), ("_lhb", "lpA1"), ("_lxa", "lpC0"), ("_lxb", "lpC1"),
        ("_lda", "lpA0"), ("_ldb", "lpA1"), ("_ldi", "s_ldc"), ("_ldc", "s_ldc"),
        ("_ldg", "s_ldg"), ("_qld", "s_ldc"), ("_lx", "lpB0"),
        ("_la", "lpA0"), ("_lb", "lpA1"), ("_ld", "s_ldx"),
        ("_qd", "s_d"), ("_qa", "s_a"), ("_qev", "s_ev"), ("_qpe", "s_pe"),
        ("_pe", "s_pe"), ("_ev", "s_ev"), ("_pl", "s_pl"),
        ("_d", "s_d"), ("_a", "s_a"), ("_y", "s_y"),
    ]

    def _canon(self, name):
        if name.startswith("g_") or name in ("st", "stg", "ldw"):
            return name
        if name == "ldwg":
            return "s_ldg"
        if "_ss" in name:
            return "sl" + name[name.index("_ss"):]
        for suf, c in self._SEM_MAP:
            if name.endswith(suf):
                return c
        return name

    def sem(self, name):
        name = self._canon(name)
        if name not in self.counts:
            self.counts[name] = 0
        return name

    def wait(self, eng, sem, val):
        sem = self._canon(sem)
        assert sem in self.counts, sem
        assert val <= self.counts[sem], (sem, val, self.counts[sem])
        if val > 0:
            self.thunks[eng].append(("wait", sem, val))

    def op(self, eng, fn, inc=None, small=False):
        inc = self._canon(inc) if inc is not None else None
        """Emit one engine op, optionally incrementing `inc` on completion.

        small=True: the op produces a tiny (reduce/stat) output — also emit a
        trailing dummy op carrying a second increment, so hardware consumers
        (which may observe the first inc before writeback lands) get extra
        slack, while the simulator's race model is satisfied by the first.
        Returns the count the consumer should wait for."""
        incs = []
        ret = None
        if inc is not None:
            self.sem(inc)
            self.counts[inc] += 1
            ret = self.counts[inc]
            incs.append((inc, 1))
        self.thunks[eng].append(["op", fn, incs])
        self.ops_since_barrier[eng] += 1
        if small and inc is not None:
            ret = self.op(eng, self._drain_fn(eng), inc=inc)
        return ret

    def dma(self, eng, sem, fn):
        sem = self._canon(sem)
        self.sem(sem)
        self.dma_sems.add(sem)
        self.counts[sem] += 16
        self.thunks[eng].append(["op", fn, [(sem, 16)]])
        self.ops_since_barrier[eng] += 1
        return self.counts[sem]

    def _dummy_fn(self, eng):
        return lambda e: e.nop()

    def _drain_fn(self, eng):
        if eng == "sync":
            return lambda e: e.nop()
        return lambda e: e.drain(fusable=True)

    def flush_deferred(self, eng):
        while self.deferred[eng]:
            self.op(eng, self._dummy_fn(eng))

    def _last_real_op(self, eng):
        for item in reversed(self.thunks[eng]):
            if item[0] == "op":
                return item
        return None

    def barrier(self):
        active = [e for e in ENGINES
                  if self.ops_since_barrier[e] > 0 or self.deferred[e]]
        for e in active:
            gs = self.engine_sem[e]
            self.op(e, self._drain_fn(e), inc=gs)
        targets = [(self.engine_sem[e], self.counts[self.engine_sem[e]]) for e in active]
        dmas = [(s, self.counts[s]) for s in sorted(self.dma_sems) if self.counts[s] > 0]
        for e in ENGINES:
            for s, v in targets:
                if self.engine_sem[e] == s:
                    continue
                self.wait(e, s, v)
            for s, v in dmas:
                self.wait(e, s, v)
        for e in ENGINES:
            self.ops_since_barrier[e] = 0

    def finish(self):
        for e in ENGINES:
            self.flush_deferred(e)
        for s in sorted(self.dma_sems):
            if self.counts[s] > 0:
                self.wait("sync", s, self.counts[s])

    def build(self, block, h):
        for ename in ENGINES:
            thunks = self.thunks[ename]
            if not thunks:
                continue

            def body(engine, thunks=thunks):
                for item in thunks:
                    if item[0] == "wait":
                        engine.wait_ge(h[item[1]], item[2])
                    else:
                        ins = item[1](engine)
                        assert len(item[2]) <= 1, item[2]
                        for s, n in item[2]:
                            ins.then_inc(h[s], n)

            getattr(block, ename)(body)


class Banks:

    def __init__(self, p, psums):
        self.p = p
        self.banks = [[ps, None] for ps in psums]
        self.i = 0

    def next(self):
        b = self.banks[self.i % len(self.banks)]
        self.i += 1
        if b[1] is not None:
            self.p.wait("tensor", b[1][0], b[1][1])
            b[1] = None
        return b

    @staticmethod
    def release(b, sem, count):
        b[1] = (sem, count)


class StSlots:

    def __init__(self, p, tag, n):
        self.p = p
        self.sems = [p.sem(f"{tag}_ss{i}") for i in range(n)]
        self.last = [0] * n

    def guard(self, eng, i):
        if self.last[i] > 0:
            self.p.wait(eng, self.sems[i], self.last[i])

    def store(self, i, fn):
        self.last[i] = self.p.dma("sync", self.sems[i], fn)


def load_bias_cols(p, sem, dst, src2d, ncols):
    last = None
    for c in range(ncols):
        last = p.dma("sync", sem, lambda s, c=c: s.dma_start(
            out=dst[:, c:c + 1],
            in_=src2d[0:1, c * 128:(c + 1) * 128].rearrange("o (p q) -> (o p) q", q=1)))
    return last


def mm_acc(p, out_ap, pairs, sem_pe):
    n = len(pairs)
    for i, (l, r) in enumerate(pairs):
        p.op("tensor",
             lambda t, l=l, r=r, st=(i == 0), sp=(i == n - 1), o=out_ap:
             t.matmul(o, l, r, start=st, stop=sp),
             inc=(sem_pe if i == n - 1 else None))
    return p.counts[sem_pe]



SCRATCH = {
    "xn_f32": ((S, D), F32),
    "xn_bf": ((S, D), BF16),
    "qf": ((D, S), BF16),
    "kf": ((D, S), BF16),
    "vt": ((S, D), BF16),
    "of": ((D, S), BF16),
    "attn_br": ((S, D), F32),
    "x1_f32": ((S, D), F32),
    "x1_bf": ((S, D), BF16),
    "xres": ((S, D), F32),
    "xn2_bf": ((S, D), BF16),
    "h1_bf": ((S, D), BF16),
    "attn2": ((S, D), F32),
    "wqT_src": ((D, D), BF16),
    "wkT_src": ((D, D), BF16),
    "wvT_src": ((D, D), BF16),
    "woT_src": ((D, D), BF16),
    "h2f": ((NT, HID // 128, 128, 128), BF16),
    "ymha": ((S, D), F32),
    "ymoe": ((S, D), F32),
    "out": ((S, D), F32),
    "sc1": ((1, 8), F32),
}

INPUT_SPECS = {
    "x": ((S, D), F32),
    "Wqkv": ((D, 3 * D), F32), "bqkv": ((1, 3 * D), F32),
    "Wproj": ((D, D), F32), "bproj": ((1, D), F32), "gamma": ((1, 1), F32),
    "qkv_w": ((D, 3 * D), F32), "qkv_b": ((1, 3 * D), F32),
    "out_w": ((D, D), F32), "out_b": ((1, D), F32),
    "norm1_w": ((1, D), F32), "norm1_b": ((1, D), F32),
    "sw1": ((D, FF), F32), "sb1": ((1, FF), F32),
    "sw2": ((FF, D), F32), "sb2": ((1, D), F32),
    "rw1": ((8, D, FF), F32), "rb1": ((8, FF), F32),
    "rw2": ((8, FF, D), F32), "rb2": ((8, D), F32),
    "centroid": ((8, D), F32), "route_bias": ((1, 8), F32),
    "norm2_w": ((1, D), F32), "norm2_b": ((1, D), F32),
    "Wq": ((D, D), F32), "Wk": ((D, D), F32), "Wv": ((D, D), F32), "Wo": ((D, D), F32),
    "ln1_g": ((1, D), F32), "ln1_b": ((1, D), F32),
    "w1": ((HID, D), F32), "b1": ((1, HID), F32),
    "ln2_g": ((1, HID), F32), "ln2_b": ((1, HID), F32),
    "w2": ((D, HID), F32), "b2": ((1, D), F32),
    "ident_f": ((128, 128), F32),
    "diffT": ((128, 128), F32), "triT": ((128, 128), F32),
    "ones_f": ((1, 128), F32), "onescol_b": ((128, 1), BF16),
    "triT_b": ((128, 128), BF16), "neg20": ((1, 1), F32),
}


def make_consts():
    i = np.arange(128)
    return {
        "ident_f": np.eye(128, dtype=np.float32),
        "diffT": (i[None, :] - i[:, None]).astype(np.float32),
        "triT": (i[None, :] >= i[:, None]).astype(np.float32),
        "ones_f": np.ones((1, 128), np.float32),
        "onescol_b": np.ones((128, 1)).astype(ml_dtypes.bfloat16),
        "triT_b": (i[None, :] >= i[:, None]).astype(ml_dtypes.bfloat16),
        "neg20": np.full((1, 1), -20.0, np.float32),
    }



def emit_ln_stream(p, io, cst, scope, tag, src_name, dst_f32, dst_bf,
                   extra_ln=None):
    nc = p.nc
    e = scope.enter_context
    st = e(nc.sbuf_tensor(tag + "_stat", [128, 64], F32))
    xt = [e(nc.sbuf_tensor(f"{tag}_xt{j}", [128, D], F32)) for j in range(4)]
    xm = [e(nc.sbuf_tensor(f"{tag}_xm{j}", [128, D], F32)) for j in range(4)]
    xnb = [e(nc.sbuf_tensor(f"{tag}_xnb{j}", [128, D], F32)) for j in range(4)]
    bf = [e(nc.sbuf_tensor(f"{tag}_bf{j}", [128, D], BF16)) for j in range(4)]
    sqd = [e(nc.sbuf_tensor(f"{tag}_sqd{j}", [128, D], BF16)) for j in range(4)]
    if extra_ln is not None:
        g_name, b_name, dst2_bf = extra_ln
        gb = e(nc.sbuf_tensor(tag + "_g", [128, D], F32))
        bb = e(nc.sbuf_tensor(tag + "_b", [128, D], F32))
        h2 = [e(nc.sbuf_tensor(f"{tag}_h2{j}", [128, D], F32)) for j in range(4)]
        bf2 = [e(nc.sbuf_tensor(f"{tag}_bf2{j}", [128, D], BF16)) for j in range(4)]
        ldw = p.sem("ldw")
        p.dma("sync", ldw, lambda s: s.dma_start(
            out=gb[:], in_=io[g_name].partition_broadcast(128)))
        wv = p.dma("sync", ldw, lambda s: s.dma_start(
            out=bb[:], in_=io[b_name].partition_broadcast(128)))
        p.wait("vector", ldw, wv)

    sem_ld = p.sem(tag + "_ld")
    sem_d = p.sem(tag + "_d")
    sem_a = p.sem(tag + "_a")
    sem_st = p.sem("st")
    src = io[src_name] if isinstance(src_name, str) else src_name

    def one_ln(tiles_in, out_tiles, colbase):
        cb = colbase
        cnm, cxm, csq = {}, {}, {}
        for j in range(4):
            cnm[j] = p.op("scalar",
                          lambda s, j=j: s.activation(sqd[j][:], tiles_in[j][:],
                                                      AF.Copy, scale=-1.0 / D,
                                                      accum_out=st[:, cb + j:cb + j + 1]),
                          inc=sem_a, small=True)
        for j in range(4):
            p.wait("vector", sem_a, cnm[j])
            cxm[j] = p.op("vector",
                          lambda v, j=j: v.tensor_scalar(xm[j][:], tiles_in[j][:],
                                                         st[:, cb + j:cb + j + 1],
                                                         None, op0=ALU.add),
                          inc=sem_d, small=True)
        for j in range(4):
            p.wait("scalar", sem_d, cxm[j])
            csq[j] = p.op("scalar",
                          lambda s, j=j: s.activation(sqd[j][:], xm[j][:], AF.Square,
                                                      accum_out=st[:, cb + 8 + j:cb + 9 + j]),
                          inc=sem_a, small=True)
        p.wait("vector", sem_a, csq[3])
        cv = p.op("vector",
                  lambda v: v.tensor_scalar(st[:, cb + 16:cb + 20],
                                            st[:, cb + 8:cb + 12],
                                            1.0 / D, EPS, op0=ALU.mult, op1=ALU.add),
                  inc=sem_d, small=True)
        p.wait("scalar", sem_d, cv)
        cl = p.op("scalar", lambda s: s.activation(st[:, cb + 20:cb + 24],
                                                   st[:, cb + 16:cb + 20], AF.Ln),
                  inc=sem_a, small=True)
        p.wait("vector", sem_a, cl)
        cln = p.op("vector",
                   lambda v: v.tensor_scalar(st[:, cb + 16:cb + 20],
                                             st[:, cb + 20:cb + 24],
                                             -0.5, None, op0=ALU.mult),
                   inc=sem_d, small=True)
        p.wait("scalar", sem_d, cln)
        ca = p.op("scalar", lambda s: s.activation(st[:, cb + 24:cb + 28],
                                                   st[:, cb + 16:cb + 20], AF.Exp),
                  inc=sem_a, small=True)
        cout = {}
        for j in range(4):
            p.wait("vector", sem_a, ca)
            cout[j] = p.op("vector",
                           lambda v, j=j: v.tensor_scalar(out_tiles[j][:], xm[j][:],
                                                          st[:, cb + 24 + j:cb + 25 + j],
                                                          None, op0=ALU.mult),
                           inc=sem_d, small=True)
        return cout

    for g in range(NG):
        for j in range(4):
            tt = g * 4 + j
            c = p.dma("sync", sem_ld, lambda s, tt=tt, j=j: s.dma_start(
                out=xt[j][:], in_=src[tt * 128:(tt + 1) * 128, :]))
        p.wait("vector", sem_ld, c)
        p.wait("scalar", sem_ld, c)
        cout = one_ln(xt, xnb, 0)
        for j in range(4):
            tt = g * 4 + j
            if dst_f32 is not None:
                p.wait("sync", sem_d, cout[j])
                p.dma("sync", sem_st, lambda s, tt=tt, j=j: s.dma_start(
                    out=io[dst_f32][tt * 128:(tt + 1) * 128, :], in_=xnb[j][:]))
            p.wait("scalar", sem_d, cout[j])
            cc = p.op("scalar", lambda s, j=j: s.copy(bf[j][:], xnb[j][:]),
                      inc=sem_a)
            p.wait("sync", sem_a, cc)
            p.dma("sync", sem_st, lambda s, tt=tt, j=j: s.dma_start(
                out=io[dst_bf][tt * 128:(tt + 1) * 128, :], in_=bf[j][:]))
        if extra_ln is not None:
            cout2 = one_ln(xnb, h2, 32)
            for j in range(4):
                tt = g * 4 + j
                p.wait("gpsimd", sem_d, cout2[j])
                cw = p.op("gpsimd", lambda v, j=j: v.tensor_mul(h2[j][:], h2[j][:], gb[:]),
                          inc="g_pool", small=True)
                p.wait("gpsimd", "g_pool", cw)
                cw = p.op("gpsimd", lambda v, j=j: v.tensor_add(h2[j][:], h2[j][:], bb[:]),
                          inc="g_pool", small=True)
                p.wait("scalar", "g_pool", cw)
                cc = p.op("scalar", lambda s, j=j: s.copy(bf2[j][:], h2[j][:]),
                          inc=sem_a)
                p.wait("sync", sem_a, cc)
                p.dma("sync", sem_st, lambda s, tt=tt, j=j: s.dma_start(
                    out=io[dst2_bf][tt * 128:(tt + 1) * 128, :], in_=bf2[j][:]))
        p.barrier()


def phase_ln0(p, io, cst, scope):
    emit_ln_stream(p, io, cst, scope, "ln0", "x", "xn_f32", "xn_bf")


phase_ln0.reads = ("x",)
phase_ln0.writes = ("xn_f32", "xn_bf")



def emit_qkv(p, io, cst, scope, tag, src_bf, w_name, b_name):
    nc = p.nc
    e = scope.enter_context
    wsb = [e(nc.sbuf_tensor(f"{tag}_w{dc}", [128, 3 * D], BF16)) for dc in range(8)]
    bcols = e(nc.sbuf_tensor(f"{tag}_bc", [128, 24], F32))
    bvb = e(nc.sbuf_tensor(f"{tag}_bvb", [128, D], F32))
    xfm = [[e(nc.sbuf_tensor(f"{tag}_x{pr}{dc}", [128, 512], BF16)) for dc in range(8)]
           for pr in range(2)]
    stage = [e(nc.sbuf_tensor(f"{tag}_s{i}", [128, 512], BF16)) for i in range(6)]
    pms = [e(nc.psum_tensor(f"{tag}_p{i}", [128, 512], F32)) for i in range(4)]
    banks = Banks(p, pms)

    ldw = p.sem("ldw")
    ldwg = p.sem("ldwg")
    for dc in range(8):
        wldg = p.dma("gpsimd", ldwg, lambda g, dc=dc: g.dma_start(
            out=wsb[dc][:], in_=io[w_name][dc * 128:(dc + 1) * 128, :]))
    load_bias_cols(p, ldw, bcols, io[b_name], 24)
    wldv = p.dma("sync", ldw, lambda s: s.dma_start(
        out=bvb[:], in_=io[b_name][:, 2 * D:3 * D].partition_broadcast(128)))

    sem_pe = p.sem(tag + "_pe")
    sem_ev = p.sem(tag + "_ev")
    las = [p.sem(tag + "_la"), p.sem(tag + "_lb")]
    slots = StSlots(p, tag, 6)
    pe_group_end = {}
    waited_w = False
    for g in range(NG):
        la = las[g % 2]
        xf = xfm[g % 2]
        if g >= 2:
            p.wait("sync", sem_pe, pe_group_end[g - 2])
        for dc in range(8):
            p.dma("sync", la, lambda s, g=g, dc=dc, xf=xf: s.dma_start(
                out=xf[dc][:], in_=io[src_bf][g * 512:(g + 1) * 512,
                                              dc * 128:(dc + 1) * 128],
                transpose=True))
        ldv = p.counts[la]
        p.wait("tensor", la, ldv)
        if not waited_w:
            p.wait("tensor", ldw, wldv)
            p.wait("tensor", ldwg, wldg)
            p.wait("scalar", ldw, wldv)
            p.wait("vector", ldw, wldv)
            p.wait("vector", ldwg, wldg)
            waited_w = True
        si = 0
        for mc in range(16):
            b = banks.next()
            cpe = mm_acc(p, b[0][:],
                         [(wsb[dc][:, mc * 128:(mc + 1) * 128], xf[dc][:])
                          for dc in range(8)], sem_pe)
            sj = si % 6
            si += 1
            slots.guard("vector", sj)
            p.wait("vector", sem_pe, cpe)
            cev = p.op("vector", lambda v, mc=mc, sj=sj, b=b: v.tensor_scalar(
                stage[sj][:], b[0][:], bcols[:, mc:mc + 1], None, op0=ALU.add),
                inc=sem_ev)
            Banks.release(b, sem_ev, cev)
            dst = io["qf"] if mc < 8 else io["kf"]
            r = (mc % 8) * 128
            p.wait("sync", sem_ev, cev)
            slots.store(sj, lambda s, dst=dst, r=r, g=g, sj=sj: s.dma_start(
                out=dst[r:r + 128, g * 512:(g + 1) * 512], in_=stage[sj][:]))
        for tt in range(4):
            for half in range(2):
                b = banks.next()
                cpe = mm_acc(p, b[0][:],
                             [(xf[dc][:, tt * 128:(tt + 1) * 128],
                               wsb[dc][:, 2 * D + half * 512: 2 * D + (half + 1) * 512])
                              for dc in range(8)], sem_pe)
                sj = si % 6
                si += 1
                slots.guard("vector", sj)
                p.wait("vector", sem_pe, cpe)
                cev = p.op("vector", lambda v, sj=sj, b=b, half=half: v.tensor_tensor(
                    stage[sj][:], b[0][:], bvb[:, half * 512:(half + 1) * 512],
                    op=ALU.add), inc=sem_ev)
                Banks.release(b, sem_ev, cev)
                row = (g * 4 + tt) * 128
                p.wait("sync", sem_ev, cev)
                slots.store(sj, lambda s, row=row, half=half, sj=sj: s.dma_start(
                    out=io["vt"][row:row + 128, half * 512:(half + 1) * 512],
                    in_=stage[sj][:]))
        pe_group_end[g] = p.counts[sem_pe]
    p.barrier()


def phase_rqkv(p, io, cst, scope):
    emit_qkv(p, io, cst, scope, "rqkv", "xn_bf", "Wqkv", "bqkv")


phase_rqkv.reads = ("xn_bf", "Wqkv", "bqkv")
phase_rqkv.writes = ("qf", "kf", "vt")


def phase_mqkv(p, io, cst, scope):
    emit_qkv(p, io, cst, scope, "mqkv", "xn_bf", "qkv_w", "qkv_b")


phase_mqkv.reads = ("xn_bf", "qkv_w", "qkv_b")
phase_mqkv.writes = ("qf", "kf", "vt")



def phase_rattn(p, io, cst, scope):
    nc = p.nc
    e = scope.enter_context
    Pb = e(nc.sbuf_tensor("ra_P", [128, 32 * 512], BF16))
    Sd = [e(nc.sbuf_tensor(f"ra_Sd{i}", [128, 512], F32)) for i in range(2)]
    qg = [[e(nc.sbuf_tensor(f"ra_q{pr}{dc}", [128, 512], BF16)) for dc in range(8)]
          for pr in range(2)]
    kblk = [e(nc.sbuf_tensor(f"ra_k{i}", [128, D], BF16)) for i in range(2)]
    vblk = [e(nc.sbuf_tensor(f"ra_v{i}", [128, D], BF16)) for i in range(2)]
    maskT = e(nc.sbuf_tensor("ra_mask", [128, 128], F32))
    diffT = e(nc.sbuf_tensor("ra_diff", [128, 128], F32))
    triT = e(nc.sbuf_tensor("ra_tri", [128, 128], F32))
    onesf = e(nc.sbuf_tensor("ra_onesf", [1, 128], F32))
    onescb = e(nc.sbuf_tensor("ra_onescb", [128, 1], BF16))
    gt = e(nc.sbuf_tensor("ra_gt", [1, 8], F32))
    lngb = e(nc.sbuf_tensor("ra_lngb", [128, 1], F32))
    m0 = e(nc.sbuf_tensor("ra_m0", [128, 128], F32))
    n20 = e(nc.sbuf_tensor("ra_n20", [128, 1], F32))
    rowr = e(nc.sbuf_tensor("ra_rowr", [1, 512], F32))
    rbc = e(nc.sbuf_tensor("ra_rbc", [128, 512], F32))
    ost = [e(nc.sbuf_tensor(f"ra_o{i}", [128, 512], BF16)) for i in range(4)]
    pms = [e(nc.psum_tensor(f"ra_p{i}", [128, 512], F32)) for i in range(8)]
    ptr = None

    ldc = p.sem("ra_ldc")
    sem_pe = p.sem("ra_pe")
    sem_ev = p.sem("ra_ev")
    sem_d = p.sem("ra_d")
    sem_a = p.sem("ra_a")
    sem_pl = p.sem("ra_pl")
    sem_st = p.sem("st")
    sem_mk = p.sem("ra_mk")

    for buf, src in ((diffT, "diffT"), (triT, "triT"),
                     (onesf, "ones_f"), (onescb, "onescol_b")):
        p.dma("sync", ldc, lambda s, buf=buf, src=src: s.dma_start(out=buf[:], in_=io[src]))
    p.dma("sync", ldc, lambda s: s.dma_start(
        out=n20[:], in_=io["neg20"].partition_broadcast(128)))
    cg = p.dma("sync", ldc, lambda s: s.dma_start(out=gt[0:1, 0:1], in_=io["gamma"]))

    p.wait("scalar", ldc, cg)
    ca = p.op("scalar", lambda s: s.activation(gt[0:1, 1:2], gt[0:1, 0:1], AF.Ln),
              inc=sem_a, small=True)
    p.wait("sync", sem_a, ca)
    cs1 = p.dma("sync", sem_mk, lambda s: s.dma_start(out=io["sc1"][0:1, 0:1],
                                                      in_=gt[0:1, 1:2]))
    p.wait("sync", sem_mk, cs1)
    cs2 = p.dma("sync", sem_mk, lambda s: s.dma_start(
        out=lngb[:], in_=io["sc1"][0:1, 0:1].partition_broadcast(128)))
    p.wait("vector", sem_mk, cs2)
    p.wait("vector", ldc, p.counts[ldc])
    cd = p.op("vector", lambda v: v.tensor_scalar(m0[:], diffT[:], lngb[:, 0:1],
                                                  None, op0=ALU.mult),
              inc=sem_d, small=True)
    p.wait("scalar", sem_d, cd)
    ca = p.op("scalar", lambda s: s.activation(m0[:], m0[:], AF.Exp),
              inc=sem_a, small=True)
    p.wait("vector", sem_a, ca)
    cm = p.op("vector", lambda v: v.tensor_mul(maskT[:], m0[:], triT[:]),
              inc=sem_d, small=True)
    mask_ready = cm

    qlas = [p.sem("ra_qla"), p.sem("ra_qlb")]
    klas = [p.sem("ra_kla"), p.sem("ra_klb")]
    vlas = [p.sem("ra_vla"), p.sem("ra_vlb")]
    kf_r = io["kf"].rearrange("(a p) s -> p a s", p=128)
    pe_group_end = {}
    pe_kc = {}
    pe_pv_kc = {}
    ev_group_end = {}
    dve_pv_end = {}
    exp_count = {}
    ev_of_bank = {}
    oslots = StSlots(p, "ra_o", 4)
    waited_mask = False
    for g in range(NG):
        qla = qlas[g % 2]
        qf = qg[g % 2]
        if g >= 2:
            p.wait("sync", sem_pe, pe_group_end[g - 2])
        if g > 0:
            p.wait("tensor", sem_d, dve_pv_end[g - 1])
        for dc in range(8):
            p.dma("sync", qla, lambda s, g=g, dc=dc, qf=qf: s.dma_start(
                out=qf[dc][:], in_=io["qf"][dc * 128:(dc + 1) * 128,
                                            g * 512:(g + 1) * 512]))
        qldv = p.counts[qla]
        p.wait("tensor", qla, qldv)
        if not waited_mask:
            p.wait("gpsimd", sem_d, mask_ready)
            waited_mask = True
        if g > 0:
            p.wait("scalar", sem_pe, pe_group_end[g - 1])
        cexp = {}
        for kc in range(32):
            kla = klas[kc % 2]
            kb = kblk[kc % 2]
            guard = pe_kc.get((g, kc - 2)) or (pe_group_end.get(g - 1) if kc < 2 else None)
            if guard is not None:
                p.wait("sync", sem_pe, guard)
            p.dma("sync", kla, lambda s, kc=kc, kb=kb: s.dma_start(
                out=kb[:].rearrange("p (a f) -> p a f", f=128),
                in_=kf_r[:, :, kc * 128:(kc + 1) * 128]))
            p.wait("tensor", kla, p.counts[kla])
            b2 = pms[kc % 2]
            if (kc % 2) in ev_of_bank:
                p.wait("tensor", sem_a, ev_of_bank[kc % 2])
            cpe = mm_acc(p, b2[:],
                         [(kb[:, dc * 128:(dc + 1) * 128], qf[dc][:])
                          for dc in range(8)], sem_pe)
            pe_kc[(g, kc)] = cpe
            p.wait("scalar", sem_pe, cpe)
            if 4 * g <= kc < 4 * g + 4:
                tt = kc - 4 * g
                sdj = kc % 2
                cev = p.op("scalar", lambda s, b2=b2, sdj=sdj: s.activation(
                    Sd[sdj][:], b2[:], AF.Copy, scale=1.0 / 32.0), inc=sem_a)
                ev_of_bank[kc % 2] = cev
                p.wait("gpsimd", sem_a, cev)
                cpl = p.op("gpsimd", lambda gp, tt=tt, sdj=sdj: gp.tensor_mul(
                    Sd[sdj][:, tt * 128:(tt + 1) * 128],
                    Sd[sdj][:, tt * 128:(tt + 1) * 128],
                    maskT[:]), inc=sem_pl, small=True)
                p.wait("scalar", sem_pl, cpl)
                cexp[kc] = p.op("scalar", lambda s, kc=kc, sdj=sdj: s.activation(
                    Pb[:, kc * 512:(kc + 1) * 512], Sd[sdj][:], AF.Exp,
                    bias=n20[:, 0:1]), inc=sem_a)
            else:
                cexp[kc] = p.op("scalar", lambda s, kc=kc, b2=b2: s.activation(
                    Pb[:, kc * 512:(kc + 1) * 512], b2[:], AF.Exp,
                    bias=n20[:, 0:1], scale=1.0 / 32.0), inc=sem_a)
                ev_of_bank[kc % 2] = cexp[kc]
        exp_count[g] = cexp[31]
        for kc in range(32):
            p.wait("tensor", sem_a, cexp[kc])
            p.op("tensor", lambda t, kc=kc: t.matmul(
                pms[2][0:1, :], onescb[:], Pb[:, kc * 512:(kc + 1) * 512],
                start=(kc == 0), stop=(kc == 31)),
                inc=(sem_pe if kc == 31 else None))
        cpe = p.counts[sem_pe]
        p.wait("vector", sem_pe, cpe)
        cd = p.op("vector", lambda v: v.reciprocal(rowr[0:1, :], pms[2][0:1, :]),
                  inc=sem_d, small=True)
        p.wait("tensor", sem_d, cd)
        cpe = p.op("tensor", lambda t: t.matmul(pms[3][:], onesf[:], rowr[0:1, :],
                                                start=True, stop=True), inc=sem_pe)
        p.wait("scalar", sem_pe, cpe)
        crbc = p.op("scalar", lambda s: s.activation(rbc[:], pms[3][:], AF.Copy),
                    inc=sem_a)
        p.wait("tensor", sem_a, exp_count[g])
        p.wait("tensor", sem_d, cd)
        p.wait("tensor", sem_a, crbc)
        for kc in range(32):
            vla = vlas[kc % 2]
            vb = vblk[kc % 2]
            guard = pe_pv_kc.get((g, kc - 2)) or (pe_group_end.get(g - 1) if kc < 2 else None)
            if guard is not None:
                p.wait("sync", sem_pe, guard)
            row = kc * 128
            p.dma("sync", vla, lambda s, row=row, vb=vb: s.dma_start(
                out=vb[:], in_=io["vt"][row:row + 128, :]))
            p.wait("tensor", vla, p.counts[vla])
            for dc in range(8):
                p.op("tensor", lambda t, kc=kc, dc=dc, vb=vb: t.matmul(
                    pms[dc][:], vb[:, dc * 128:(dc + 1) * 128],
                    Pb[:, kc * 512:(kc + 1) * 512],
                    start=(kc == 0), stop=(kc == 31)),
                    inc=(sem_pe if dc == 7 else None))
            pe_pv_kc[(g, kc)] = p.counts[sem_pe]
        cpe = p.counts[sem_pe]
        for dc in range(8):
            sj = dc % 4
            p.wait("vector", sem_pe, cpe)
            oslots.guard("vector", sj)
            cev = p.op("vector", lambda v, dc=dc, sj=sj: v.tensor_tensor(
                ost[sj][:], pms[dc][:], rbc[:], op=ALU.mult), inc=sem_d)
            p.wait("sync", sem_d, cev)
            oslots.store(sj, lambda s, dc=dc, g=g, sj=sj: s.dma_start(
                out=io["of"][dc * 128:(dc + 1) * 128, g * 512:(g + 1) * 512],
                in_=ost[sj][:]))
        dve_pv_end[g] = p.counts[sem_d]
        pe_group_end[g] = p.counts[sem_pe]
    p.barrier()


phase_rattn.reads = ("qf", "kf", "vt", "gamma", "diffT", "triT",
                     "ones_f", "onescol_b", "sc1", "neg20")
phase_rattn.writes = ("of", "sc1")



def emit_proj(p, io, cst, scope, tag, w_name, b_name, dst, src="of",
              add_src=None):
    nc = p.nc
    e = scope.enter_context
    wsb = [e(nc.sbuf_tensor(f"{tag}_w{dc}", [128, D], BF16)) for dc in range(8)]
    bpb = e(nc.sbuf_tensor(f"{tag}_b", [128, D], F32))
    ofm = [[e(nc.sbuf_tensor(f"{tag}_o{pr}{dc}", [128, 512], BF16)) for dc in range(8)]
           for pr in range(2)]
    stage = [e(nc.sbuf_tensor(f"{tag}_s{i}", [128, 512], F32)) for i in range(4)]
    pms = [e(nc.psum_tensor(f"{tag}_p{i}", [128, 512], F32)) for i in range(4)]
    banks = Banks(p, pms)

    ldw = p.sem("ldw")
    ldwg = p.sem("ldwg")
    for dc in range(8):
        wldg = p.dma("gpsimd", ldwg, lambda g, dc=dc: g.dma_start(
            out=wsb[dc][:], in_=io[w_name][dc * 128:(dc + 1) * 128, :]))
    wldv = p.dma("sync", ldw, lambda s: s.dma_start(
        out=bpb[:], in_=io[b_name].partition_broadcast(128)))

    sem_pe = p.sem(tag + "_pe")
    sem_ev = p.sem(tag + "_ev")
    las = [p.sem(tag + "_la"), p.sem(tag + "_lb")]
    slots = StSlots(p, tag, 4)
    pe_group_end = {}
    waited = False
    for g in range(NG):
        la = las[g % 2]
        of = ofm[g % 2]
        if g >= 2:
            p.wait("sync", sem_pe, pe_group_end[g - 2])
        for dc in range(8):
            p.dma("sync", la, lambda s, g=g, dc=dc, of=of: s.dma_start(
                out=of[dc][:], in_=io[src][dc * 128:(dc + 1) * 128,
                                           g * 512:(g + 1) * 512]))
        ldv = p.counts[la]
        p.wait("tensor", la, ldv)
        if not waited:
            p.wait("tensor", ldw, wldv)
            p.wait("tensor", ldwg, wldg)
            p.wait("vector", ldw, wldv)
            waited = True
        si = 0
        for tt in range(4):
            for half in range(2):
                b = banks.next()
                cpe = mm_acc(p, b[0][:],
                             [(of[dc][:, tt * 128:(tt + 1) * 128],
                               wsb[dc][:, half * 512:(half + 1) * 512])
                              for dc in range(8)], sem_pe)
                sj = si % 4
                si += 1
                slots.guard("vector", sj)
                p.wait("vector", sem_pe, cpe)
                cev = p.op("vector", lambda v, sj=sj, b=b, half=half: v.tensor_tensor(
                    stage[sj][:], b[0][:], bpb[:, half * 512:(half + 1) * 512],
                    op=ALU.add), inc=sem_ev)
                Banks.release(b, sem_ev, cev)
                row = (g * 4 + tt) * 128
                p.wait("sync", sem_ev, cev)
                slots.store(sj, lambda s, row=row, half=half, sj=sj: s.dma_start(
                    out=io[dst][row:row + 128, half * 512:(half + 1) * 512],
                    in_=stage[sj][:]))
        pe_group_end[g] = p.counts[sem_pe]
    p.barrier()


def phase_rproj(p, io, cst, scope):
    emit_proj(p, io, cst, scope, "rproj", "Wproj", "bproj", "attn_br")


phase_rproj.reads = ("of", "Wproj", "bproj")
phase_rproj.writes = ("attn_br",)


def phase_mproj(p, io, cst, scope):
    emit_proj(p, io, cst, scope, "mproj", "out_w", "out_b", "ymha")


phase_mproj.reads = ("of", "out_w", "out_b")
phase_mproj.writes = ("ymha",)



def ln_tiles(p, tag, st, cb, sqd, xm, tiles_in, out_tiles, width, sem_d, sem_a):
    n = len(tiles_in)
    cnm, cxm, csq = {}, {}, {}
    for j in range(n):
        cnm[j] = p.op("scalar",
                      lambda s, j=j: s.activation(sqd[j][:, 0:width],
                                                  tiles_in[j][:, 0:width],
                                                  AF.Copy, scale=-1.0 / width,
                                                  accum_out=st[:, cb + j:cb + j + 1]),
                      inc=sem_a, small=True)
    for j in range(n):
        p.wait("vector", sem_a, cnm[j])
        cxm[j] = p.op("vector",
                      lambda v, j=j: v.tensor_scalar(xm[j][:, 0:width],
                                                     tiles_in[j][:, 0:width],
                                                     st[:, cb + j:cb + j + 1],
                                                     None, op0=ALU.add),
                      inc=sem_d, small=True)
    for j in range(n):
        p.wait("scalar", sem_d, cxm[j])
        csq[j] = p.op("scalar",
                      lambda s, j=j: s.activation(sqd[j][:, 0:width], xm[j][:, 0:width],
                                                  AF.Square,
                                                  accum_out=st[:, cb + 8 + j:cb + 9 + j]),
                      inc=sem_a, small=True)
    p.wait("vector", sem_a, csq[n - 1])
    cv = p.op("vector",
              lambda v: v.tensor_scalar(st[:, cb + 16:cb + 16 + n],
                                        st[:, cb + 8:cb + 8 + n],
                                        1.0 / width, EPS, op0=ALU.mult, op1=ALU.add),
              inc=sem_d, small=True)
    p.wait("scalar", sem_d, cv)
    cl = p.op("scalar", lambda s: s.activation(st[:, cb + 20:cb + 20 + n],
                                               st[:, cb + 16:cb + 16 + n], AF.Ln),
              inc=sem_a, small=True)
    p.wait("vector", sem_a, cl)
    cln = p.op("vector",
               lambda v: v.tensor_scalar(st[:, cb + 16:cb + 16 + n],
                                         st[:, cb + 20:cb + 20 + n],
                                         -0.5, None, op0=ALU.mult),
               inc=sem_d, small=True)
    p.wait("scalar", sem_d, cln)
    ca = p.op("scalar", lambda s: s.activation(st[:, cb + 24:cb + 24 + n],
                                               st[:, cb + 16:cb + 16 + n], AF.Exp),
              inc=sem_a, small=True)
    cout = {}
    for j in range(n):
        p.wait("vector", sem_a, ca)
        cout[j] = p.op("vector",
                       lambda v, j=j: v.tensor_scalar(out_tiles[j][:, 0:width],
                                                      xm[j][:, 0:width],
                                                      st[:, cb + 24 + j:cb + 25 + j],
                                                      None, op0=ALU.mult),
                       inc=sem_d, small=True)
    return cout



def phase_norm1(p, io, cst, scope):
    nc = p.nc
    e = scope.enter_context
    st = e(nc.sbuf_tensor("n1_stat", [128, 32], F32))
    xt = [e(nc.sbuf_tensor(f"n1_xt{j}", [128, D], F32)) for j in range(4)]
    yt = [e(nc.sbuf_tensor(f"n1_yt{j}", [128, D], F32)) for j in range(4)]
    xm = [e(nc.sbuf_tensor(f"n1_xm{j}", [128, D], F32)) for j in range(4)]
    xnb = [e(nc.sbuf_tensor(f"n1_xnb{j}", [128, D], F32)) for j in range(4)]
    sqd = [e(nc.sbuf_tensor(f"n1_sqd{j}", [128, D], BF16)) for j in range(4)]
    gb = e(nc.sbuf_tensor("n1_g", [128, D], F32))
    bb = e(nc.sbuf_tensor("n1_b", [128, D], F32))
    ldw = p.sem("ldw")
    p.dma("sync", ldw, lambda s: s.dma_start(
        out=gb[:], in_=io["norm1_w"].partition_broadcast(128)))
    wv = p.dma("sync", ldw, lambda s: s.dma_start(
        out=bb[:], in_=io["norm1_b"].partition_broadcast(128)))
    p.wait("vector", ldw, wv)
    sem_ld = p.sem("n1_ld")
    sem_d = p.sem("n1_d")
    sem_a = p.sem("n1_a")
    sem_st = p.sem("st")
    for g in range(NG):
        for j in range(4):
            tt = g * 4 + j
            p.dma("sync", sem_ld, lambda s, tt=tt, j=j: s.dma_start(
                out=xt[j][:], in_=io["xn_f32"][tt * 128:(tt + 1) * 128, :]))
            c = p.dma("sync", sem_ld, lambda s, tt=tt, j=j: s.dma_start(
                out=yt[j][:], in_=io["ymha"][tt * 128:(tt + 1) * 128, :]))
        p.wait("vector", sem_ld, c)
        p.wait("scalar", sem_ld, c)
        cs = {}
        for j in range(4):
            cs[j] = p.op("vector", lambda v, j=j: v.tensor_add(xt[j][:], xt[j][:],
                                                               yt[j][:]),
                         inc=sem_d, small=True)
            p.wait("scalar", sem_d, cs[j])
        cout = ln_tiles(p, "n1", st, 0, sqd, xm, xt, xnb, D, sem_d, sem_a)
        for j in range(4):
            tt = g * 4 + j
            p.wait("vector", sem_d, cout[j])
            cm_ = p.op("vector", lambda v, j=j: v.tensor_mul(xnb[j][:], xnb[j][:], gb[:]),
                       inc=sem_d)
            p.wait("vector", sem_d, cm_)
            cw = p.op("vector", lambda v, j=j: v.tensor_add(xnb[j][:], xnb[j][:], bb[:]),
                      inc=sem_d, small=True)
            p.wait("sync", sem_d, cw)
            p.dma("sync", sem_st, lambda s, tt=tt, j=j: s.dma_start(
                out=io["x1_f32"][tt * 128:(tt + 1) * 128, :], in_=xnb[j][:]))
            p.wait("gpsimd", sem_d, cw)
            p.dma("gpsimd", p.sem("stg"), lambda gp, tt=tt, j=j: gp.dma_start(
                out=io["x1_bf"][tt * 128:(tt + 1) * 128, :], in_=xnb[j][:]))
        p.barrier()


phase_norm1.reads = ("xn_f32", "ymha", "norm1_w", "norm1_b")
phase_norm1.writes = ("x1_f32", "x1_bf")



def emit_mha(p, io, cst, scope, tag, causal, sc1col):
    nc = p.nc
    e = scope.enter_context
    Pb2 = [e(nc.sbuf_tensor(f"{tag}_P{i}", [128, 32 * 512], BF16)) for i in range(2)]
    khb = [e(nc.sbuf_tensor(f"{tag}_kh{i}", [64, S], BF16)) for i in range(2)]
    vhb = [e(nc.sbuf_tensor(f"{tag}_vh{i}", [128, 32 * 64], BF16)) for i in range(2)]
    qh = [e(nc.sbuf_tensor(f"{tag}_qh{i}", [64, 512], BF16)) for i in range(2)]
    onesf = e(nc.sbuf_tensor(tag + "_onesf", [1, 128], F32))
    onescb = e(nc.sbuf_tensor(tag + "_onescb", [128, 1], BF16))
    trib = e(nc.sbuf_tensor(tag + "_trib", [128, 128], BF16))
    n20 = e(nc.sbuf_tensor(tag + "_n20", [128, 1], F32))
    rowr = e(nc.sbuf_tensor(tag + "_rowr", [1, 512], F32))
    rbc = e(nc.sbuf_tensor(tag + "_rbc", [64, 512], F32))
    ost = [e(nc.sbuf_tensor(f"{tag}_o{i}", [64, 512], BF16)) for i in range(2)]
    pms = [e(nc.psum_tensor(f"{tag}_p{i}", [128, 512], F32)) for i in range(5)]

    ldc = p.sem(tag + "_ldc")
    sem_pe = p.sem(tag + "_pe")
    sem_d = p.sem(tag + "_d")
    sem_a = p.sem(tag + "_a")
    sem_pl = p.sem(tag + "_pl")
    khla = [p.sem(tag + "_khl"), p.sem(tag + "_khm")]
    vhla = [p.sem(tag + "_vhl"), p.sem(tag + "_vhm")]
    qla = [p.sem(tag + "_qla"), p.sem(tag + "_qlb")]
    oslots = StSlots(p, tag + "o", 2)

    for buf, srcn in ((onesf, "ones_f"), (onescb, "onescol_b"), (trib, "triT_b")):
        cld = p.dma("sync", ldc, lambda s, buf=buf, srcn=srcn: s.dma_start(
            out=buf[:], in_=io[srcn]))
    cld = p.dma("sync", ldc, lambda s: s.dma_start(
        out=n20[:], in_=io["neg20"].partition_broadcast(128)))

    vt_r = io["vt"].rearrange("(kc pp) c -> pp kc c", pp=128)
    iters = [(h, g) for h in range(16) for g in range(NG)]
    s_done = {}
    pv_done = {}
    exp_done = {}
    exp_of_bank = {}
    dve_pc = {}
    kv_loaded = {}
    state = {}
    waited_c = False

    def load_kv(h):
        hp = h % 2
        if h >= 2:
            last_idx = (h - 2) * NG + NG - 1
            p.wait("sync", sem_pe, s_done[last_idx])
            p.wait("sync", sem_pe, pv_done[last_idx])
        ck = p.dma("sync", khla[hp], lambda s, h=h, hp=hp: s.dma_start(
            out=khb[hp][:], in_=io["kf"][h * 64:(h + 1) * 64, :]))
        cv_ = p.dma("sync", vhla[hp], lambda s, h=h, hp=hp: s.dma_start(
            out=vhb[hp][:].rearrange("pp (kc c) -> pp kc c", c=64),
            in_=vt_r[:, :, h * 64:(h + 1) * 64]))
        kv_loaded[h] = (ck, cv_)

    def emit_iter(idx):
        nonlocal waited_c
        h, g = iters[idx]
        pi = idx % 2
        kmax = 4 * g + 4 if causal else 32
        kh = khb[h % 2]
        qb = qh[pi]
        if idx == 0:
            load_kv(0)
            load_kv(1)
        if g == 1 and 1 <= h < 15:
            load_kv(h + 1)
        if idx >= 2:
            p.wait("sync", sem_pe, s_done[idx - 2])
        cq = p.dma("sync", qla[pi], lambda s, h=h, g=g, qb=qb: s.dma_start(
            out=qb[:], in_=io["qf"][h * 64:(h + 1) * 64,
                                    g * 512:(g + 1) * 512]))
        p.wait("tensor", qla[pi], cq)
        p.wait("tensor", khla[h % 2], kv_loaded[h][0])
        if not waited_c:
            p.wait("scalar", ldc, cld)
            p.wait("gpsimd", ldc, cld)
            waited_c = True
        if idx >= 2:
            p.wait("scalar", sem_pe, pv_done[idx - 2])
        Pb = Pb2[pi]
        cexp = {}
        cmsk = {}
        prev = state.pop(idx - 1, None)
        if prev is not None:
            pkmax, pcexp, pcmsk, ph, pg, ppi = prev
            pPb = Pb2[ppi]
            pvh = vhb[ph % 2]
            p.wait("tensor", vhla[ph % 2], kv_loaded[ph][1])
            if idx > 1:
                p.wait("tensor", sem_d, dve_pc[idx - 2])
        else:
            pkmax = 0
        for kc in range(max(kmax, pkmax)):
            if kc < kmax:
                b = pms[kc % 2]
                if (kc % 2) in exp_of_bank:
                    p.wait("tensor", sem_a, exp_of_bank[kc % 2])
                cpe = p.op("tensor", lambda t, kc=kc, b=b, qb=qb, kh=kh: t.matmul(
                    b[:], kh[:, kc * 128:(kc + 1) * 128], qb[:],
                    start=True, stop=True), inc=sem_pe)
                p.wait("scalar", sem_pe, cpe)
                cexp[kc] = p.op("scalar", lambda s, kc=kc, b=b, Pb=Pb: s.activation(
                    Pb[:, kc * 512:(kc + 1) * 512], b[:], AF.Exp,
                    bias=n20[:, 0:1], scale=0.125), inc=sem_a)
                exp_of_bank[kc % 2] = cexp[kc]
                if causal and kc >= 4 * g:
                    tt = kc - 4 * g
                    p.wait("gpsimd", sem_a, cexp[kc])
                    cm = p.op("gpsimd", lambda gp, kc=kc, tt=tt, Pb=Pb: gp.tensor_mul(
                        Pb[:, kc * 512 + tt * 128: kc * 512 + (tt + 1) * 128],
                        Pb[:, kc * 512 + tt * 128: kc * 512 + (tt + 1) * 128],
                        trib[:]), inc=sem_pl, small=True)
                    for tt2 in range(tt):
                        cm = p.op("gpsimd", lambda gp, kc=kc, tt2=tt2, Pb=Pb: gp.memset(
                            Pb[:, kc * 512 + tt2 * 128: kc * 512 + (tt2 + 1) * 128],
                            0.0), inc=sem_pl, small=True)
                    cmsk[kc] = cm
            if prev is not None and kc < pkmax:
                p.wait("tensor", sem_a, pcexp[kc])
                if kc in pcmsk:
                    p.wait("tensor", sem_pl, pcmsk[kc])
                p.op("tensor", lambda t, kc=kc, last=(kc == pkmax - 1), vh=pvh,
                     Pb=pPb: t.matmul(
                    pms[2][0:64, :], vh[:, kc * 64:(kc + 1) * 64],
                    Pb[:, kc * 512:(kc + 1) * 512],
                    start=(kc == 0), stop=last))
                p.op("tensor", lambda t, kc=kc, last=(kc == pkmax - 1),
                     Pb=pPb: t.matmul(
                    pms[3][64:65, :], onescb[:],
                    Pb[:, kc * 512:(kc + 1) * 512],
                    start=(kc == 0), stop=last),
                    inc=(sem_pe if kc == pkmax - 1 else None))
        s_done[idx] = p.counts[sem_pe]
        state[idx] = (kmax, cexp, cmsk, h, g, pi)
        if prev is not None:
            emit_tail(idx - 1, prev)

    def emit_tail(idx, st):
        kmax, cexp, cmsk, h, g, pi = st
        pv_done[idx] = p.counts[sem_pe]
        p.wait("vector", sem_pe, pv_done[idx])
        cd = p.op("vector", lambda v: v.reciprocal(rowr[0:1, :], pms[3][64:65, :]),
                  inc=sem_d, small=True)
        p.wait("tensor", sem_d, cd)
        cpe = p.op("tensor", lambda t: t.matmul(pms[4][0:64, :],
                                                onesf[0:1, 0:64], rowr[0:1, :],
                                                start=True, stop=True),
                   inc=sem_pe)
        p.wait("scalar", sem_pe, cpe)
        crbc = p.op("scalar", lambda s: s.activation(rbc[:], pms[4][0:64, :],
                                                     AF.Copy), inc=sem_a)
        sj = idx % 2
        oslots.guard("vector", sj)
        p.wait("vector", sem_a, crbc)
        cev = p.op("vector", lambda v, sj=sj: v.tensor_tensor(
            ost[sj][:], pms[2][0:64, :], rbc[:], op=ALU.mult), inc=sem_d)
        dve_pc[idx] = cev
        p.wait("sync", sem_d, cev)
        oslots.store(sj, lambda s, h=h, g=g, sj=sj: s.dma_start(
            out=io["of"][h * 64:(h + 1) * 64, g * 512:(g + 1) * 512],
            in_=ost[sj][:]))

    for idx in range(len(iters)):
        emit_iter(idx)
    last = len(iters) - 1
    st = state.pop(last)
    pkmax, pcexp, pcmsk, ph, pg, ppi = st
    p.wait("tensor", vhla[ph % 2], kv_loaded[ph][1])
    p.wait("tensor", sem_d, dve_pc[last - 1])
    for kc in range(pkmax):
        p.wait("tensor", sem_a, pcexp[kc])
        if kc in pcmsk:
            p.wait("tensor", sem_pl, pcmsk[kc])
        p.op("tensor", lambda t, kc=kc, last_=(kc == pkmax - 1): t.matmul(
            pms[2][0:64, :], vhb[ph % 2][:, kc * 64:(kc + 1) * 64],
            Pb2[ppi][:, kc * 512:(kc + 1) * 512],
            start=(kc == 0), stop=last_))
        p.op("tensor", lambda t, kc=kc, last_=(kc == pkmax - 1): t.matmul(
            pms[3][64:65, :], onescb[:],
            Pb2[ppi][:, kc * 512:(kc + 1) * 512],
            start=(kc == 0), stop=last_),
            inc=(sem_pe if kc == pkmax - 1 else None))
    emit_tail(last, st)
    p.barrier()


def phase_mattn(p, io, cst, scope):
    emit_mha(p, io, cst, scope, "ma", causal=True, sc1col=2)


phase_mattn.reads = ("qf", "kf", "vt", "ones_f", "onescol_b", "triT_b", "neg20")
phase_mattn.writes = ("of",)


def phase_l1attn(p, io, cst, scope):
    emit_mha(p, io, cst, scope, "la", causal=False, sc1col=3)


phase_l1attn.reads = ("qf", "kf", "vt", "ones_f", "onescol_b", "triT_b", "neg20")
phase_l1attn.writes = ("of",)



def phase_moe(p, io, cst, scope):
    nc = p.nc
    e = scope.enter_context
    xfm = [[e(nc.sbuf_tensor(f"moe_x{pr}{dc}", [128, 512], BF16)) for dc in range(8)]
           for pr in range(2)]
    x1t = [e(nc.sbuf_tensor(f"moe_x1t{j}", [128, D], F32)) for j in range(4)]
    yb = [e(nc.sbuf_tensor(f"moe_y{j}", [128, D], F32)) for j in range(4)]
    w1s = [[e(nc.sbuf_tensor(f"moe_w1{sl}{dc}", [128, FF], BF16)) for dc in range(8)]
           for sl in range(2)]
    w2s = [e(nc.sbuf_tensor(f"moe_w2{hc}", [128, D], BF16)) for hc in range(16)]
    b1c = [e(nc.sbuf_tensor(f"moe_b1c{sl}", [128, 16], F32)) for sl in range(2)]
    hsb = [e(nc.sbuf_tensor(f"moe_h{hc}", [128, 512], BF16)) for hc in range(16)]
    cent = e(nc.sbuf_tensor("moe_cent", [8, D], F32))
    centT = e(nc.sbuf_tensor("moe_centT", [128, 64], BF16))
    bias9 = e(nc.sbuf_tensor("moe_b9", [9, D], F32))
    rbb = e(nc.sbuf_tensor("moe_rbb", [128, 8], F32))
    gst = e(nc.sbuf_tensor("moe_gst", [128, 76], F32))
    gating = e(nc.sbuf_tensor("moe_gat", [128, 36], F32))
    gT = e(nc.sbuf_tensor("moe_gT", [9, 512], F32))
    identf = e(nc.sbuf_tensor("moe_idf", [128, 128], F32))
    pms = [e(nc.psum_tensor(f"moe_p{i}", [128, 512], F32)) for i in range(7)]

    ldc = p.sem("moe_ldc")
    ldg = p.sem("moe_ldg")
    sem_pe = p.sem("moe_pe")
    sem_d = p.sem("moe_d")
    sem_a = p.sem("moe_a")
    sem_y = p.sem("moe_y")
    slots = StSlots(p, "moe", 4)
    las = [p.sem("moe_la"), p.sem("moe_lb")]
    lx = p.sem("moe_lx")

    g3 = lambda ap, w=8: ap.rearrange("p (a b) -> p a b", b=w)
    gcol3 = lambda ap: ap.rearrange("p (a o) -> p a o", o=1)

    p.dma("sync", ldc, lambda s: s.dma_start(out=identf[:], in_=io["ident_f"]))
    p.dma("sync", ldc, lambda s: s.dma_start(
        out=rbb[:], in_=io["route_bias"].partition_broadcast(128)))
    p.dma("sync", ldc, lambda s: s.dma_start(out=bias9[0:1, :], in_=io["sb2"]))
    p.dma("sync", ldc, lambda s: s.dma_start(out=bias9[1:9, :], in_=io["rb2"]))
    cld = p.dma("sync", ldc, lambda s: s.dma_start(out=cent[:], in_=io["centroid"]))
    p.wait("tensor", ldc, cld)
    for dc in range(8):
        p.op("tensor", lambda t, dc=dc: t.transpose(
            pms[6][0:128, dc * 8:(dc + 1) * 8], cent[:, dc * 128:(dc + 1) * 128],
            identf[0:8, 0:8]),
            inc=(sem_pe if dc == 7 else None))
    cpe = p.counts[sem_pe]
    p.wait("scalar", sem_pe, cpe)
    ccT = p.op("scalar", lambda s: s.activation(centT[:], pms[6][:, 0:64], AF.Copy),
               inc=sem_a)

    def w1_src(ei):
        return io["sw1"] if ei == 0 else io["rw1"][ei - 1:ei, :, :].squeeze(0)

    def w2_src(ei):
        return io["sw2"] if ei == 0 else io["rw2"][ei - 1:ei, :, :].squeeze(0)

    def b1_src(ei):
        return io["sb1"] if ei == 0 else io["rb1"][ei - 1:ei, :]

    wload_w1 = {}
    wload_b1 = {}
    wload_w2 = {}
    pe_w1_free = {}
    act_b1_free = {}
    pe_w2_free = [None]
    w1sem = [p.sem("moe_w1a"), p.sem("moe_w1b")]
    b1sem = [p.sem("moe_b1a"), p.sem("moe_b1b")]
    w2sem = p.sem("moe_w2s")

    def load_w1(eseq):
        ei = eseq % 9
        sl = eseq % 2
        if pe_w1_free.get(sl) is not None:
            p.wait("gpsimd", sem_pe, pe_w1_free[sl])
        if act_b1_free.get(sl) is not None:
            p.wait("sync", sem_a, act_b1_free[sl])
        c = None
        for dc in range(8):
            c = p.dma("gpsimd", w1sem[sl], lambda g_, ei=ei, sl=sl, dc=dc: g_.dma_start(
                out=w1s[sl][dc][:], in_=w1_src(ei)[dc * 128:(dc + 1) * 128, :]))
        wload_w1[eseq] = c
        wload_b1[eseq] = load_bias_cols(p, b1sem[sl], b1c[sl], b1_src(ei), 16)

    def load_w2(eseq):
        ei = eseq % 9
        if pe_w2_free[0] is not None:
            p.wait("gpsimd", sem_pe, pe_w2_free[0])
        c = None
        for hc in range(16):
            c = p.dma("gpsimd", w2sem, lambda g_, ei=ei, hc=hc: g_.dma_start(
                out=w2s[hc][:], in_=w2_src(ei)[hc * 128:(hc + 1) * 128, :]))
        wload_w2[eseq] = c

    load_w1(0)
    load_w2(0)
    y_count = {}
    pe_group_end = {}
    gat_ready = {}
    hev = {}
    opev = {}
    for g in range(NG):
        la = las[g % 2]
        xf = xfm[g % 2]
        if g >= 2:
            p.wait("sync", sem_pe, pe_group_end[g - 2])
        for dc in range(8):
            p.dma("sync", la, lambda s, g=g, dc=dc, xf=xf: s.dma_start(
                out=xf[dc][:], in_=io["x1_bf"][g * 512:(g + 1) * 512,
                                               dc * 128:(dc + 1) * 128],
                transpose=True))
        ldv = p.counts[la]
        if g > 0:
            p.wait("sync", sem_y, y_count[(g - 1, 3, 1)])
        for j in range(4):
            tt = g * 4 + j
            p.dma("sync", lx, lambda s, tt=tt, j=j: s.dma_start(
                out=x1t[j][:], in_=io["x1_f32"][tt * 128:(tt + 1) * 128, :]))
        clx = p.counts[lx]
        p.wait("tensor", la, ldv)
        if g == 0:
            p.wait("tensor", sem_a, ccT)
        if g > 0:
            p.wait("tensor", sem_d, gat_ready[g - 1])
        for tt in range(4):
            for dc in range(8):
                p.op("tensor", lambda t, tt=tt, dc=dc, xf=xf: t.matmul(
                    pms[5][:, tt * 8:(tt + 1) * 8],
                    xf[dc][:, tt * 128:(tt + 1) * 128], centT[:, dc * 8:(dc + 1) * 8],
                    start=(dc == 0), stop=(dc == 7)),
                    inc=(sem_pe if (tt == 3 and dc == 7) else None))
        cpe = p.counts[sem_pe]
        p.wait("scalar", sem_pe, cpe)
        css = p.op("scalar", lambda s: s.activation(gst[:, 0:32], pms[5][:, 0:32],
                                                    AF.Copy), inc=sem_a, small=True)
        p.wait("vector", sem_a, css)
        c1 = p.op("vector", lambda v: v.reduce_max(gcol3(gst[:, 32:36]),
                                                   g3(gst[:, 0:32]),
                                                   axis=AX.X, negate=True),
                  inc=sem_d, small=True)
        p.wait("scalar", sem_d, c1)
        ce = None
        for tt in range(4):
            ce = p.op("scalar", lambda s, tt=tt: s.activation(
                gst[:, tt * 8:(tt + 1) * 8], gst[:, tt * 8:(tt + 1) * 8],
                AF.Exp, bias=gst[:, 32 + tt:33 + tt]), inc=sem_a,
                small=(tt == 3))
        p.wait("vector", sem_a, ce)
        c2 = p.op("vector", lambda v: v.reduce_sum(gcol3(gst[:, 36:40]),
                                                   g3(gst[:, 0:32]), axis=AX.X),
                  inc=sem_d, small=True)
        p.wait("vector", sem_d, c2)
        c3 = p.op("vector", lambda v: v.reciprocal(gst[:, 40:44], gst[:, 36:40]),
                  inc=sem_d, small=True)
        p.wait("vector", sem_d, c3)
        csim = None
        for tt in range(4):
            csim = p.op("vector", lambda v, tt=tt: v.tensor_scalar(
                gating[:, tt * 9 + 1:tt * 9 + 9], gst[:, tt * 8:(tt + 1) * 8],
                gst[:, 40 + tt:41 + tt], None, op0=ALU.mult),
                inc=sem_d, small=(tt == 3))
        p.wait("vector", sem_d, csim)
        ct = None
        for tt in range(4):
            ct = p.op("vector", lambda v, tt=tt: v.tensor_add(
                gst[:, tt * 8:(tt + 1) * 8], gating[:, tt * 9 + 1:tt * 9 + 9],
                rbb[:]), inc=sem_d, small=(tt == 3))
        p.wait("vector", sem_d, ct)
        c4 = p.op("vector", lambda v: v.reduce_max(gcol3(gst[:, 32:36]),
                                                   g3(gst[:, 0:32]), axis=AX.X),
                  inc=sem_d, small=True)
        p.wait("vector", sem_d, c4)
        cge1 = None
        for tt in range(4):
            cge1 = p.op("vector", lambda v, tt=tt: v.tensor_scalar(
                gst[:, 44 + tt * 8:52 + tt * 8], gst[:, tt * 8:(tt + 1) * 8],
                gst[:, 32 + tt:33 + tt], None, op0=ALU.is_ge),
                inc=sem_d, small=(tt == 3))
        p.wait("vector", sem_d, cge1)
        c5 = p.op("vector", lambda v: v.scalar_tensor_tensor(
            gst[:, 0:32], gst[:, 44:76], -1e30, gst[:, 0:32],
            op0=ALU.mult, op1=ALU.add), inc=sem_d, small=True)
        p.wait("vector", sem_d, c5)
        c6 = p.op("vector", lambda v: v.reduce_max(gcol3(gst[:, 36:40]),
                                                   g3(gst[:, 0:32]), axis=AX.X),
                  inc=sem_d, small=True)
        p.wait("vector", sem_d, c6)
        cge2 = None
        for tt in range(4):
            cge2 = p.op("vector", lambda v, tt=tt: v.tensor_scalar(
                gst[:, tt * 8:(tt + 1) * 8], gst[:, tt * 8:(tt + 1) * 8],
                gst[:, 36 + tt:37 + tt], None, op0=ALU.is_ge),
                inc=sem_d, small=(tt == 3))
        p.wait("vector", sem_d, cge2)
        c7 = p.op("vector", lambda v: v.tensor_add(gst[:, 44:76], gst[:, 44:76],
                                                   gst[:, 0:32]),
                  inc=sem_d, small=True)
        p.wait("vector", sem_d, c7)
        c8 = p.op("vector", lambda v: v.tensor_mul(
            g3(gst[:, 0:32]),
            gating[:, 0:36].rearrange("p (a b) -> p a b", b=9)[:, :, 1:9],
            g3(gst[:, 44:76])), inc=sem_d, small=True)
        p.wait("vector", sem_d, c8)
        c9 = p.op("vector", lambda v: v.reduce_sum(gcol3(gst[:, 36:40]),
                                                   g3(gst[:, 0:32]), axis=AX.X),
                  inc=sem_d, small=True)
        p.wait("vector", sem_d, c9)
        c10 = p.op("vector", lambda v: v.reciprocal(gst[:, 40:44], gst[:, 36:40]),
                   inc=sem_d, small=True)
        p.wait("vector", sem_d, c10)
        cgat = None
        for tt in range(4):
            p.op("vector", lambda v, tt=tt: v.memset(gating[:, tt * 9:tt * 9 + 1], 1.0))
            cgat = p.op("vector", lambda v, tt=tt: v.tensor_scalar(
                gating[:, tt * 9 + 1:tt * 9 + 9], gst[:, tt * 8:(tt + 1) * 8],
                gst[:, 40 + tt:41 + tt], None, op0=ALU.mult),
                inc=sem_d, small=(tt == 3))
        gat_ready[g] = cgat
        p.wait("vector", sem_d, cgat)
        def expert_h(ei, g=g, xf=xf):
            eseq = g * 9 + ei
            sl = eseq % 2
            if eseq + 1 < NG * 9:
                load_w1(eseq + 1)
            p.wait("tensor", w1sem[sl], wload_w1[eseq])
            p.wait("scalar", b1sem[sl], wload_b1[eseq])
            for hc in range(16):
                b = pms[hc % 2]
                guard = hev.get(hc % 2)
                if guard is not None:
                    p.wait("tensor", sem_a, guard)
                cpe = mm_acc(p, b[:],
                             [(w1s[sl][dc][:, hc * 128:(hc + 1) * 128], xf[dc][:])
                              for dc in range(8)], sem_pe)
                p.wait("scalar", sem_pe, cpe)
                hev[hc % 2] = p.op("scalar", lambda s, hc=hc, b=b, sl=sl: s.activation(
                    hsb[hc][:], b[:], AF.Gelu, bias=b1c[sl][:, hc:hc + 1]),
                    inc=sem_a)
            return p.counts[sem_a], p.counts[sem_pe]

        def expert_out(ei, hdone, pe_h_end, g=g):
            eseq = g * 9 + ei
            sl = eseq % 2
            pe_w1_free[sl] = pe_h_end
            act_b1_free[sl] = hdone
            p.wait("tensor", w2sem, wload_w2[eseq])
            p.wait("tensor", sem_a, hdone)
            for j in range(4):
                for half in range(2):
                    b = pms[2 + (j * 2 + half) % 2]
                    guard = opev.get((j * 2 + half) % 2)
                    if guard is not None:
                        p.wait("tensor", sem_y, guard)
                    cpe = mm_acc(p, b[:],
                                 [(hsb[hc][:, j * 128:(j + 1) * 128],
                                   w2s[hc][:, half * 512:(half + 1) * 512])
                                  for hc in range(16)], sem_pe)
                    p.wait("vector", sem_pe, cpe)
                    p.wait("vector", sem_y, y_count[(g, j, half)])
                    gci = ei
                    cy = p.op("vector", lambda v, j=j, half=half, b=b,
                              gci=gci: v.scalar_tensor_tensor(
                        yb[j][:, half * 512:(half + 1) * 512], b[:],
                        gating[:, j * 9 + gci:j * 9 + gci + 1],
                        yb[j][:, half * 512:(half + 1) * 512],
                        op0=ALU.mult, op1=ALU.add),
                        inc=sem_y, small=(ei == 8))
                    y_count[(g, j, half)] = cy
                    opev[(j * 2 + half) % 2] = cy
            pe_w2_free[0] = p.counts[sem_pe]
            if eseq + 1 < NG * 9:
                load_w2(eseq + 1)

        h0, pe0 = expert_h(0)
        p.wait("tensor", sem_d, cgat)
        for tt in range(4):
            p.op("tensor", lambda t, tt=tt: t.transpose(
                pms[6][0:9, tt * 128:(tt + 1) * 128],
                gating[:, tt * 9:(tt + 1) * 9], identf[:]),
                inc=(sem_pe if tt == 3 else None))
        cpe = p.counts[sem_pe]
        p.wait("scalar", sem_pe, cpe)
        cgT = p.op("scalar", lambda s: s.activation(gT[:], pms[6][0:9, :], AF.Copy),
                   inc=sem_a)
        p.wait("tensor", sem_a, cgT)
        p.wait("tensor", ldc, p.counts[ldc])
        p.wait("vector", lx, clx)
        py_free = None
        for j in range(4):
            slots.guard("vector", j)
            for half in range(2):
                if py_free is not None:
                    p.wait("tensor", sem_y, py_free)
                cpe = p.op("tensor", lambda t, j=j, half=half: t.matmul(
                    pms[4][:], gT[:, j * 128:(j + 1) * 128],
                    bias9[:, half * 512:(half + 1) * 512],
                    start=True, stop=True), inc=sem_pe)
                p.wait("vector", sem_pe, cpe)
                cy = p.op("vector", lambda v, j=j, half=half: v.tensor_add(
                    yb[j][:, half * 512:(half + 1) * 512],
                    x1t[j][:, half * 512:(half + 1) * 512], pms[4][:]),
                    inc=sem_y)
                y_count[(g, j, half)] = cy
                py_free = cy
        expert_out(0, h0, pe0)
        for ei in range(1, 9):
            hd, ped = expert_h(ei)
            expert_out(ei, hd, ped)
        for j in range(4):
            tt = g * 4 + j
            p.wait("sync", sem_y, y_count[(g, j, 1)])
            slots.store(j, lambda s, tt=tt, j=j: s.dma_start(
                out=io["ymoe"][tt * 128:(tt + 1) * 128, :], in_=yb[j][:]))
        pe_group_end[g] = p.counts[sem_pe]
    p.barrier()


phase_moe.reads = ("x1_bf", "x1_f32", "sw1", "sb1", "sw2", "sb2", "rw1", "rb1",
                   "rw2", "rb2", "centroid", "route_bias", "ident_f")
phase_moe.writes = ("ymoe",)



def phase_l1prep(p, io, cst, scope):
    nc = p.nc
    e = scope.enter_context
    st = e(nc.sbuf_tensor("lp_stat", [128, 32], F32))
    yt = [e(nc.sbuf_tensor(f"lp_yt{j}", [128, D], F32)) for j in range(4)]
    x0t = [e(nc.sbuf_tensor(f"lp_x0{j}", [128, D], F32)) for j in range(4)]
    abt = [e(nc.sbuf_tensor(f"lp_ab{j}", [128, D], F32)) for j in range(4)]
    xm = [e(nc.sbuf_tensor(f"lp_xm{j}", [128, D], F32)) for j in range(4)]
    t1 = [e(nc.sbuf_tensor(f"lp_t1{j}", [128, D], F32)) for j in range(4)]
    t2 = [e(nc.sbuf_tensor(f"lp_t2{j}", [128, D], F32)) for j in range(4)]
    sqd = [e(nc.sbuf_tensor(f"lp_sqd{j}", [128, D], BF16)) for j in range(4)]
    n2g = e(nc.sbuf_tensor("lp_n2g", [128, D], F32))
    n2b = e(nc.sbuf_tensor("lp_n2b", [128, D], F32))
    l1g = e(nc.sbuf_tensor("lp_l1g", [128, D], F32))
    l1b = e(nc.sbuf_tensor("lp_l1b", [128, D], F32))
    ldw = p.sem("ldw")
    for buf, nm in ((n2g, "norm2_w"), (n2b, "norm2_b"), (l1g, "ln1_g"), (l1b, "ln1_b")):
        wv = p.dma("sync", ldw, lambda s, buf=buf, nm=nm: s.dma_start(
            out=buf[:], in_=io[nm].partition_broadcast(128)))
    p.wait("vector", ldw, wv)
    sem_ld = p.sem("lp_ld")
    sem_d = p.sem("lp_d")
    sem_a = p.sem("lp_a")
    sem_st = p.sem("st")
    stg = p.sem("stg")
    for g in range(NG):
        for j in range(4):
            tt = g * 4 + j
            p.dma("sync", sem_ld, lambda s, tt=tt, j=j: s.dma_start(
                out=yt[j][:], in_=io["ymoe"][tt * 128:(tt + 1) * 128, :]))
            p.dma("sync", sem_ld, lambda s, tt=tt, j=j: s.dma_start(
                out=x0t[j][:], in_=io["x"][tt * 128:(tt + 1) * 128, :]))
            c = p.dma("sync", sem_ld, lambda s, tt=tt, j=j: s.dma_start(
                out=abt[j][:], in_=io["attn_br"][tt * 128:(tt + 1) * 128, :]))
        p.wait("vector", sem_ld, c)
        p.wait("scalar", sem_ld, c)
        cout = ln_tiles(p, "lp", st, 0, sqd, xm, yt, t1, D, sem_d, sem_a)
        for j in range(4):
            tt = g * 4 + j
            p.wait("vector", sem_d, cout[j])
            cm_ = p.op("vector", lambda v, j=j: v.tensor_mul(t1[j][:], t1[j][:], n2g[:]),
                       inc=sem_d)
            p.wait("vector", sem_d, cm_)
            cm_ = p.op("vector", lambda v, j=j: v.tensor_add(t1[j][:], t1[j][:], n2b[:]),
                      inc=sem_d)
            p.wait("vector", sem_d, cm_)
            cm_ = p.op("vector", lambda v, j=j: v.tensor_add(t1[j][:], t1[j][:], x0t[j][:]),
                       inc=sem_d)
            p.wait("vector", sem_d, cm_)
            cr = p.op("vector", lambda v, j=j: v.tensor_add(t1[j][:], t1[j][:], abt[j][:]),
                      inc=sem_d, small=True)
            p.wait("sync", sem_d, cr)
            p.dma("sync", sem_st, lambda s, tt=tt, j=j: s.dma_start(
                out=io["xres"][tt * 128:(tt + 1) * 128, :], in_=t1[j][:]))
            p.wait("scalar", sem_d, cr)
        cout = ln_tiles(p, "lp", st, 0, sqd, xm, t1, t2, D, sem_d, sem_a)
        for j in range(4):
            tt = g * 4 + j
            p.wait("gpsimd", sem_d, cout[j])
            p.dma("gpsimd", stg, lambda gp, tt=tt, j=j: gp.dma_start(
                out=io["xn2_bf"][tt * 128:(tt + 1) * 128, :], in_=t2[j][:]))
            p.wait("scalar", sem_d, cout[j])
        p.wait("vector", "st", p.counts["st"])
        cout = ln_tiles(p, "lp", st, 0, sqd, xm, t2, t1, D, sem_d, sem_a)
        for j in range(4):
            tt = g * 4 + j
            p.wait("vector", sem_d, cout[j])
            cm_ = p.op("vector", lambda v, j=j: v.tensor_mul(t1[j][:], t1[j][:], l1g[:]),
                       inc=sem_d)
            p.wait("vector", sem_d, cm_)
            ch = p.op("vector", lambda v, j=j: v.tensor_add(t1[j][:], t1[j][:], l1b[:]),
                      inc=sem_d, small=True)
            p.wait("gpsimd", sem_d, ch)
            p.dma("gpsimd", stg, lambda gp, tt=tt, j=j: gp.dma_start(
                out=io["h1_bf"][tt * 128:(tt + 1) * 128, :], in_=t1[j][:]))
        p.barrier()


phase_l1prep.reads = ("ymoe", "x", "attn_br", "norm2_w", "norm2_b", "ln1_g", "ln1_b")
phase_l1prep.writes = ("xres", "xn2_bf", "h1_bf")



def phase_l1wcast(p, io, cst, scope):
    stg = p.sem("stg")
    for wname, dname in (("Wq", "wqT_src"), ("Wk", "wkT_src"), ("Wv", "wvT_src"),
                         ("Wo", "woT_src"), ):
        p.dma("gpsimd", stg, lambda gp, wname=wname, dname=dname: gp.dma_start(
            out=io[dname], in_=io[wname]))
    p.barrier()


phase_l1wcast.reads = ("Wq", "Wk", "Wv", "Wo")
phase_l1wcast.writes = ("wqT_src", "wkT_src", "wvT_src", "woT_src")



def phase_l1qkv(p, io, cst, scope):
    nc = p.nc
    e = scope.enter_context
    tag = "l1q"
    wsb = [e(nc.sbuf_tensor(f"{tag}_w{dc}", [128, 3 * D], BF16)) for dc in range(8)]
    xfm = [[e(nc.sbuf_tensor(f"{tag}_x{pr}{dc}", [128, 512], BF16)) for dc in range(8)]
           for pr in range(2)]
    stage = [e(nc.sbuf_tensor(f"{tag}_s{i}", [128, 512], BF16)) for i in range(6)]
    pms = [e(nc.psum_tensor(f"{tag}_p{i}", [128, 512], F32)) for i in range(4)]
    banks = Banks(p, pms)
    ldw = p.sem("ldw")
    wldv = None
    for dc in range(8):
        for mi, wn in enumerate(("wqT_src", "wkT_src", "wvT_src")):
            wldv = p.dma("sync", ldw, lambda s, dc=dc, mi=mi, wn=wn: s.dma_start(
                out=wsb[dc][:, mi * D:(mi + 1) * D],
                in_=io[wn][:, dc * 128:(dc + 1) * 128], transpose=True))
    sem_pe = p.sem(tag + "_pe")
    sem_ev = p.sem(tag + "_ev")
    las = [p.sem(tag + "_la"), p.sem(tag + "_lb")]
    slots = StSlots(p, tag, 6)
    pe_group_end = {}
    waited_w = False
    for g in range(NG):
        la = las[g % 2]
        xf = xfm[g % 2]
        if g >= 2:
            p.wait("sync", sem_pe, pe_group_end[g - 2])
        for dc in range(8):
            p.dma("sync", la, lambda s, g=g, dc=dc, xf=xf: s.dma_start(
                out=xf[dc][:], in_=io["xn2_bf"][g * 512:(g + 1) * 512,
                                                dc * 128:(dc + 1) * 128],
                transpose=True))
        ldv = p.counts[la]
        p.wait("tensor", la, ldv)
        if not waited_w:
            p.wait("tensor", ldw, wldv)
            waited_w = True
        si = 0
        for mc in range(16):
            b = banks.next()
            wcol = (0 if mc < 8 else 1) * D + (mc % 8) * 128
            cpe = mm_acc(p, b[0][:],
                         [(wsb[dc][:, wcol:wcol + 128], xf[dc][:])
                          for dc in range(8)], sem_pe)
            sj = si % 6
            si += 1
            slots.guard("scalar", sj)
            p.wait("scalar", sem_pe, cpe)
            cev = p.op("scalar", lambda s, sj=sj, b=b: s.activation(
                stage[sj][:], b[0][:], AF.Copy), inc=sem_ev)
            Banks.release(b, sem_ev, cev)
            dst = io["qf"] if mc < 8 else io["kf"]
            r = (mc % 8) * 128
            p.wait("sync", sem_ev, cev)
            slots.store(sj, lambda s, dst=dst, r=r, g=g, sj=sj: s.dma_start(
                out=dst[r:r + 128, g * 512:(g + 1) * 512], in_=stage[sj][:]))
        for tt in range(4):
            for half in range(2):
                b = banks.next()
                cpe = mm_acc(p, b[0][:],
                             [(xf[dc][:, tt * 128:(tt + 1) * 128],
                               wsb[dc][:, 2 * D + half * 512: 2 * D + (half + 1) * 512])
                              for dc in range(8)], sem_pe)
                sj = si % 6
                si += 1
                slots.guard("scalar", sj)
                p.wait("scalar", sem_pe, cpe)
                cev = p.op("scalar", lambda s, sj=sj, b=b: s.activation(
                    stage[sj][:], b[0][:], AF.Copy), inc=sem_ev)
                Banks.release(b, sem_ev, cev)
                row = (g * 4 + tt) * 128
                p.wait("sync", sem_ev, cev)
                slots.store(sj, lambda s, row=row, half=half, sj=sj: s.dma_start(
                    out=io["vt"][row:row + 128, half * 512:(half + 1) * 512],
                    in_=stage[sj][:]))
        pe_group_end[g] = p.counts[sem_pe]
    p.barrier()


phase_l1qkv.reads = ("xn2_bf", "wqT_src", "wkT_src", "wvT_src")
phase_l1qkv.writes = ("qf", "kf", "vt")



def phase_l1proj(p, io, cst, scope):
    nc = p.nc
    e = scope.enter_context
    tag = "l1p"
    wsb = [e(nc.sbuf_tensor(f"{tag}_w{dc}", [128, D], BF16)) for dc in range(8)]
    ofm = [[e(nc.sbuf_tensor(f"{tag}_o{pr}{dc}", [128, 512], BF16)) for dc in range(8)]
           for pr in range(2)]
    stage = [e(nc.sbuf_tensor(f"{tag}_s{i}", [128, 512], F32)) for i in range(4)]
    pms = [e(nc.psum_tensor(f"{tag}_p{i}", [128, 512], F32)) for i in range(4)]
    banks = Banks(p, pms)
    ldw = p.sem("ldw")
    wldv = None
    for dc in range(8):
        wldv = p.dma("sync", ldw, lambda s, dc=dc: s.dma_start(
            out=wsb[dc][:], in_=io["woT_src"][:, dc * 128:(dc + 1) * 128],
            transpose=True))
    sem_pe = p.sem(tag + "_pe")
    sem_ev = p.sem(tag + "_ev")
    las = [p.sem(tag + "_la"), p.sem(tag + "_lb")]
    slots = StSlots(p, tag, 4)
    pe_group_end = {}
    waited = False
    for g in range(NG):
        la = las[g % 2]
        of = ofm[g % 2]
        if g >= 2:
            p.wait("sync", sem_pe, pe_group_end[g - 2])
        for dc in range(8):
            p.dma("sync", la, lambda s, g=g, dc=dc, of=of: s.dma_start(
                out=of[dc][:], in_=io["of"][dc * 128:(dc + 1) * 128,
                                            g * 512:(g + 1) * 512]))
        ldv = p.counts[la]
        p.wait("tensor", la, ldv)
        if not waited:
            p.wait("tensor", ldw, wldv)
            waited = True
        si = 0
        for tt in range(4):
            for half in range(2):
                b = banks.next()
                cpe = mm_acc(p, b[0][:],
                             [(of[dc][:, tt * 128:(tt + 1) * 128],
                               wsb[dc][:, half * 512:(half + 1) * 512])
                              for dc in range(8)], sem_pe)
                sj = si % 4
                si += 1
                slots.guard("scalar", sj)
                p.wait("scalar", sem_pe, cpe)
                cev = p.op("scalar", lambda s, sj=sj, b=b: s.activation(
                    stage[sj][:], b[0][:], AF.Copy), inc=sem_ev)
                Banks.release(b, sem_ev, cev)
                row = (g * 4 + tt) * 128
                p.wait("sync", sem_ev, cev)
                slots.store(sj, lambda s, row=row, half=half, sj=sj: s.dma_start(
                    out=io["attn2"][row:row + 128, half * 512:(half + 1) * 512],
                    in_=stage[sj][:]))
        pe_group_end[g] = p.counts[sem_pe]
    p.barrier()


phase_l1proj.reads = ("of", "woT_src")
phase_l1proj.writes = ("attn2",)



def emit_quant_stats(p, io, scope, tag, w_name, nrow, ncol, sc1col):
    nc = p.nc
    e = scope.enter_context
    wt = [e(nc.sbuf_tensor(f"{tag}_wt{i}", [128, 512], F32)) for i in range(2)]
    stt = e(nc.sbuf_tensor(f"{tag}_st", [128, 80], F32))
    dmp = [e(nc.sbuf_tensor(f"{tag}_dmp{i}", [128, 512], BF16)) for i in range(2)]
    identf = e(nc.sbuf_tensor(f"{tag}_idf", [128, 128], F32))
    pq = e(nc.psum_tensor(f"{tag}_pq", [128, 512], F32))
    ntile = nrow * ncol // (128 * 512)
    assert ntile <= 64
    lds = [p.sem(tag + "_lda"), p.sem(tag + "_ldb")]
    ldi = p.sem(tag + "_ldi")
    sem_d = p.sem(tag + "_d")
    sem_a = p.sem(tag + "_a")
    sem_pe = p.sem(tag + "_pe")
    sem_mk = p.sem(tag + "_mk")
    cld = p.dma("sync", ldi, lambda s: s.dma_start(out=identf[:], in_=io["ident_f"]))
    w2d = io[w_name].rearrange("(a p) c -> a p c", p=128)
    npc = ncol // 512
    act_free = {}
    for i in range(ntile):
        r, cblk = i // npc, i % npc
        if act_free.get(i % 2) is not None:
            p.wait("sync", sem_a, act_free[i % 2])
        c = p.dma("sync", lds[i % 2], lambda s, r=r, cblk=cblk, i=i: s.dma_start(
            out=wt[i % 2][:], in_=w2d[r, :, cblk * 512:(cblk + 1) * 512]))
        p.wait("scalar", lds[i % 2], c)
        act_free[i % 2] = p.op("scalar", lambda s, i=i: s.activation(
            dmp[i % 2][:], wt[i % 2][:], AF.Abs, accum_out=stt[:, i:i + 1]),
            inc=sem_a, small=(i == ntile - 1))
    p.wait("vector", sem_a, act_free[(ntile - 1) % 2])
    cr = p.op("vector", lambda v: v.reduce_sum(stt[:, 64:65], stt[:, 0:ntile],
                                               axis=AX.X), inc=sem_d, small=True)
    p.wait("tensor", sem_d, cr)
    p.wait("tensor", ldi, cld)
    cpe = p.op("tensor", lambda t: t.transpose(pq[0:1, 0:128], stt[:, 64:65],
                                               identf[:]), inc=sem_pe)
    p.wait("vector", sem_pe, cpe)
    cs = p.op("vector", lambda v: v.reduce_sum(stt[0:1, 66:67], pq[0:1, 0:128],
                                               axis=AX.X), inc=sem_d, small=True)
    p.wait("scalar", sem_d, cs)
    cm = p.op("scalar", lambda s: s.activation(stt[0:1, 67:68], stt[0:1, 66:67],
                                               AF.Copy, scale=1.0 / (nrow * ncol)),
              inc=sem_a, small=True)
    p.wait("vector", sem_a, cm)
    cc = p.op("vector", lambda v: v.tensor_single_scalar(stt[0:1, 68:69],
                                                         stt[0:1, 67:68], 1e-5,
                                                         op=ALU.max),
              inc=sem_d, small=True)
    p.wait("sync", sem_d, cc)
    cs1 = p.dma("sync", sem_mk, lambda s: s.dma_start(
        out=io["sc1"][0:1, sc1col:sc1col + 1], in_=stt[0:1, 68:69]))
    return cs1, sem_mk


def emit_quantize_wT(p, io, scope, tag, w_name, nrow, ncol, sc1col, dst):
    nc = p.nc
    e = scope.enter_context
    wt = [e(nc.sbuf_tensor(f"{tag}_qw{i}", [128, 512], F32)) for i in range(2)]
    g1 = e(nc.sbuf_tensor(f"{tag}_g1", [128, 512], F32))
    g2 = e(nc.sbuf_tensor(f"{tag}_g2", [128, 512], F32))
    qb = [e(nc.sbuf_tensor(f"{tag}_qb{i}", [128, 512], BF16)) for i in range(2)]
    hcol = e(nc.sbuf_tensor(f"{tag}_hcol", [128, 2], F32))
    identb = e(nc.sbuf_tensor(f"{tag}_idb", [128, 128], BF16))
    pqb = e(nc.psum_tensor(f"{tag}_pqb", [128, 1024], BF16))
    lds = [p.sem(tag + "_qlda"), p.sem(tag + "_qldb")]
    ld = p.sem(tag + "_qld")
    sem_d = p.sem(tag + "_qd")
    sem_a = p.sem(tag + "_qa")
    sem_pe = p.sem(tag + "_qpe")
    sem_ev = p.sem(tag + "_qev")
    cld = p.dma("gpsimd", p.sem(tag + "_qldg"),
                lambda g_: g_.dma_start(out=identb[:], in_=io["ident_f"]))
    cmc = p.dma("sync", ld, lambda s: s.dma_start(
        out=hcol[:, 0:1], in_=io["sc1"][0:1, sc1col:sc1col + 1].partition_broadcast(128)))
    p.wait("vector", ld, cmc)
    ch = p.op("vector", lambda v: v.tensor_scalar(hcol[:, 0:1], hcol[:, 0:1], 0.5,
                                                  None, op0=ALU.mult),
              inc=sem_d, small=True)
    p.wait("vector", sem_d, ch)
    cn = p.op("vector", lambda v: v.tensor_scalar(hcol[:, 1:2], hcol[:, 0:1], -1.0,
                                                  None, op0=ALU.mult),
              inc=sem_d, small=True)
    p.wait("vector", sem_d, cn)
    p.wait("tensor", p.sem(tag + "_qldg"), cld)
    w2d = io[w_name].rearrange("(a p) c -> a p c", p=128)
    npc = ncol // 512
    dve_free = {}
    pe_qb_free = {}
    ev_free = None
    for i in range(nrow // 128 * npc):
        r, cblk = i // npc, i % npc
        if dve_free.get(i % 2) is not None:
            p.wait("sync", sem_d, dve_free[i % 2])
        c = p.dma("sync", lds[i % 2], lambda s, r=r, cblk=cblk, i=i: s.dma_start(
            out=wt[i % 2][:], in_=w2d[r, :, cblk * 512:(cblk + 1) * 512]))
        p.wait("vector", lds[i % 2], c)
        ca = p.op("vector", lambda v, i=i: v.tensor_scalar(
            g1[:], wt[i % 2][:], hcol[:, 0:1], None, op0=ALU.is_ge), inc=sem_d)
        p.wait("vector", sem_d, ca)
        cb = p.op("vector", lambda v, i=i: v.tensor_scalar(
            g2[:], wt[i % 2][:], hcol[:, 1:2], None, op0=ALU.is_le), inc=sem_d)
        p.wait("vector", sem_d, cb)
        if pe_qb_free.get(i % 2) is not None:
            p.wait("vector", sem_pe, pe_qb_free[i % 2])
        cq = p.op("vector", lambda v, i=i: v.tensor_sub(qb[i % 2][:], g1[:], g2[:]),
                  inc=sem_d, small=True)
        dve_free[i % 2] = p.counts[sem_d]
        p.wait("tensor", sem_d, cq)
        if ev_free is not None:
            p.wait("tensor", sem_ev, ev_free)
        for t4 in range(4):
            p.op("tensor", lambda t, i=i, t4=t4: t.transpose(
                pqb[:, t4 * 128:(t4 + 1) * 128],
                qb[i % 2][:, t4 * 128:(t4 + 1) * 128], identb[:]),
                inc=(sem_pe if t4 == 3 else None))
        cpe = p.counts[sem_pe]
        pe_qb_free[i % 2] = cpe
        p.wait("scalar", sem_pe, cpe)
        cev = None
        for t4 in range(4):
            cchunk = cblk * 4 + t4
            cev = p.op("scalar", lambda s, r=r, cchunk=cchunk, t4=t4: s.activation(
                dst[:, cchunk * nrow + r * 128: cchunk * nrow + (r + 1) * 128],
                pqb[:, t4 * 128:(t4 + 1) * 128], AF.Copy), inc=sem_ev)
        ev_free = cev
    return p.counts[sem_ev]



def phase_l1ffna(p, io, cst, scope):
    nc = p.nc
    e = scope.enter_context
    c1, s1 = emit_quant_stats(p, io, scope, "qs1", "w1", HID, D, 4)
    c2, s2 = emit_quant_stats(p, io, scope, "qs2", "w2", D, HID, 5)
    p.wait("sync", s1, c1)
    p.wait("sync", s2, c2)
    p.barrier()
    w1q = e(nc.sbuf_tensor("fa_w1q", [128, 8 * HID], BF16))
    emit_quantize_wT(p, io, scope, "fa", "w1", HID, D, 4, w1q)
    p.barrier()
    hfm = [[e(nc.sbuf_tensor(f"fa_h{pr}{dc}", [128, 512], BF16)) for dc in range(8)]
           for pr in range(2)]
    b1bc = e(nc.sbuf_tensor("fa_b1bc", [128, HID], F32))
    g2bc = e(nc.sbuf_tensor("fa_g2bc", [128, HID], F32))
    bb2bc = e(nc.sbuf_tensor("fa_bb2bc", [128, HID], F32))
    mcc = e(nc.sbuf_tensor("fa_mcc", [128, 1], F32))
    htok = e(nc.sbuf_tensor("fa_htok", [128, HID], F32))
    hG = e(nc.sbuf_tensor("fa_hG", [128, HID], F32))
    h2b = e(nc.sbuf_tensor("fa_h2b", [128, HID], BF16))
    hstage = [e(nc.sbuf_tensor(f"fa_hs{i}", [128, 1024], BF16)) for i in range(2)]
    stt = e(nc.sbuf_tensor("fa_st", [128, 32], F32))
    sqd1 = e(nc.sbuf_tensor("fa_sqd", [128, HID], BF16))
    identb = e(nc.sbuf_tensor("fa_idb2", [128, 128], BF16))
    pms = [e(nc.psum_tensor(f"fa_p{i}", [128, 512], F32)) for i in range(2)]
    pqb = e(nc.psum_tensor("fa_pqb2", [128, 1024], BF16))
    ldw = p.sem("ldw")
    ldg = p.sem("fa_ldg")
    p.dma("sync", ldw, lambda s: s.dma_start(
        out=b1bc[:], in_=io["b1"].partition_broadcast(128)))
    p.dma("sync", ldw, lambda s: s.dma_start(
        out=g2bc[:], in_=io["ln2_g"].partition_broadcast(128)))
    p.dma("sync", ldw, lambda s: s.dma_start(
        out=bb2bc[:], in_=io["ln2_b"].partition_broadcast(128)))
    wldv = p.dma("sync", ldw, lambda s: s.dma_start(
        out=mcc[:], in_=io["sc1"][0:1, 4:5].partition_broadcast(128)))
    cldb = p.dma("gpsimd", ldg, lambda g_: g_.dma_start(out=identb[:],
                                                        in_=io["ident_f"]))
    sem_pe = p.sem("fa_pe")
    sem_d = p.sem("fa_d")
    sem_a = p.sem("fa_a")
    sem_ev = p.sem("fa_ev")
    las = [p.sem("fa_la"), p.sem("fa_lb")]
    slots = StSlots(p, "fa", 2)
    pe_group_end = {}
    evh_bank = {}
    tr_free = None
    waited = False
    for g in range(NG):
        la = las[g % 2]
        hf = hfm[g % 2]
        if g >= 2:
            p.wait("sync", sem_pe, pe_group_end[g - 2])
        for dc in range(8):
            p.dma("sync", la, lambda s, g=g, dc=dc, hf=hf: s.dma_start(
                out=hf[dc][:], in_=io["h1_bf"][g * 512:(g + 1) * 512,
                                               dc * 128:(dc + 1) * 128],
                transpose=True))
        ldv = p.counts[la]
        p.wait("tensor", la, ldv)
        if not waited:
            p.wait("tensor", ldw, wldv)
            p.wait("vector", ldw, wldv)
            p.wait("tensor", ldg, cldb)
            waited = True
        for j in range(4):
            tt = g * 4 + j
            for hg in range(8):
                b = pms[hg % 2]
                if evh_bank.get(hg % 2) is not None:
                    p.wait("tensor", sem_d, evh_bank[hg % 2])
                cpe = mm_acc(p, b[:],
                             [(hf[dc][:, j * 128:(j + 1) * 128],
                               w1q[:, dc * HID + hg * 512: dc * HID + (hg + 1) * 512])
                              for dc in range(8)], sem_pe)
                p.wait("vector", sem_pe, cpe)
                evh_bank[hg % 2] = p.op(
                    "vector", lambda v, hg=hg, b=b: v.scalar_tensor_tensor(
                        htok[:, hg * 512:(hg + 1) * 512], b[:], mcc[:, 0:1],
                        b1bc[:, hg * 512:(hg + 1) * 512], op0=ALU.mult, op1=ALU.add),
                    inc=sem_d, small=(hg == 7))
            p.wait("scalar", sem_d, evh_bank[1])
            cgel = p.op("scalar", lambda s: s.activation(hG[:], htok[:], AF.Gelu),
                        inc=sem_a, small=True)
            p.wait("vector", sem_a, cgel)
            p.wait("scalar", sem_a, cgel)
            cout = ln_tiles(p, "fa", stt, 0, [sqd1], [htok], [hG], [hG], HID,
                            sem_d, sem_a)
            p.wait("vector", sem_d, cout[0])
            cm_ = p.op("vector", lambda v: v.tensor_mul(hG[:], hG[:], g2bc[:]),
                       inc=sem_d)
            p.wait("vector", sem_d, cm_)
            ch2 = p.op("vector", lambda v: v.tensor_add(hG[:], hG[:], bb2bc[:]),
                       inc=sem_d)
            p.wait("vector", sem_d, ch2)
            ch2 = p.op("vector", lambda v: v.tensor_copy(h2b[:], hG[:]),
                       inc=sem_d, small=True)
            p.wait("tensor", sem_d, ch2)
            for g8 in range(4):
                if tr_free is not None:
                    p.wait("tensor", sem_ev, tr_free)
                for t8 in range(8):
                    p.op("tensor", lambda t, g8=g8, t8=t8: t.transpose(
                        pqb[:, t8 * 128:(t8 + 1) * 128],
                        h2b[:, (g8 * 8 + t8) * 128:(g8 * 8 + t8 + 1) * 128],
                        identb[:]),
                        inc=(sem_pe if t8 == 7 else None))
                cpe = p.counts[sem_pe]
                sj = g8 % 2
                slots.guard("scalar", sj)
                p.wait("scalar", sem_pe, cpe)
                cev = p.op("scalar", lambda s, sj=sj: s.activation(
                    hstage[sj][:], pqb[:], AF.Copy), inc=sem_ev)
                tr_free = cev
                p.wait("sync", sem_ev, cev)
                slots.store(sj, lambda s, tt=tt, g8=g8, sj=sj: s.dma_start(
                    out=io["h2f"][tt, g8 * 8:(g8 + 1) * 8, :, :].transpose([1, 0, 2]),
                    in_=hstage[sj][:].rearrange("p (a c) -> p a c", c=128)))
        pe_group_end[g] = p.counts[sem_pe]
    p.barrier()


phase_l1ffna.reads = ("w1", "w2", "b1", "ln2_g", "ln2_b", "h1_bf", "ident_f",
                      "sc1")
phase_l1ffna.writes = ("h2f", "sc1")



def phase_l1ffnb(p, io, cst, scope):
    nc = p.nc
    e = scope.enter_context
    w2q = e(nc.sbuf_tensor("fb_w2q", [128, 32 * D], BF16))
    emit_quantize_wT(p, io, scope, "fb", "w2", D, HID, 5, w2q)
    p.barrier()
    h2fm = [e(nc.sbuf_tensor(f"fb_h{pr}", [128, HID], BF16)) for pr in range(2)]
    b2bc = e(nc.sbuf_tensor("fb_b2bc", [128, D], F32))
    mcc = e(nc.sbuf_tensor("fb_mcc", [128, 1], F32))
    xrt = [e(nc.sbuf_tensor(f"fb_xr{i}", [128, D], F32)) for i in range(2)]
    a2t = [e(nc.sbuf_tensor(f"fb_a2{i}", [128, D], F32)) for i in range(2)]
    ost = [e(nc.sbuf_tensor(f"fb_o{i}", [128, D], F32)) for i in range(2)]
    pms = [e(nc.psum_tensor(f"fb_p{i}", [128, 512], F32)) for i in range(4)]
    banks = Banks(p, pms)
    ldw = p.sem("ldw")
    p.dma("sync", ldw, lambda s: s.dma_start(
        out=b2bc[:], in_=io["b2"].partition_broadcast(128)))
    wldv = p.dma("sync", ldw, lambda s: s.dma_start(
        out=mcc[:], in_=io["sc1"][0:1, 5:6].partition_broadcast(128)))
    sem_pe = p.sem("fb_pe")
    sem_d = p.sem("fb_d")
    lh = [p.sem("fb_lha"), p.sem("fb_lhb")]
    lx = [p.sem("fb_lxa"), p.sem("fb_lxb")]
    slots = StSlots(p, "fb", 2)
    pe_tile_end = {}
    dve_tile_end = {}
    p.wait("vector", ldw, wldv)
    for tt in range(NT):
        pr = tt % 2
        if tt >= 2:
            p.wait("sync", sem_pe, pe_tile_end[tt - 2])
        ch = p.dma("sync", lh[pr], lambda s, tt=tt, pr=pr: s.dma_start(
            out=h2fm[pr][:].rearrange("pp (a c) -> pp a c", c=128),
            in_=io["h2f"][tt, :, :, :].transpose([1, 0, 2])))
        if tt >= 2:
            p.wait("sync", sem_d, dve_tile_end[tt - 2])
        p.dma("sync", lx[pr], lambda s, tt=tt, pr=pr: s.dma_start(
            out=xrt[pr][:], in_=io["xres"][tt * 128:(tt + 1) * 128, :]))
        cx = p.dma("sync", lx[pr], lambda s, tt=tt, pr=pr: s.dma_start(
            out=a2t[pr][:], in_=io["attn2"][tt * 128:(tt + 1) * 128, :]))
        p.wait("tensor", lh[pr], ch)
        slots.guard("vector", pr)
        for half in range(2):
            b = banks.next()
            cpe = mm_acc(p, b[0][:],
                         [(h2fm[pr][:, hc * 128:(hc + 1) * 128],
                           w2q[:, hc * D + half * 512: hc * D + (half + 1) * 512])
                          for hc in range(32)], sem_pe)
            p.wait("vector", sem_pe, cpe)
            p.wait("vector", lx[pr], cx)
            co = p.op("vector", lambda v, pr=pr, half=half, b=b: v.scalar_tensor_tensor(
                ost[pr][:, half * 512:(half + 1) * 512], b[0][:], mcc[:, 0:1],
                b2bc[:, half * 512:(half + 1) * 512], op0=ALU.mult, op1=ALU.add),
                inc=sem_d)
            Banks.release(b, sem_d, co)
            p.wait("vector", sem_d, co)
            co = p.op("vector", lambda v, pr=pr, half=half: v.tensor_add(
                ost[pr][:, half * 512:(half + 1) * 512],
                ost[pr][:, half * 512:(half + 1) * 512],
                xrt[pr][:, half * 512:(half + 1) * 512]), inc=sem_d)
            p.wait("vector", sem_d, co)
            co = p.op("vector", lambda v, pr=pr, half=half: v.tensor_add(
                ost[pr][:, half * 512:(half + 1) * 512],
                ost[pr][:, half * 512:(half + 1) * 512],
                a2t[pr][:, half * 512:(half + 1) * 512]),
                inc=sem_d, small=(half == 1))
        cfin = p.counts[sem_d]
        dve_tile_end[tt] = cfin
        p.wait("sync", sem_d, cfin)
        slots.store(pr, lambda s, tt=tt, pr=pr: s.dma_start(
            out=io["out"][tt * 128:(tt + 1) * 128, :], in_=ost[pr][:]))
        pe_tile_end[tt] = p.counts[sem_pe]
    p.barrier()


phase_l1ffnb.reads = ("w2", "b2", "h2f", "xres", "attn2", "ident_f", "sc1")
phase_l1ffnb.writes = ("out",)




PHASES_ALL = [phase_ln0, phase_rqkv, phase_rattn, phase_rproj,
              phase_mqkv, phase_mattn, phase_mproj, phase_norm1, phase_moe,
              phase_l1prep, phase_l1wcast, phase_l1qkv, phase_l1attn,
              phase_l1proj, phase_l1ffna, phase_l1ffnb]
GROUPS = [PHASES_ALL]


def build_group(phases, produced_before):
    nc = bass.Bass("TRN2", target_bir_lowering=False, debug=False)
    reads = set()
    writes = set()
    for ph in phases:
        reads |= set(ph.reads)
        writes |= set(ph.writes)
    io = {}
    ext_in = []
    ext_out = []
    for name in sorted(reads | writes):
        if name in INPUT_SPECS:
            shape, dt = INPUT_SPECS[name]
            io[name] = nc.dram_tensor(name, list(shape), dt, kind="ExternalInput").ap()
            ext_in.append(name)
        else:
            shape, dt = SCRATCH[name]
            if name in reads and name in produced_before and name not in writes:
                kind = "ExternalInput"
                ext_in.append(name)
            elif name in writes:
                kind = "ExternalOutput"
                ext_out.append(name)
            else:
                kind = "Internal"
            io[name] = nc.dram_tensor(name, list(shape), dt, kind=kind).ap()
    p = Prog(nc)
    with ExitStack() as stack:
        p.dmy = {e: stack.enter_context(nc.sbuf_tensor(f"dmy_{e}", [128, 2], F32))
                 for e in ("vector", "scalar", "gpsimd")}
        cst = None
        for ph in phases:
            with ExitStack() as scope:
                ph(p, io, cst, scope)
                p.barrier()
        p.finish()
        sems_needed = [s for s, c in p.counts.items() if c > 0 or s in p.dma_sems]
        with ExitStack() as semstack:
            h = {s: semstack.enter_context(nc.semaphore(s)) for s in sems_needed}
            with nc.Block() as block:
                p.build(block, h)
    return nc, ext_in, ext_out


def kernel(**inputs):
    full = prepare_inputs(inputs)
    consts = make_consts()
    produced = set()
    scratch_vals = [dict() for _ in range(NCORES)]
    final = None
    for phases in GROUPS:
        nc, ext_in, ext_out = build_group(phases, produced)
        in_maps = []
        for c in range(NCORES):
            m = {}
            for name in ext_in:
                if name in consts:
                    m[name] = consts[name]
                elif name in full:
                    m[name] = full[name] if not isinstance(full[name], list) else full[name][c]
                else:
                    m[name] = scratch_vals[c][name]
            in_maps.append(m)
        res = run_bass_kernel_spmd(nc, in_maps, core_ids=list(range(NCORES)))
        for c in range(NCORES):
            for name in ext_out:
                scratch_vals[c][name] = res.results[c][name]
        produced |= set(ext_out)
    out = np.stack([scratch_vals[c][OUTPUT_NAME] for c in range(NCORES)], axis=0)
    return out


OUTPUT_NAME = "out"


def prepare_inputs(inputs):
    f = {}
    f["x"] = [np.ascontiguousarray(inputs["L0_attn_x"][c]) for c in range(NCORES)]
    ren = {
        "Wqkv": "L0_attn_Wqkv", "bqkv": "L0_attn_bqkv", "Wproj": "L0_attn_Wproj",
        "bproj": "L0_attn_bproj", "gamma": "L0_attn_gamma",
        "qkv_w": "L0_ffn_qkv_w", "qkv_b": "L0_ffn_qkv_b", "out_w": "L0_ffn_out_w",
        "out_b": "L0_ffn_out_b", "norm1_w": "L0_ffn_norm1_w", "norm1_b": "L0_ffn_norm1_b",
        "sw1": "L0_ffn_sw1", "sb1": "L0_ffn_sb1", "sw2": "L0_ffn_sw2", "sb2": "L0_ffn_sb2",
        "rw1": "L0_ffn_rw1", "rb1": "L0_ffn_rb1", "rw2": "L0_ffn_rw2", "rb2": "L0_ffn_rb2",
        "centroid": "L0_ffn_centroid", "route_bias": "L0_ffn_route_bias",
        "norm2_w": "L0_ffn_norm2_w", "norm2_b": "L0_ffn_norm2_b",
        "Wq": "L1_attn_Wq", "Wk": "L1_attn_Wk", "Wv": "L1_attn_Wv", "Wo": "L1_attn_Wo",
        "ln1_g": "L1_ffn_ln1_g", "ln1_b": "L1_ffn_ln1_b", "w1": "L1_ffn_w1",
        "b1": "L1_ffn_b1", "ln2_g": "L1_ffn_ln2_g", "ln2_b": "L1_ffn_ln2_b",
        "w2": "L1_ffn_w2", "b2": "L1_ffn_b2",
    }
    for k, src in ren.items():
        a = np.asarray(inputs[src])
        shape = INPUT_SPECS[k][0]
        f[k] = np.ascontiguousarray(a.reshape(shape))
    return f


# version 07
# speedup vs baseline: 1.2743x; 1.2743x over previous
import numpy as np
from contextlib import ExitStack

import concourse.bass as bass
import concourse.mybir as mybir
from concourse.bass_utils import run_bass_kernel_spmd
import ml_dtypes

F32 = mybir.dt.float32
BF16 = mybir.dt.bfloat16
AF = mybir.ActivationFunctionType
ALU = mybir.AluOpType
AX = mybir.AxisListType

NCORES = 8
S = 4096
D = 1024
NT = S // 128
NG = S // 512
NH, HD = 16, 64
FF = 2048
HID = 4096
EPS = 1e-5

ENGINES = ("sync", "tensor", "vector", "scalar", "gpsimd")


class Prog:

    def __init__(self, nc):
        self.nc = nc
        self.thunks = {e: [] for e in ENGINES}
        self.counts = {}
        self.deferred = {e: [] for e in ENGINES}
        self.ops_since_barrier = {e: 0 for e in ENGINES}
        self.engine_sem = {"tensor": "g_pe", "vector": "g_dve", "scalar": "g_act",
                          "gpsimd": "g_pool", "sync": "g_sy"}
        for s in self.engine_sem.values():
            self.sem(s)
        self.dma_sems = set()
        self.dmy = None

    _SEM_MAP = [
        ("_qlda", "lpA0"), ("_qldb", "lpA1"), ("_qldg", "s_ldg"),
        ("_qla", "lpA0"), ("_qlb", "lpA1"), ("_kla", "lpB0"), ("_klb", "lpB1"),
        ("_vla", "lpC0"), ("_vlb", "lpC1"), ("_khl", "lpB0"), ("_khm", "lpB1"),
        ("_vhl", "lpC0"), ("_vhm", "lpC1"),
        ("_lha", "lpA0"), ("_lhb", "lpA1"), ("_lxa", "lpC0"), ("_lxb", "lpC1"),
        ("_lda", "lpA0"), ("_ldb", "lpA1"), ("_ldi", "s_ldc"), ("_ldc", "s_ldc"),
        ("_ldg", "s_ldg"), ("_qld", "s_ldc"), ("_lx", "lpB0"),
        ("_la", "lpA0"), ("_lb", "lpA1"), ("_ld", "s_ldx"),
        ("_qd", "s_d"), ("_qa", "s_a"), ("_qev", "s_ev"), ("_qpe", "s_pe"),
        ("_pe", "s_pe"), ("_ev", "s_ev"), ("_pl", "s_pl"),
        ("_d", "s_d"), ("_a", "s_a"), ("_y", "s_y"),
    ]

    def _canon(self, name):
        if name.startswith("g_") or name in ("st", "stg", "ldw"):
            return name
        if name == "ldwg":
            return "s_ldg"
        if "_ss" in name:
            return "sl" + name[name.index("_ss"):]
        for suf, c in self._SEM_MAP:
            if name.endswith(suf):
                return c
        return name

    def sem(self, name):
        name = self._canon(name)
        if name not in self.counts:
            self.counts[name] = 0
        return name

    def wait(self, eng, sem, val):
        sem = self._canon(sem)
        assert sem in self.counts, sem
        assert val <= self.counts[sem], (sem, val, self.counts[sem])
        if val > 0:
            self.thunks[eng].append(("wait", sem, val))

    def op(self, eng, fn, inc=None, small=False):
        inc = self._canon(inc) if inc is not None else None
        """Emit one engine op, optionally incrementing `inc` on completion.

        small=True: the op produces a tiny (reduce/stat) output — also emit a
        trailing dummy op carrying a second increment, so hardware consumers
        (which may observe the first inc before writeback lands) get extra
        slack, while the simulator's race model is satisfied by the first.
        Returns the count the consumer should wait for."""
        incs = []
        ret = None
        if inc is not None:
            self.sem(inc)
            self.counts[inc] += 1
            ret = self.counts[inc]
            incs.append((inc, 1))
        self.thunks[eng].append(["op", fn, incs])
        self.ops_since_barrier[eng] += 1
        if small and inc is not None:
            ret = self.op(eng, self._drain_fn(eng), inc=inc)
        return ret

    def dma(self, eng, sem, fn):
        sem = self._canon(sem)
        self.sem(sem)
        self.dma_sems.add(sem)
        self.counts[sem] += 16
        self.thunks[eng].append(["op", fn, [(sem, 16)]])
        self.ops_since_barrier[eng] += 1
        return self.counts[sem]

    def _dummy_fn(self, eng):
        return lambda e: e.nop()

    def _drain_fn(self, eng):
        if eng == "sync":
            return lambda e: e.nop()
        return lambda e: e.drain(fusable=True)

    def flush_deferred(self, eng):
        while self.deferred[eng]:
            self.op(eng, self._dummy_fn(eng))

    def _last_real_op(self, eng):
        for item in reversed(self.thunks[eng]):
            if item[0] == "op":
                return item
        return None

    def barrier(self):
        active = [e for e in ENGINES
                  if self.ops_since_barrier[e] > 0 or self.deferred[e]]
        for e in active:
            gs = self.engine_sem[e]
            self.op(e, self._drain_fn(e), inc=gs)
        targets = [(self.engine_sem[e], self.counts[self.engine_sem[e]]) for e in active]
        dmas = [(s, self.counts[s]) for s in sorted(self.dma_sems) if self.counts[s] > 0]
        for e in ENGINES:
            for s, v in targets:
                if self.engine_sem[e] == s:
                    continue
                self.wait(e, s, v)
            for s, v in dmas:
                self.wait(e, s, v)
        for e in ENGINES:
            self.ops_since_barrier[e] = 0

    def finish(self):
        for e in ENGINES:
            self.flush_deferred(e)
        for s in sorted(self.dma_sems):
            if self.counts[s] > 0:
                self.wait("sync", s, self.counts[s])

    def build(self, block, h):
        for ename in ENGINES:
            thunks = self.thunks[ename]
            if not thunks:
                continue

            def body(engine, thunks=thunks):
                for item in thunks:
                    if item[0] == "wait":
                        engine.wait_ge(h[item[1]], item[2])
                    else:
                        ins = item[1](engine)
                        assert len(item[2]) <= 1, item[2]
                        for s, n in item[2]:
                            ins.then_inc(h[s], n)

            getattr(block, ename)(body)


class Banks:

    def __init__(self, p, psums):
        self.p = p
        self.banks = [[ps, None] for ps in psums]
        self.i = 0

    def next(self):
        b = self.banks[self.i % len(self.banks)]
        self.i += 1
        if b[1] is not None:
            self.p.wait("tensor", b[1][0], b[1][1])
            b[1] = None
        return b

    @staticmethod
    def release(b, sem, count):
        b[1] = (sem, count)


class StSlots:

    def __init__(self, p, tag, n):
        self.p = p
        self.sems = [p.sem(f"{tag}_ss{i}") for i in range(n)]
        self.last = [0] * n

    def guard(self, eng, i):
        if self.last[i] > 0:
            self.p.wait(eng, self.sems[i], self.last[i])

    def store(self, i, fn):
        self.last[i] = self.p.dma("sync", self.sems[i], fn)


def load_bias_cols(p, sem, dst, src2d, ncols):
    last = None
    for c in range(ncols):
        last = p.dma("sync", sem, lambda s, c=c: s.dma_start(
            out=dst[:, c:c + 1],
            in_=src2d[0:1, c * 128:(c + 1) * 128].rearrange("o (p q) -> (o p) q", q=1)))
    return last


def mm_acc(p, out_ap, pairs, sem_pe):
    n = len(pairs)
    for i, (l, r) in enumerate(pairs):
        p.op("tensor",
             lambda t, l=l, r=r, st=(i == 0), sp=(i == n - 1), o=out_ap:
             t.matmul(o, l, r, start=st, stop=sp),
             inc=(sem_pe if i == n - 1 else None))
    return p.counts[sem_pe]



SCRATCH = {
    "xn_f32": ((S, D), F32),
    "xn_bf": ((S, D), BF16),
    "qf": ((D, S), BF16),
    "kf": ((D, S), BF16),
    "vt": ((S, D), BF16),
    "of": ((D, S), BF16),
    "attn_br": ((S, D), F32),
    "x1_f32": ((S, D), F32),
    "x1_bf": ((S, D), BF16),
    "xres": ((S, D), F32),
    "xn2_bf": ((S, D), BF16),
    "h1_bf": ((S, D), BF16),
    "attn2": ((S, D), F32),
    "wqT_src": ((D, D), BF16),
    "wkT_src": ((D, D), BF16),
    "wvT_src": ((D, D), BF16),
    "woT_src": ((D, D), BF16),
    "h2f": ((NT, HID // 128, 128, 128), BF16),
    "ymha": ((S, D), F32),
    "ymoe": ((S, D), F32),
    "out": ((S, D), F32),
    "sc1": ((1, 8), F32),
}

INPUT_SPECS = {
    "x": ((S, D), F32),
    "Wqkv": ((D, 3 * D), F32), "bqkv": ((1, 3 * D), F32),
    "Wproj": ((D, D), F32), "bproj": ((1, D), F32), "gamma": ((1, 1), F32),
    "qkv_w": ((D, 3 * D), F32), "qkv_b": ((1, 3 * D), F32),
    "out_w": ((D, D), F32), "out_b": ((1, D), F32),
    "norm1_w": ((1, D), F32), "norm1_b": ((1, D), F32),
    "sw1": ((D, FF), F32), "sb1": ((1, FF), F32),
    "sw2": ((FF, D), F32), "sb2": ((1, D), F32),
    "rw1": ((8, D, FF), F32), "rb1": ((8, FF), F32),
    "rw2": ((8, FF, D), F32), "rb2": ((8, D), F32),
    "centroid": ((8, D), F32), "route_bias": ((1, 8), F32),
    "norm2_w": ((1, D), F32), "norm2_b": ((1, D), F32),
    "Wq": ((D, D), F32), "Wk": ((D, D), F32), "Wv": ((D, D), F32), "Wo": ((D, D), F32),
    "ln1_g": ((1, D), F32), "ln1_b": ((1, D), F32),
    "w1": ((HID, D), F32), "b1": ((1, HID), F32),
    "ln2_g": ((1, HID), F32), "ln2_b": ((1, HID), F32),
    "w2": ((D, HID), F32), "b2": ((1, D), F32),
    "ident_f": ((128, 128), F32),
    "diffT": ((128, 128), F32), "triT": ((128, 128), F32),
    "ones_f": ((1, 128), F32), "onescol_b": ((128, 1), BF16),
    "triT_b": ((128, 128), BF16), "neg20": ((1, 1), F32),
}


def make_consts():
    i = np.arange(128)
    return {
        "ident_f": np.eye(128, dtype=np.float32),
        "diffT": (i[None, :] - i[:, None]).astype(np.float32),
        "triT": (i[None, :] >= i[:, None]).astype(np.float32),
        "ones_f": np.ones((1, 128), np.float32),
        "onescol_b": np.ones((128, 1)).astype(ml_dtypes.bfloat16),
        "triT_b": (i[None, :] >= i[:, None]).astype(ml_dtypes.bfloat16),
        "neg20": np.full((1, 1), -20.0, np.float32),
    }



def emit_ln_stream(p, io, cst, scope, tag, src_name, dst_f32, dst_bf,
                   extra_ln=None):
    nc = p.nc
    e = scope.enter_context
    st = e(nc.sbuf_tensor(tag + "_stat", [128, 64], F32))
    xt = [e(nc.sbuf_tensor(f"{tag}_xt{j}", [128, D], F32)) for j in range(4)]
    xm = [e(nc.sbuf_tensor(f"{tag}_xm{j}", [128, D], F32)) for j in range(4)]
    xnb = [e(nc.sbuf_tensor(f"{tag}_xnb{j}", [128, D], F32)) for j in range(4)]
    bf = [e(nc.sbuf_tensor(f"{tag}_bf{j}", [128, D], BF16)) for j in range(4)]
    sqd = [e(nc.sbuf_tensor(f"{tag}_sqd{j}", [128, D], BF16)) for j in range(4)]
    if extra_ln is not None:
        g_name, b_name, dst2_bf = extra_ln
        gb = e(nc.sbuf_tensor(tag + "_g", [128, D], F32))
        bb = e(nc.sbuf_tensor(tag + "_b", [128, D], F32))
        h2 = [e(nc.sbuf_tensor(f"{tag}_h2{j}", [128, D], F32)) for j in range(4)]
        bf2 = [e(nc.sbuf_tensor(f"{tag}_bf2{j}", [128, D], BF16)) for j in range(4)]
        ldw = p.sem("ldw")
        p.dma("sync", ldw, lambda s: s.dma_start(
            out=gb[:], in_=io[g_name].partition_broadcast(128)))
        wv = p.dma("sync", ldw, lambda s: s.dma_start(
            out=bb[:], in_=io[b_name].partition_broadcast(128)))
        p.wait("vector", ldw, wv)

    sem_ld = p.sem(tag + "_ld")
    sem_d = p.sem(tag + "_d")
    sem_a = p.sem(tag + "_a")
    sem_st = p.sem("st")
    src = io[src_name] if isinstance(src_name, str) else src_name

    def one_ln(tiles_in, out_tiles, colbase):
        cb = colbase
        cnm, cxm, csq = {}, {}, {}
        for j in range(4):
            cnm[j] = p.op("scalar",
                          lambda s, j=j: s.activation(sqd[j][:], tiles_in[j][:],
                                                      AF.Copy, scale=-1.0 / D,
                                                      accum_out=st[:, cb + j:cb + j + 1]),
                          inc=sem_a, small=True)
        for j in range(4):
            p.wait("vector", sem_a, cnm[j])
            cxm[j] = p.op("vector",
                          lambda v, j=j: v.tensor_scalar(xm[j][:], tiles_in[j][:],
                                                         st[:, cb + j:cb + j + 1],
                                                         None, op0=ALU.add),
                          inc=sem_d, small=True)
        for j in range(4):
            p.wait("scalar", sem_d, cxm[j])
            csq[j] = p.op("scalar",
                          lambda s, j=j: s.activation(sqd[j][:], xm[j][:], AF.Square,
                                                      accum_out=st[:, cb + 8 + j:cb + 9 + j]),
                          inc=sem_a, small=True)
        p.wait("vector", sem_a, csq[3])
        cv = p.op("vector",
                  lambda v: v.tensor_scalar(st[:, cb + 16:cb + 20],
                                            st[:, cb + 8:cb + 12],
                                            1.0 / D, EPS, op0=ALU.mult, op1=ALU.add),
                  inc=sem_d, small=True)
        p.wait("scalar", sem_d, cv)
        cl = p.op("scalar", lambda s: s.activation(st[:, cb + 20:cb + 24],
                                                   st[:, cb + 16:cb + 20], AF.Ln),
                  inc=sem_a, small=True)
        p.wait("vector", sem_a, cl)
        cln = p.op("vector",
                   lambda v: v.tensor_scalar(st[:, cb + 16:cb + 20],
                                             st[:, cb + 20:cb + 24],
                                             -0.5, None, op0=ALU.mult),
                   inc=sem_d, small=True)
        p.wait("scalar", sem_d, cln)
        ca = p.op("scalar", lambda s: s.activation(st[:, cb + 24:cb + 28],
                                                   st[:, cb + 16:cb + 20], AF.Exp),
                  inc=sem_a, small=True)
        cout = {}
        for j in range(4):
            p.wait("vector", sem_a, ca)
            cout[j] = p.op("vector",
                           lambda v, j=j: v.tensor_scalar(out_tiles[j][:], xm[j][:],
                                                          st[:, cb + 24 + j:cb + 25 + j],
                                                          None, op0=ALU.mult),
                           inc=sem_d, small=True)
        return cout

    for g in range(NG):
        for j in range(4):
            tt = g * 4 + j
            c = p.dma("sync", sem_ld, lambda s, tt=tt, j=j: s.dma_start(
                out=xt[j][:], in_=src[tt * 128:(tt + 1) * 128, :]))
        p.wait("vector", sem_ld, c)
        p.wait("scalar", sem_ld, c)
        cout = one_ln(xt, xnb, 0)
        for j in range(4):
            tt = g * 4 + j
            if dst_f32 is not None:
                p.wait("sync", sem_d, cout[j])
                p.dma("sync", sem_st, lambda s, tt=tt, j=j: s.dma_start(
                    out=io[dst_f32][tt * 128:(tt + 1) * 128, :], in_=xnb[j][:]))
            p.wait("scalar", sem_d, cout[j])
            cc = p.op("scalar", lambda s, j=j: s.copy(bf[j][:], xnb[j][:]),
                      inc=sem_a)
            p.wait("sync", sem_a, cc)
            p.dma("sync", sem_st, lambda s, tt=tt, j=j: s.dma_start(
                out=io[dst_bf][tt * 128:(tt + 1) * 128, :], in_=bf[j][:]))
        if extra_ln is not None:
            cout2 = one_ln(xnb, h2, 32)
            for j in range(4):
                tt = g * 4 + j
                p.wait("gpsimd", sem_d, cout2[j])
                cw = p.op("gpsimd", lambda v, j=j: v.tensor_mul(h2[j][:], h2[j][:], gb[:]),
                          inc="g_pool", small=True)
                p.wait("gpsimd", "g_pool", cw)
                cw = p.op("gpsimd", lambda v, j=j: v.tensor_add(h2[j][:], h2[j][:], bb[:]),
                          inc="g_pool", small=True)
                p.wait("scalar", "g_pool", cw)
                cc = p.op("scalar", lambda s, j=j: s.copy(bf2[j][:], h2[j][:]),
                          inc=sem_a)
                p.wait("sync", sem_a, cc)
                p.dma("sync", sem_st, lambda s, tt=tt, j=j: s.dma_start(
                    out=io[dst2_bf][tt * 128:(tt + 1) * 128, :], in_=bf2[j][:]))
        p.barrier()


def phase_ln0(p, io, cst, scope):
    emit_ln_stream(p, io, cst, scope, "ln0", "x", "xn_f32", "xn_bf")


phase_ln0.reads = ("x",)
phase_ln0.writes = ("xn_f32", "xn_bf")



def emit_qkv(p, io, cst, scope, tag, src_bf, w_name, b_name):
    nc = p.nc
    e = scope.enter_context
    wsb = [e(nc.sbuf_tensor(f"{tag}_w{dc}", [128, 3 * D], BF16)) for dc in range(8)]
    bcols = e(nc.sbuf_tensor(f"{tag}_bc", [128, 24], F32))
    bvb = e(nc.sbuf_tensor(f"{tag}_bvb", [128, D], F32))
    xfm = [[e(nc.sbuf_tensor(f"{tag}_x{pr}{dc}", [128, 512], BF16)) for dc in range(8)]
           for pr in range(2)]
    stage = [e(nc.sbuf_tensor(f"{tag}_s{i}", [128, 512], BF16)) for i in range(6)]
    pms = [e(nc.psum_tensor(f"{tag}_p{i}", [128, 512], F32)) for i in range(4)]
    banks = Banks(p, pms)

    ldw = p.sem("ldw")
    ldwg = p.sem("ldwg")
    for dc in range(8):
        wldg = p.dma("gpsimd", ldwg, lambda g, dc=dc: g.dma_start(
            out=wsb[dc][:], in_=io[w_name][dc * 128:(dc + 1) * 128, :]))
    load_bias_cols(p, ldw, bcols, io[b_name], 24)
    wldv = p.dma("sync", ldw, lambda s: s.dma_start(
        out=bvb[:], in_=io[b_name][:, 2 * D:3 * D].partition_broadcast(128)))

    sem_pe = p.sem(tag + "_pe")
    sem_ev = p.sem(tag + "_ev")
    las = [p.sem(tag + "_la"), p.sem(tag + "_lb")]
    slots = StSlots(p, tag, 6)
    pe_group_end = {}
    waited_w = False
    for g in range(NG):
        la = las[g % 2]
        xf = xfm[g % 2]
        if g >= 2:
            p.wait("sync", sem_pe, pe_group_end[g - 2])
        for dc in range(8):
            p.dma("sync", la, lambda s, g=g, dc=dc, xf=xf: s.dma_start(
                out=xf[dc][:], in_=io[src_bf][g * 512:(g + 1) * 512,
                                              dc * 128:(dc + 1) * 128],
                transpose=True))
        ldv = p.counts[la]
        p.wait("tensor", la, ldv)
        if not waited_w:
            p.wait("tensor", ldw, wldv)
            p.wait("tensor", ldwg, wldg)
            p.wait("scalar", ldw, wldv)
            p.wait("vector", ldw, wldv)
            p.wait("vector", ldwg, wldg)
            waited_w = True
        si = 0
        for mc in range(16):
            b = banks.next()
            cpe = mm_acc(p, b[0][:],
                         [(wsb[dc][:, mc * 128:(mc + 1) * 128], xf[dc][:])
                          for dc in range(8)], sem_pe)
            sj = si % 6
            si += 1
            slots.guard("vector", sj)
            p.wait("vector", sem_pe, cpe)
            cev = p.op("vector", lambda v, mc=mc, sj=sj, b=b: v.tensor_scalar(
                stage[sj][:], b[0][:], bcols[:, mc:mc + 1], None, op0=ALU.add),
                inc=sem_ev)
            Banks.release(b, sem_ev, cev)
            dst = io["qf"] if mc < 8 else io["kf"]
            r = (mc % 8) * 128
            p.wait("sync", sem_ev, cev)
            slots.store(sj, lambda s, dst=dst, r=r, g=g, sj=sj: s.dma_start(
                out=dst[r:r + 128, g * 512:(g + 1) * 512], in_=stage[sj][:]))
        for tt in range(4):
            for half in range(2):
                b = banks.next()
                cpe = mm_acc(p, b[0][:],
                             [(xf[dc][:, tt * 128:(tt + 1) * 128],
                               wsb[dc][:, 2 * D + half * 512: 2 * D + (half + 1) * 512])
                              for dc in range(8)], sem_pe)
                sj = si % 6
                si += 1
                slots.guard("vector", sj)
                p.wait("vector", sem_pe, cpe)
                cev = p.op("vector", lambda v, sj=sj, b=b, half=half: v.tensor_tensor(
                    stage[sj][:], b[0][:], bvb[:, half * 512:(half + 1) * 512],
                    op=ALU.add), inc=sem_ev)
                Banks.release(b, sem_ev, cev)
                row = (g * 4 + tt) * 128
                p.wait("sync", sem_ev, cev)
                slots.store(sj, lambda s, row=row, half=half, sj=sj: s.dma_start(
                    out=io["vt"][row:row + 128, half * 512:(half + 1) * 512],
                    in_=stage[sj][:]))
        pe_group_end[g] = p.counts[sem_pe]
    p.barrier()


def phase_rqkv(p, io, cst, scope):
    emit_qkv(p, io, cst, scope, "rqkv", "xn_bf", "Wqkv", "bqkv")


phase_rqkv.reads = ("xn_bf", "Wqkv", "bqkv")
phase_rqkv.writes = ("qf", "kf", "vt")


def phase_mqkv(p, io, cst, scope):
    emit_qkv(p, io, cst, scope, "mqkv", "xn_bf", "qkv_w", "qkv_b")


phase_mqkv.reads = ("xn_bf", "qkv_w", "qkv_b")
phase_mqkv.writes = ("qf", "kf", "vt")



def phase_rattn(p, io, cst, scope):
    nc = p.nc
    e = scope.enter_context
    Pb = e(nc.sbuf_tensor("ra_P", [128, 32 * 512], BF16))
    Sd = [e(nc.sbuf_tensor(f"ra_Sd{i}", [128, 512], F32)) for i in range(2)]
    qg = [[e(nc.sbuf_tensor(f"ra_q{pr}{dc}", [128, 512], BF16)) for dc in range(8)]
          for pr in range(2)]
    kblk = [e(nc.sbuf_tensor(f"ra_k{i}", [128, D], BF16)) for i in range(2)]
    vblk = [e(nc.sbuf_tensor(f"ra_v{i}", [128, D], BF16)) for i in range(2)]
    maskT = e(nc.sbuf_tensor("ra_mask", [128, 128], F32))
    diffT = e(nc.sbuf_tensor("ra_diff", [128, 128], F32))
    triT = e(nc.sbuf_tensor("ra_tri", [128, 128], F32))
    onesf = e(nc.sbuf_tensor("ra_onesf", [1, 128], F32))
    onescb = e(nc.sbuf_tensor("ra_onescb", [128, 1], BF16))
    gt = e(nc.sbuf_tensor("ra_gt", [1, 8], F32))
    lngb = e(nc.sbuf_tensor("ra_lngb", [128, 1], F32))
    m0 = e(nc.sbuf_tensor("ra_m0", [128, 128], F32))
    n20 = e(nc.sbuf_tensor("ra_n20", [128, 1], F32))
    rowr = e(nc.sbuf_tensor("ra_rowr", [1, 512], F32))
    rbc = e(nc.sbuf_tensor("ra_rbc", [128, 512], F32))
    ost = [e(nc.sbuf_tensor(f"ra_o{i}", [128, 512], BF16)) for i in range(4)]
    pms = [e(nc.psum_tensor(f"ra_p{i}", [128, 512], F32)) for i in range(8)]
    ptr = None

    ldc = p.sem("ra_ldc")
    sem_pe = p.sem("ra_pe")
    sem_ev = p.sem("ra_ev")
    sem_d = p.sem("ra_d")
    sem_a = p.sem("ra_a")
    sem_pl = p.sem("ra_pl")
    sem_st = p.sem("st")
    sem_mk = p.sem("ra_mk")

    for buf, src in ((diffT, "diffT"), (triT, "triT"),
                     (onesf, "ones_f"), (onescb, "onescol_b")):
        p.dma("sync", ldc, lambda s, buf=buf, src=src: s.dma_start(out=buf[:], in_=io[src]))
    p.dma("sync", ldc, lambda s: s.dma_start(
        out=n20[:], in_=io["neg20"].partition_broadcast(128)))
    cg = p.dma("sync", ldc, lambda s: s.dma_start(out=gt[0:1, 0:1], in_=io["gamma"]))

    p.wait("scalar", ldc, cg)
    ca = p.op("scalar", lambda s: s.activation(gt[0:1, 1:2], gt[0:1, 0:1], AF.Ln),
              inc=sem_a, small=True)
    p.wait("sync", sem_a, ca)
    cs1 = p.dma("sync", sem_mk, lambda s: s.dma_start(out=io["sc1"][0:1, 0:1],
                                                      in_=gt[0:1, 1:2]))
    p.wait("sync", sem_mk, cs1)
    cs2 = p.dma("sync", sem_mk, lambda s: s.dma_start(
        out=lngb[:], in_=io["sc1"][0:1, 0:1].partition_broadcast(128)))
    p.wait("vector", sem_mk, cs2)
    p.wait("vector", ldc, p.counts[ldc])
    cd = p.op("vector", lambda v: v.tensor_scalar(m0[:], diffT[:], lngb[:, 0:1],
                                                  None, op0=ALU.mult),
              inc=sem_d, small=True)
    p.wait("scalar", sem_d, cd)
    ca = p.op("scalar", lambda s: s.activation(m0[:], m0[:], AF.Exp),
              inc=sem_a, small=True)
    p.wait("vector", sem_a, ca)
    cm = p.op("vector", lambda v: v.tensor_mul(maskT[:], m0[:], triT[:]),
              inc=sem_d, small=True)
    mask_ready = cm

    qlas = [p.sem("ra_qla"), p.sem("ra_qlb")]
    klas = [p.sem("ra_kla"), p.sem("ra_klb")]
    vlas = [p.sem("ra_vla"), p.sem("ra_vlb")]
    kf_r = io["kf"].rearrange("(a p) s -> p a s", p=128)
    pe_group_end = {}
    pe_kc = {}
    pe_pv_kc = {}
    ev_group_end = {}
    dve_pv_end = {}
    exp_count = {}
    ev_of_bank = {}
    oslots = StSlots(p, "ra_o", 4)
    waited_mask = False
    for g in range(NG):
        qla = qlas[g % 2]
        qf = qg[g % 2]
        if g >= 2:
            p.wait("sync", sem_pe, pe_group_end[g - 2])
        if g > 0:
            p.wait("tensor", sem_d, dve_pv_end[g - 1])
        for dc in range(8):
            p.dma("sync", qla, lambda s, g=g, dc=dc, qf=qf: s.dma_start(
                out=qf[dc][:], in_=io["qf"][dc * 128:(dc + 1) * 128,
                                            g * 512:(g + 1) * 512]))
        qldv = p.counts[qla]
        p.wait("tensor", qla, qldv)
        if not waited_mask:
            p.wait("gpsimd", sem_d, mask_ready)
            waited_mask = True
        if g > 0:
            p.wait("scalar", sem_pe, pe_group_end[g - 1])
        cexp = {}
        for kc in range(32):
            kla = klas[kc % 2]
            kb = kblk[kc % 2]
            guard = pe_kc.get((g, kc - 2)) or (pe_group_end.get(g - 1) if kc < 2 else None)
            if guard is not None:
                p.wait("sync", sem_pe, guard)
            p.dma("sync", kla, lambda s, kc=kc, kb=kb: s.dma_start(
                out=kb[:].rearrange("p (a f) -> p a f", f=128),
                in_=kf_r[:, :, kc * 128:(kc + 1) * 128]))
            p.wait("tensor", kla, p.counts[kla])
            b2 = pms[kc % 2]
            if (kc % 2) in ev_of_bank:
                p.wait("tensor", sem_a, ev_of_bank[kc % 2])
            cpe = mm_acc(p, b2[:],
                         [(kb[:, dc * 128:(dc + 1) * 128], qf[dc][:])
                          for dc in range(8)], sem_pe)
            pe_kc[(g, kc)] = cpe
            p.wait("scalar", sem_pe, cpe)
            if 4 * g <= kc < 4 * g + 4:
                tt = kc - 4 * g
                sdj = kc % 2
                cev = p.op("scalar", lambda s, b2=b2, sdj=sdj: s.activation(
                    Sd[sdj][:], b2[:], AF.Copy, scale=1.0 / 32.0), inc=sem_a)
                ev_of_bank[kc % 2] = cev
                p.wait("gpsimd", sem_a, cev)
                cpl = p.op("gpsimd", lambda gp, tt=tt, sdj=sdj: gp.tensor_mul(
                    Sd[sdj][:, tt * 128:(tt + 1) * 128],
                    Sd[sdj][:, tt * 128:(tt + 1) * 128],
                    maskT[:]), inc=sem_pl, small=True)
                p.wait("scalar", sem_pl, cpl)
                cexp[kc] = p.op("scalar", lambda s, kc=kc, sdj=sdj: s.activation(
                    Pb[:, kc * 512:(kc + 1) * 512], Sd[sdj][:], AF.Exp,
                    bias=n20[:, 0:1]), inc=sem_a)
            else:
                cexp[kc] = p.op("scalar", lambda s, kc=kc, b2=b2: s.activation(
                    Pb[:, kc * 512:(kc + 1) * 512], b2[:], AF.Exp,
                    bias=n20[:, 0:1], scale=1.0 / 32.0), inc=sem_a)
                ev_of_bank[kc % 2] = cexp[kc]
        exp_count[g] = cexp[31]
        for kc in range(32):
            p.wait("tensor", sem_a, cexp[kc])
            p.op("tensor", lambda t, kc=kc: t.matmul(
                pms[2][0:1, :], onescb[:], Pb[:, kc * 512:(kc + 1) * 512],
                start=(kc == 0), stop=(kc == 31)),
                inc=(sem_pe if kc == 31 else None))
        cpe = p.counts[sem_pe]
        p.wait("vector", sem_pe, cpe)
        cd = p.op("vector", lambda v: v.reciprocal(rowr[0:1, :], pms[2][0:1, :]),
                  inc=sem_d, small=True)
        p.wait("tensor", sem_d, cd)
        cpe = p.op("tensor", lambda t: t.matmul(pms[3][:], onesf[:], rowr[0:1, :],
                                                start=True, stop=True), inc=sem_pe)
        p.wait("scalar", sem_pe, cpe)
        crbc = p.op("scalar", lambda s: s.activation(rbc[:], pms[3][:], AF.Copy),
                    inc=sem_a)
        p.wait("tensor", sem_a, exp_count[g])
        p.wait("tensor", sem_d, cd)
        p.wait("tensor", sem_a, crbc)
        for kc in range(32):
            vla = vlas[kc % 2]
            vb = vblk[kc % 2]
            guard = pe_pv_kc.get((g, kc - 2)) or (pe_group_end.get(g - 1) if kc < 2 else None)
            if guard is not None:
                p.wait("sync", sem_pe, guard)
            row = kc * 128
            p.dma("sync", vla, lambda s, row=row, vb=vb: s.dma_start(
                out=vb[:], in_=io["vt"][row:row + 128, :]))
            p.wait("tensor", vla, p.counts[vla])
            for dc in range(8):
                p.op("tensor", lambda t, kc=kc, dc=dc, vb=vb: t.matmul(
                    pms[dc][:], vb[:, dc * 128:(dc + 1) * 128],
                    Pb[:, kc * 512:(kc + 1) * 512],
                    start=(kc == 0), stop=(kc == 31)),
                    inc=(sem_pe if dc == 7 else None))
            pe_pv_kc[(g, kc)] = p.counts[sem_pe]
        cpe = p.counts[sem_pe]
        for dc in range(8):
            sj = dc % 4
            p.wait("vector", sem_pe, cpe)
            oslots.guard("vector", sj)
            cev = p.op("vector", lambda v, dc=dc, sj=sj: v.tensor_tensor(
                ost[sj][:], pms[dc][:], rbc[:], op=ALU.mult), inc=sem_d)
            p.wait("sync", sem_d, cev)
            oslots.store(sj, lambda s, dc=dc, g=g, sj=sj: s.dma_start(
                out=io["of"][dc * 128:(dc + 1) * 128, g * 512:(g + 1) * 512],
                in_=ost[sj][:]))
        dve_pv_end[g] = p.counts[sem_d]
        pe_group_end[g] = p.counts[sem_pe]
    p.barrier()


phase_rattn.reads = ("qf", "kf", "vt", "gamma", "diffT", "triT",
                     "ones_f", "onescol_b", "sc1", "neg20")
phase_rattn.writes = ("of", "sc1")



def emit_proj(p, io, cst, scope, tag, w_name, b_name, dst, src="of",
              add_src=None):
    nc = p.nc
    e = scope.enter_context
    wsb = [e(nc.sbuf_tensor(f"{tag}_w{dc}", [128, D], BF16)) for dc in range(8)]
    bpb = e(nc.sbuf_tensor(f"{tag}_b", [128, D], F32))
    ofm = [[e(nc.sbuf_tensor(f"{tag}_o{pr}{dc}", [128, 512], BF16)) for dc in range(8)]
           for pr in range(2)]
    stage = [e(nc.sbuf_tensor(f"{tag}_s{i}", [128, 512], F32)) for i in range(4)]
    pms = [e(nc.psum_tensor(f"{tag}_p{i}", [128, 512], F32)) for i in range(4)]
    banks = Banks(p, pms)

    ldw = p.sem("ldw")
    ldwg = p.sem("ldwg")
    for dc in range(8):
        wldg = p.dma("gpsimd", ldwg, lambda g, dc=dc: g.dma_start(
            out=wsb[dc][:], in_=io[w_name][dc * 128:(dc + 1) * 128, :]))
    wldv = p.dma("sync", ldw, lambda s: s.dma_start(
        out=bpb[:], in_=io[b_name].partition_broadcast(128)))

    sem_pe = p.sem(tag + "_pe")
    sem_ev = p.sem(tag + "_ev")
    las = [p.sem(tag + "_la"), p.sem(tag + "_lb")]
    slots = StSlots(p, tag, 4)
    pe_group_end = {}
    waited = False
    for g in range(NG):
        la = las[g % 2]
        of = ofm[g % 2]
        if g >= 2:
            p.wait("sync", sem_pe, pe_group_end[g - 2])
        for dc in range(8):
            p.dma("sync", la, lambda s, g=g, dc=dc, of=of: s.dma_start(
                out=of[dc][:], in_=io[src][dc * 128:(dc + 1) * 128,
                                           g * 512:(g + 1) * 512]))
        ldv = p.counts[la]
        p.wait("tensor", la, ldv)
        if not waited:
            p.wait("tensor", ldw, wldv)
            p.wait("tensor", ldwg, wldg)
            p.wait("vector", ldw, wldv)
            waited = True
        si = 0
        for tt in range(4):
            for half in range(2):
                b = banks.next()
                cpe = mm_acc(p, b[0][:],
                             [(of[dc][:, tt * 128:(tt + 1) * 128],
                               wsb[dc][:, half * 512:(half + 1) * 512])
                              for dc in range(8)], sem_pe)
                sj = si % 4
                si += 1
                slots.guard("vector", sj)
                p.wait("vector", sem_pe, cpe)
                cev = p.op("vector", lambda v, sj=sj, b=b, half=half: v.tensor_tensor(
                    stage[sj][:], b[0][:], bpb[:, half * 512:(half + 1) * 512],
                    op=ALU.add), inc=sem_ev)
                Banks.release(b, sem_ev, cev)
                row = (g * 4 + tt) * 128
                p.wait("sync", sem_ev, cev)
                slots.store(sj, lambda s, row=row, half=half, sj=sj: s.dma_start(
                    out=io[dst][row:row + 128, half * 512:(half + 1) * 512],
                    in_=stage[sj][:]))
        pe_group_end[g] = p.counts[sem_pe]
    p.barrier()


def phase_rproj(p, io, cst, scope):
    emit_proj(p, io, cst, scope, "rproj", "Wproj", "bproj", "attn_br")


phase_rproj.reads = ("of", "Wproj", "bproj")
phase_rproj.writes = ("attn_br",)


def phase_mproj(p, io, cst, scope):
    emit_proj(p, io, cst, scope, "mproj", "out_w", "out_b", "ymha")


phase_mproj.reads = ("of", "out_w", "out_b")
phase_mproj.writes = ("ymha",)



def ln_tiles(p, tag, st, cb, sqd, xm, tiles_in, out_tiles, width, sem_d, sem_a):
    n = len(tiles_in)
    cnm, cxm, csq = {}, {}, {}
    for j in range(n):
        cnm[j] = p.op("scalar",
                      lambda s, j=j: s.activation(sqd[j][:, 0:width],
                                                  tiles_in[j][:, 0:width],
                                                  AF.Copy, scale=-1.0 / width,
                                                  accum_out=st[:, cb + j:cb + j + 1]),
                      inc=sem_a, small=True)
    for j in range(n):
        p.wait("vector", sem_a, cnm[j])
        cxm[j] = p.op("vector",
                      lambda v, j=j: v.tensor_scalar(xm[j][:, 0:width],
                                                     tiles_in[j][:, 0:width],
                                                     st[:, cb + j:cb + j + 1],
                                                     None, op0=ALU.add),
                      inc=sem_d, small=True)
    for j in range(n):
        p.wait("scalar", sem_d, cxm[j])
        csq[j] = p.op("scalar",
                      lambda s, j=j: s.activation(sqd[j][:, 0:width], xm[j][:, 0:width],
                                                  AF.Square,
                                                  accum_out=st[:, cb + 8 + j:cb + 9 + j]),
                      inc=sem_a, small=True)
    p.wait("vector", sem_a, csq[n - 1])
    cv = p.op("vector",
              lambda v: v.tensor_scalar(st[:, cb + 16:cb + 16 + n],
                                        st[:, cb + 8:cb + 8 + n],
                                        1.0 / width, EPS, op0=ALU.mult, op1=ALU.add),
              inc=sem_d, small=True)
    p.wait("scalar", sem_d, cv)
    cl = p.op("scalar", lambda s: s.activation(st[:, cb + 20:cb + 20 + n],
                                               st[:, cb + 16:cb + 16 + n], AF.Ln),
              inc=sem_a, small=True)
    p.wait("vector", sem_a, cl)
    cln = p.op("vector",
               lambda v: v.tensor_scalar(st[:, cb + 16:cb + 16 + n],
                                         st[:, cb + 20:cb + 20 + n],
                                         -0.5, None, op0=ALU.mult),
               inc=sem_d, small=True)
    p.wait("scalar", sem_d, cln)
    ca = p.op("scalar", lambda s: s.activation(st[:, cb + 24:cb + 24 + n],
                                               st[:, cb + 16:cb + 16 + n], AF.Exp),
              inc=sem_a, small=True)
    cout = {}
    for j in range(n):
        p.wait("vector", sem_a, ca)
        cout[j] = p.op("vector",
                       lambda v, j=j: v.tensor_scalar(out_tiles[j][:, 0:width],
                                                      xm[j][:, 0:width],
                                                      st[:, cb + 24 + j:cb + 25 + j],
                                                      None, op0=ALU.mult),
                       inc=sem_d, small=True)
    return cout



def phase_norm1(p, io, cst, scope):
    nc = p.nc
    e = scope.enter_context
    st = e(nc.sbuf_tensor("n1_stat", [128, 32], F32))
    xt = [e(nc.sbuf_tensor(f"n1_xt{j}", [128, D], F32)) for j in range(4)]
    yt = [e(nc.sbuf_tensor(f"n1_yt{j}", [128, D], F32)) for j in range(4)]
    xm = [e(nc.sbuf_tensor(f"n1_xm{j}", [128, D], F32)) for j in range(4)]
    xnb = [e(nc.sbuf_tensor(f"n1_xnb{j}", [128, D], F32)) for j in range(4)]
    sqd = [e(nc.sbuf_tensor(f"n1_sqd{j}", [128, D], BF16)) for j in range(4)]
    gb = e(nc.sbuf_tensor("n1_g", [128, D], F32))
    bb = e(nc.sbuf_tensor("n1_b", [128, D], F32))
    ldw = p.sem("ldw")
    p.dma("sync", ldw, lambda s: s.dma_start(
        out=gb[:], in_=io["norm1_w"].partition_broadcast(128)))
    wv = p.dma("sync", ldw, lambda s: s.dma_start(
        out=bb[:], in_=io["norm1_b"].partition_broadcast(128)))
    p.wait("vector", ldw, wv)
    sem_ld = p.sem("n1_ld")
    sem_d = p.sem("n1_d")
    sem_a = p.sem("n1_a")
    sem_st = p.sem("st")
    for g in range(NG):
        for j in range(4):
            tt = g * 4 + j
            p.dma("sync", sem_ld, lambda s, tt=tt, j=j: s.dma_start(
                out=xt[j][:], in_=io["xn_f32"][tt * 128:(tt + 1) * 128, :]))
            c = p.dma("sync", sem_ld, lambda s, tt=tt, j=j: s.dma_start(
                out=yt[j][:], in_=io["ymha"][tt * 128:(tt + 1) * 128, :]))
        p.wait("vector", sem_ld, c)
        p.wait("scalar", sem_ld, c)
        cs = {}
        for j in range(4):
            cs[j] = p.op("vector", lambda v, j=j: v.tensor_add(xt[j][:], xt[j][:],
                                                               yt[j][:]),
                         inc=sem_d, small=True)
            p.wait("scalar", sem_d, cs[j])
        cout = ln_tiles(p, "n1", st, 0, sqd, xm, xt, xnb, D, sem_d, sem_a)
        for j in range(4):
            tt = g * 4 + j
            p.wait("vector", sem_d, cout[j])
            cm_ = p.op("vector", lambda v, j=j: v.tensor_mul(xnb[j][:], xnb[j][:], gb[:]),
                       inc=sem_d)
            p.wait("vector", sem_d, cm_)
            cw = p.op("vector", lambda v, j=j: v.tensor_add(xnb[j][:], xnb[j][:], bb[:]),
                      inc=sem_d, small=True)
            p.wait("sync", sem_d, cw)
            p.dma("sync", sem_st, lambda s, tt=tt, j=j: s.dma_start(
                out=io["x1_f32"][tt * 128:(tt + 1) * 128, :], in_=xnb[j][:]))
            p.wait("gpsimd", sem_d, cw)
            p.dma("gpsimd", p.sem("stg"), lambda gp, tt=tt, j=j: gp.dma_start(
                out=io["x1_bf"][tt * 128:(tt + 1) * 128, :], in_=xnb[j][:]))
        p.barrier()


phase_norm1.reads = ("xn_f32", "ymha", "norm1_w", "norm1_b")
phase_norm1.writes = ("x1_f32", "x1_bf")



def emit_mha(p, io, cst, scope, tag, causal, sc1col):
    nc = p.nc
    e = scope.enter_context
    Pb2 = [e(nc.sbuf_tensor(f"{tag}_P{i}", [128, 32 * 512], BF16)) for i in range(2)]
    khb = [e(nc.sbuf_tensor(f"{tag}_kh{i}", [128, S], BF16)) for i in range(2)]
    vhb = [e(nc.sbuf_tensor(f"{tag}_vh{i}", [128, 32 * 65], BF16)) for i in range(2)]
    qh = [e(nc.sbuf_tensor(f"{tag}_qh{i}", [128, 512], BF16)) for i in range(2)]
    onesf = e(nc.sbuf_tensor(tag + "_onesf", [1, 128], F32))
    onescb = e(nc.sbuf_tensor(tag + "_onescb", [128, 1], BF16))
    trib = e(nc.sbuf_tensor(tag + "_trib", [128, 128], BF16))
    n20 = e(nc.sbuf_tensor(tag + "_n20", [128, 1], F32))
    rowr = e(nc.sbuf_tensor(tag + "_rowr", [1, 512], F32))
    rbc = e(nc.sbuf_tensor(tag + "_rbc", [64, 512], F32))
    ost = [e(nc.sbuf_tensor(f"{tag}_o{i}", [64, 512], BF16)) for i in range(2)]
    pms = [e(nc.psum_tensor(f"{tag}_p{i}", [128, 512], F32)) for i in range(5)]

    ldc = p.sem(tag + "_ldc")
    sem_pe = p.sem(tag + "_pe")
    sem_d = p.sem(tag + "_d")
    sem_a = p.sem(tag + "_a")
    sem_pl = p.sem(tag + "_pl")
    khla = [p.sem(tag + "_khl"), p.sem(tag + "_khm")]
    vhla = [p.sem(tag + "_vhl"), p.sem(tag + "_vhm")]
    qla = [p.sem(tag + "_qla"), p.sem(tag + "_qlb")]
    oslots = StSlots(p, tag + "o", 2)

    for buf, srcn in ((onesf, "ones_f"), (onescb, "onescol_b"), (trib, "triT_b")):
        cld = p.dma("sync", ldc, lambda s, buf=buf, srcn=srcn: s.dma_start(
            out=buf[:], in_=io[srcn]))
    cld = p.dma("sync", ldc, lambda s: s.dma_start(
        out=n20[:], in_=io["neg20"].partition_broadcast(128)))

    c_init = None
    for i in range(2):
        p.op("gpsimd", lambda gp, i=i: gp.memset(khb[i][64:128, :], 0.0),
             inc=sem_pl, small=True)
        p.op("gpsimd", lambda gp, i=i: gp.memset(qh[i][64:128, :], 0.0),
             inc=sem_pl, small=True)
        c_init = p.op("gpsimd", lambda gp, i=i: gp.memset(
            vhb[i][:].rearrange("pp (kc c) -> pp kc c", c=65)[:, :, 64:65], 1.0),
            inc=sem_pl, small=True)

    vt_r = io["vt"].rearrange("(kc pp) c -> pp kc c", pp=128)
    iters = [(h, g) for h in range(16) for g in range(NG)]
    s_done = {}
    pv_done = {}
    exp_done = {}
    exp_of_bank = {}
    dve_pc = {}
    kv_loaded = {}
    state = {}
    waited_c = False

    def load_kv(h):
        hp = h % 2
        if h >= 2:
            last_idx = (h - 2) * NG + NG - 1
            p.wait("sync", sem_pe, s_done[last_idx])
            p.wait("sync", sem_pe, pv_done[last_idx])
        ck = p.dma("sync", khla[hp], lambda s, h=h, hp=hp: s.dma_start(
            out=khb[hp][0:64, :], in_=io["kf"][h * 64:(h + 1) * 64, :]))
        cv_ = p.dma("sync", vhla[hp], lambda s, h=h, hp=hp: s.dma_start(
            out=vhb[hp][:].rearrange("pp (kc c) -> pp kc c", c=65)[:, :, 0:64],
            in_=vt_r[:, :, h * 64:(h + 1) * 64]))
        kv_loaded[h] = (ck, cv_)

    def emit_iter(idx):
        nonlocal waited_c
        h, g = iters[idx]
        pi = idx % 2
        kmax = 4 * g + 4 if causal else 32
        kh = khb[h % 2]
        qb = qh[pi]
        if idx == 0:
            p.wait("tensor", sem_pl, c_init)
            load_kv(0)
            load_kv(1)
        if g == 1 and 1 <= h < 15:
            load_kv(h + 1)
        if idx >= 2:
            p.wait("sync", sem_pe, s_done[idx - 2])
        cq = p.dma("sync", qla[pi], lambda s, h=h, g=g, qb=qb: s.dma_start(
            out=qb[0:64, :], in_=io["qf"][h * 64:(h + 1) * 64,
                                          g * 512:(g + 1) * 512]))
        p.wait("tensor", qla[pi], cq)
        p.wait("tensor", khla[h % 2], kv_loaded[h][0])
        if not waited_c:
            p.wait("scalar", ldc, cld)
            p.wait("gpsimd", ldc, cld)
            waited_c = True
        if idx >= 2:
            p.wait("scalar", sem_pe, pv_done[idx - 2])
        Pb = Pb2[pi]
        cexp = {}
        cmsk = {}
        prev = state.pop(idx - 1, None)
        if prev is not None:
            pkmax, pcexp, pcmsk, ph, pg, ppi = prev
            pPb = Pb2[ppi]
            pvh = vhb[ph % 2]
            pvb = pms[2 + (idx - 1) % 2]
            p.wait("tensor", vhla[ph % 2], kv_loaded[ph][1])
            if idx > 2:
                p.wait("tensor", sem_d, dve_pc[idx - 3])
        else:
            pkmax = 0
        for kc in range(max(kmax, pkmax)):
            if kc < kmax:
                b = pms[kc % 2]
                if (kc % 2) in exp_of_bank:
                    p.wait("tensor", sem_a, exp_of_bank[kc % 2])
                cpe = p.op("tensor", lambda t, kc=kc, b=b, qb=qb, kh=kh: t.matmul(
                    b[:], kh[:, kc * 128:(kc + 1) * 128], qb[:],
                    start=True, stop=True), inc=sem_pe)
                p.wait("scalar", sem_pe, cpe)
                cexp[kc] = p.op("scalar", lambda s, kc=kc, b=b, Pb=Pb: s.activation(
                    Pb[:, kc * 512:(kc + 1) * 512], b[:], AF.Exp,
                    bias=n20[:, 0:1], scale=0.125), inc=sem_a)
                exp_of_bank[kc % 2] = cexp[kc]
                if causal and kc >= 4 * g:
                    tt = kc - 4 * g
                    p.wait("gpsimd", sem_a, cexp[kc])
                    cm = p.op("gpsimd", lambda gp, kc=kc, tt=tt, Pb=Pb: gp.tensor_mul(
                        Pb[:, kc * 512 + tt * 128: kc * 512 + (tt + 1) * 128],
                        Pb[:, kc * 512 + tt * 128: kc * 512 + (tt + 1) * 128],
                        trib[:]), inc=sem_pl, small=True)
                    for tt2 in range(tt):
                        cm = p.op("gpsimd", lambda gp, kc=kc, tt2=tt2, Pb=Pb: gp.memset(
                            Pb[:, kc * 512 + tt2 * 128: kc * 512 + (tt2 + 1) * 128],
                            0.0), inc=sem_pl, small=True)
                    cmsk[kc] = cm
            if prev is not None and kc < pkmax:
                p.wait("tensor", sem_a, pcexp[kc])
                if kc in pcmsk:
                    p.wait("tensor", sem_pl, pcmsk[kc])
                p.op("tensor", lambda t, kc=kc, last=(kc == pkmax - 1), vh=pvh,
                     Pb=pPb, pvb=pvb: t.matmul(
                    pvb[0:65, :], vh[:, kc * 65:kc * 65 + 65],
                    Pb[:, kc * 512:(kc + 1) * 512],
                    start=(kc == 0), stop=last),
                    inc=(sem_pe if kc == pkmax - 1 else None))
        s_done[idx] = p.counts[sem_pe]
        state[idx] = (kmax, cexp, cmsk, h, g, pi)
        if prev is not None:
            emit_tail(idx - 1, prev)

    def emit_tail(idx, st):
        kmax, cexp, cmsk, h, g, pi = st
        pvb = pms[2 + idx % 2]
        pv_done[idx] = p.counts[sem_pe]
        p.wait("vector", sem_pe, pv_done[idx])
        cd = p.op("vector", lambda v, pvb=pvb: v.reciprocal(
            rowr[0:1, :], pvb[64:65, :]), inc=sem_d, small=True)
        p.wait("tensor", sem_d, cd)
        cpe = p.op("tensor", lambda t: t.matmul(pms[4][0:64, :],
                                                onesf[0:1, 0:64], rowr[0:1, :],
                                                start=True, stop=True),
                   inc=sem_pe)
        p.wait("scalar", sem_pe, cpe)
        crbc = p.op("scalar", lambda s: s.activation(rbc[:], pms[4][0:64, :],
                                                     AF.Copy), inc=sem_a)
        sj = idx % 2
        oslots.guard("vector", sj)
        p.wait("vector", sem_a, crbc)
        cev = p.op("vector", lambda v, sj=sj, pvb=pvb: v.tensor_tensor(
            ost[sj][:], pvb[0:64, :], rbc[:], op=ALU.mult), inc=sem_d)
        dve_pc[idx] = cev
        p.wait("sync", sem_d, cev)
        oslots.store(sj, lambda s, h=h, g=g, sj=sj: s.dma_start(
            out=io["of"][h * 64:(h + 1) * 64, g * 512:(g + 1) * 512],
            in_=ost[sj][:]))

    for idx in range(len(iters)):
        emit_iter(idx)
    last = len(iters) - 1
    st = state.pop(last)
    pkmax, pcexp, pcmsk, ph, pg, ppi = st
    p.wait("tensor", vhla[ph % 2], kv_loaded[ph][1])
    p.wait("tensor", sem_d, dve_pc[last - 1])
    pvb_l = pms[2 + last % 2]
    for kc in range(pkmax):
        p.wait("tensor", sem_a, pcexp[kc])
        if kc in pcmsk:
            p.wait("tensor", sem_pl, pcmsk[kc])
        p.op("tensor", lambda t, kc=kc, last_=(kc == pkmax - 1): t.matmul(
            pvb_l[0:65, :], vhb[ph % 2][:, kc * 65:kc * 65 + 65],
            Pb2[ppi][:, kc * 512:(kc + 1) * 512],
            start=(kc == 0), stop=last_),
            inc=(sem_pe if kc == pkmax - 1 else None))
    emit_tail(last, st)
    p.barrier()


def phase_mattn(p, io, cst, scope):
    emit_mha(p, io, cst, scope, "ma", causal=True, sc1col=2)


phase_mattn.reads = ("qf", "kf", "vt", "ones_f", "onescol_b", "triT_b", "neg20")
phase_mattn.writes = ("of",)


def phase_l1attn(p, io, cst, scope):
    emit_mha(p, io, cst, scope, "la", causal=False, sc1col=3)


phase_l1attn.reads = ("qf", "kf", "vt", "ones_f", "onescol_b", "triT_b", "neg20")
phase_l1attn.writes = ("of",)



def phase_moe(p, io, cst, scope):
    nc = p.nc
    e = scope.enter_context
    xfm = [[e(nc.sbuf_tensor(f"moe_x{pr}{dc}", [128, 512], BF16)) for dc in range(8)]
           for pr in range(2)]
    x1t = [e(nc.sbuf_tensor(f"moe_x1t{j}", [128, D], F32)) for j in range(4)]
    yb = [e(nc.sbuf_tensor(f"moe_y{j}", [128, D], F32)) for j in range(4)]
    w1s = [[e(nc.sbuf_tensor(f"moe_w1{sl}{dc}", [128, FF], BF16)) for dc in range(8)]
           for sl in range(2)]
    w2s = [e(nc.sbuf_tensor(f"moe_w2{hc}", [128, D], BF16)) for hc in range(16)]
    b1c = [e(nc.sbuf_tensor(f"moe_b1c{sl}", [128, 16], F32)) for sl in range(2)]
    hsb = [e(nc.sbuf_tensor(f"moe_h{hc}", [128, 512], BF16)) for hc in range(16)]
    cent = e(nc.sbuf_tensor("moe_cent", [8, D], F32))
    centT = e(nc.sbuf_tensor("moe_centT", [128, 64], BF16))
    bias9 = e(nc.sbuf_tensor("moe_b9", [9, D], F32))
    rbb = e(nc.sbuf_tensor("moe_rbb", [128, 8], F32))
    gst = e(nc.sbuf_tensor("moe_gst", [128, 76], F32))
    gating = e(nc.sbuf_tensor("moe_gat", [128, 36], F32))
    gT = e(nc.sbuf_tensor("moe_gT", [9, 512], F32))
    identf = e(nc.sbuf_tensor("moe_idf", [128, 128], F32))
    pms = [e(nc.psum_tensor(f"moe_p{i}", [128, 512], F32)) for i in range(7)]

    ldc = p.sem("moe_ldc")
    ldg = p.sem("moe_ldg")
    sem_pe = p.sem("moe_pe")
    sem_d = p.sem("moe_d")
    sem_a = p.sem("moe_a")
    sem_y = p.sem("moe_y")
    slots = StSlots(p, "moe", 4)
    las = [p.sem("moe_la"), p.sem("moe_lb")]
    lx = p.sem("moe_lx")

    g3 = lambda ap, w=8: ap.rearrange("p (a b) -> p a b", b=w)
    gcol3 = lambda ap: ap.rearrange("p (a o) -> p a o", o=1)

    p.dma("sync", ldc, lambda s: s.dma_start(out=identf[:], in_=io["ident_f"]))
    p.dma("sync", ldc, lambda s: s.dma_start(
        out=rbb[:], in_=io["route_bias"].partition_broadcast(128)))
    p.dma("sync", ldc, lambda s: s.dma_start(out=bias9[0:1, :], in_=io["sb2"]))
    p.dma("sync", ldc, lambda s: s.dma_start(out=bias9[1:9, :], in_=io["rb2"]))
    cld = p.dma("sync", ldc, lambda s: s.dma_start(out=cent[:], in_=io["centroid"]))
    p.wait("tensor", ldc, cld)
    for dc in range(8):
        p.op("tensor", lambda t, dc=dc: t.transpose(
            pms[6][0:128, dc * 8:(dc + 1) * 8], cent[:, dc * 128:(dc + 1) * 128],
            identf[0:8, 0:8]),
            inc=(sem_pe if dc == 7 else None))
    cpe = p.counts[sem_pe]
    p.wait("scalar", sem_pe, cpe)
    ccT = p.op("scalar", lambda s: s.activation(centT[:], pms[6][:, 0:64], AF.Copy),
               inc=sem_a)

    def w1_src(ei):
        return io["sw1"] if ei == 0 else io["rw1"][ei - 1:ei, :, :].squeeze(0)

    def w2_src(ei):
        return io["sw2"] if ei == 0 else io["rw2"][ei - 1:ei, :, :].squeeze(0)

    def b1_src(ei):
        return io["sb1"] if ei == 0 else io["rb1"][ei - 1:ei, :]

    wload_w1 = {}
    wload_b1 = {}
    wload_w2 = {}
    pe_w1_free = {}
    act_b1_free = {}
    pe_w2_free = [None]
    w1sem = [p.sem("moe_w1a"), p.sem("moe_w1b")]
    b1sem = [p.sem("moe_b1a"), p.sem("moe_b1b")]
    w2sem = p.sem("moe_w2s")

    def load_w1(eseq):
        ei = eseq % 9
        sl = eseq % 2
        if pe_w1_free.get(sl) is not None:
            p.wait("gpsimd", sem_pe, pe_w1_free[sl])
        if act_b1_free.get(sl) is not None:
            p.wait("sync", sem_a, act_b1_free[sl])
        c = None
        for dc in range(8):
            c = p.dma("gpsimd", w1sem[sl], lambda g_, ei=ei, sl=sl, dc=dc: g_.dma_start(
                out=w1s[sl][dc][:], in_=w1_src(ei)[dc * 128:(dc + 1) * 128, :]))
        wload_w1[eseq] = c
        wload_b1[eseq] = load_bias_cols(p, b1sem[sl], b1c[sl], b1_src(ei), 16)

    def load_w2(eseq):
        ei = eseq % 9
        if pe_w2_free[0] is not None:
            p.wait("gpsimd", sem_pe, pe_w2_free[0])
        c = None
        for hc in range(16):
            c = p.dma("gpsimd", w2sem, lambda g_, ei=ei, hc=hc: g_.dma_start(
                out=w2s[hc][:], in_=w2_src(ei)[hc * 128:(hc + 1) * 128, :]))
        wload_w2[eseq] = c

    load_w1(0)
    load_w2(0)
    y_count = {}
    pe_group_end = {}
    gat_ready = {}
    hev = {}
    opev = {}
    for g in range(NG):
        la = las[g % 2]
        xf = xfm[g % 2]
        if g >= 2:
            p.wait("sync", sem_pe, pe_group_end[g - 2])
        for dc in range(8):
            p.dma("sync", la, lambda s, g=g, dc=dc, xf=xf: s.dma_start(
                out=xf[dc][:], in_=io["x1_bf"][g * 512:(g + 1) * 512,
                                               dc * 128:(dc + 1) * 128],
                transpose=True))
        ldv = p.counts[la]
        if g > 0:
            p.wait("sync", sem_y, y_count[(g - 1, 3, 1)])
        for j in range(4):
            tt = g * 4 + j
            p.dma("sync", lx, lambda s, tt=tt, j=j: s.dma_start(
                out=x1t[j][:], in_=io["x1_f32"][tt * 128:(tt + 1) * 128, :]))
        clx = p.counts[lx]
        p.wait("tensor", la, ldv)
        if g == 0:
            p.wait("tensor", sem_a, ccT)
        if g > 0:
            p.wait("tensor", sem_d, gat_ready[g - 1])
        for tt in range(4):
            for dc in range(8):
                p.op("tensor", lambda t, tt=tt, dc=dc, xf=xf: t.matmul(
                    pms[5][:, tt * 8:(tt + 1) * 8],
                    xf[dc][:, tt * 128:(tt + 1) * 128], centT[:, dc * 8:(dc + 1) * 8],
                    start=(dc == 0), stop=(dc == 7)),
                    inc=(sem_pe if (tt == 3 and dc == 7) else None))
        cpe = p.counts[sem_pe]
        p.wait("scalar", sem_pe, cpe)
        css = p.op("scalar", lambda s: s.activation(gst[:, 0:32], pms[5][:, 0:32],
                                                    AF.Copy), inc=sem_a, small=True)
        p.wait("vector", sem_a, css)
        c1 = p.op("vector", lambda v: v.reduce_max(gcol3(gst[:, 32:36]),
                                                   g3(gst[:, 0:32]),
                                                   axis=AX.X, negate=True),
                  inc=sem_d, small=True)
        p.wait("scalar", sem_d, c1)
        ce = None
        for tt in range(4):
            ce = p.op("scalar", lambda s, tt=tt: s.activation(
                gst[:, tt * 8:(tt + 1) * 8], gst[:, tt * 8:(tt + 1) * 8],
                AF.Exp, bias=gst[:, 32 + tt:33 + tt]), inc=sem_a,
                small=(tt == 3))
        p.wait("vector", sem_a, ce)
        c2 = p.op("vector", lambda v: v.reduce_sum(gcol3(gst[:, 36:40]),
                                                   g3(gst[:, 0:32]), axis=AX.X),
                  inc=sem_d, small=True)
        p.wait("vector", sem_d, c2)
        c3 = p.op("vector", lambda v: v.reciprocal(gst[:, 40:44], gst[:, 36:40]),
                  inc=sem_d, small=True)
        p.wait("vector", sem_d, c3)
        csim = None
        for tt in range(4):
            csim = p.op("vector", lambda v, tt=tt: v.tensor_scalar(
                gating[:, tt * 9 + 1:tt * 9 + 9], gst[:, tt * 8:(tt + 1) * 8],
                gst[:, 40 + tt:41 + tt], None, op0=ALU.mult),
                inc=sem_d, small=(tt == 3))
        p.wait("vector", sem_d, csim)
        ct = None
        for tt in range(4):
            ct = p.op("vector", lambda v, tt=tt: v.tensor_add(
                gst[:, tt * 8:(tt + 1) * 8], gating[:, tt * 9 + 1:tt * 9 + 9],
                rbb[:]), inc=sem_d, small=(tt == 3))
        p.wait("vector", sem_d, ct)
        c4 = p.op("vector", lambda v: v.reduce_max(gcol3(gst[:, 32:36]),
                                                   g3(gst[:, 0:32]), axis=AX.X),
                  inc=sem_d, small=True)
        p.wait("vector", sem_d, c4)
        cge1 = None
        for tt in range(4):
            cge1 = p.op("vector", lambda v, tt=tt: v.tensor_scalar(
                gst[:, 44 + tt * 8:52 + tt * 8], gst[:, tt * 8:(tt + 1) * 8],
                gst[:, 32 + tt:33 + tt], None, op0=ALU.is_ge),
                inc=sem_d, small=(tt == 3))
        p.wait("vector", sem_d, cge1)
        c5 = p.op("vector", lambda v: v.scalar_tensor_tensor(
            gst[:, 0:32], gst[:, 44:76], -1e30, gst[:, 0:32],
            op0=ALU.mult, op1=ALU.add), inc=sem_d, small=True)
        p.wait("vector", sem_d, c5)
        c6 = p.op("vector", lambda v: v.reduce_max(gcol3(gst[:, 36:40]),
                                                   g3(gst[:, 0:32]), axis=AX.X),
                  inc=sem_d, small=True)
        p.wait("vector", sem_d, c6)
        cge2 = None
        for tt in range(4):
            cge2 = p.op("vector", lambda v, tt=tt: v.tensor_scalar(
                gst[:, tt * 8:(tt + 1) * 8], gst[:, tt * 8:(tt + 1) * 8],
                gst[:, 36 + tt:37 + tt], None, op0=ALU.is_ge),
                inc=sem_d, small=(tt == 3))
        p.wait("vector", sem_d, cge2)
        c7 = p.op("vector", lambda v: v.tensor_add(gst[:, 44:76], gst[:, 44:76],
                                                   gst[:, 0:32]),
                  inc=sem_d, small=True)
        p.wait("vector", sem_d, c7)
        c8 = p.op("vector", lambda v: v.tensor_mul(
            g3(gst[:, 0:32]),
            gating[:, 0:36].rearrange("p (a b) -> p a b", b=9)[:, :, 1:9],
            g3(gst[:, 44:76])), inc=sem_d, small=True)
        p.wait("vector", sem_d, c8)
        c9 = p.op("vector", lambda v: v.reduce_sum(gcol3(gst[:, 36:40]),
                                                   g3(gst[:, 0:32]), axis=AX.X),
                  inc=sem_d, small=True)
        p.wait("vector", sem_d, c9)
        c10 = p.op("vector", lambda v: v.reciprocal(gst[:, 40:44], gst[:, 36:40]),
                   inc=sem_d, small=True)
        p.wait("vector", sem_d, c10)
        cgat = None
        for tt in range(4):
            p.op("vector", lambda v, tt=tt: v.memset(gating[:, tt * 9:tt * 9 + 1], 1.0))
            cgat = p.op("vector", lambda v, tt=tt: v.tensor_scalar(
                gating[:, tt * 9 + 1:tt * 9 + 9], gst[:, tt * 8:(tt + 1) * 8],
                gst[:, 40 + tt:41 + tt], None, op0=ALU.mult),
                inc=sem_d, small=(tt == 3))
        gat_ready[g] = cgat
        p.wait("vector", sem_d, cgat)
        def expert_h(ei, g=g, xf=xf):
            eseq = g * 9 + ei
            sl = eseq % 2
            if eseq + 1 < NG * 9:
                load_w1(eseq + 1)
            p.wait("tensor", w1sem[sl], wload_w1[eseq])
            p.wait("scalar", b1sem[sl], wload_b1[eseq])
            for hc in range(16):
                b = pms[hc % 2]
                guard = hev.get(hc % 2)
                if guard is not None:
                    p.wait("tensor", sem_a, guard)
                cpe = mm_acc(p, b[:],
                             [(w1s[sl][dc][:, hc * 128:(hc + 1) * 128], xf[dc][:])
                              for dc in range(8)], sem_pe)
                p.wait("scalar", sem_pe, cpe)
                hev[hc % 2] = p.op("scalar", lambda s, hc=hc, b=b, sl=sl: s.activation(
                    hsb[hc][:], b[:], AF.Gelu, bias=b1c[sl][:, hc:hc + 1]),
                    inc=sem_a)
            return p.counts[sem_a], p.counts[sem_pe]

        def expert_out(ei, hdone, pe_h_end, g=g):
            eseq = g * 9 + ei
            sl = eseq % 2
            pe_w1_free[sl] = pe_h_end
            act_b1_free[sl] = hdone
            p.wait("tensor", w2sem, wload_w2[eseq])
            p.wait("tensor", sem_a, hdone)
            for j in range(4):
                for half in range(2):
                    b = pms[2 + (j * 2 + half) % 2]
                    guard = opev.get((j * 2 + half) % 2)
                    if guard is not None:
                        p.wait("tensor", sem_y, guard)
                    cpe = mm_acc(p, b[:],
                                 [(hsb[hc][:, j * 128:(j + 1) * 128],
                                   w2s[hc][:, half * 512:(half + 1) * 512])
                                  for hc in range(16)], sem_pe)
                    p.wait("vector", sem_pe, cpe)
                    p.wait("vector", sem_y, y_count[(g, j, half)])
                    gci = ei
                    cy = p.op("vector", lambda v, j=j, half=half, b=b,
                              gci=gci: v.scalar_tensor_tensor(
                        yb[j][:, half * 512:(half + 1) * 512], b[:],
                        gating[:, j * 9 + gci:j * 9 + gci + 1],
                        yb[j][:, half * 512:(half + 1) * 512],
                        op0=ALU.mult, op1=ALU.add),
                        inc=sem_y, small=(ei == 8))
                    y_count[(g, j, half)] = cy
                    opev[(j * 2 + half) % 2] = cy
            pe_w2_free[0] = p.counts[sem_pe]
            if eseq + 1 < NG * 9:
                load_w2(eseq + 1)

        h0, pe0 = expert_h(0)
        p.wait("tensor", sem_d, cgat)
        for tt in range(4):
            p.op("tensor", lambda t, tt=tt: t.transpose(
                pms[6][0:9, tt * 128:(tt + 1) * 128],
                gating[:, tt * 9:(tt + 1) * 9], identf[:]),
                inc=(sem_pe if tt == 3 else None))
        cpe = p.counts[sem_pe]
        p.wait("scalar", sem_pe, cpe)
        cgT = p.op("scalar", lambda s: s.activation(gT[:], pms[6][0:9, :], AF.Copy),
                   inc=sem_a)
        p.wait("tensor", sem_a, cgT)
        p.wait("tensor", ldc, p.counts[ldc])
        p.wait("vector", lx, clx)
        py_free = None
        for j in range(4):
            slots.guard("vector", j)
            for half in range(2):
                if py_free is not None:
                    p.wait("tensor", sem_y, py_free)
                cpe = p.op("tensor", lambda t, j=j, half=half: t.matmul(
                    pms[4][:], gT[:, j * 128:(j + 1) * 128],
                    bias9[:, half * 512:(half + 1) * 512],
                    start=True, stop=True), inc=sem_pe)
                p.wait("vector", sem_pe, cpe)
                cy = p.op("vector", lambda v, j=j, half=half: v.tensor_add(
                    yb[j][:, half * 512:(half + 1) * 512],
                    x1t[j][:, half * 512:(half + 1) * 512], pms[4][:]),
                    inc=sem_y)
                y_count[(g, j, half)] = cy
                py_free = cy
        expert_out(0, h0, pe0)
        for ei in range(1, 9):
            hd, ped = expert_h(ei)
            expert_out(ei, hd, ped)
        for j in range(4):
            tt = g * 4 + j
            p.wait("sync", sem_y, y_count[(g, j, 1)])
            slots.store(j, lambda s, tt=tt, j=j: s.dma_start(
                out=io["ymoe"][tt * 128:(tt + 1) * 128, :], in_=yb[j][:]))
        pe_group_end[g] = p.counts[sem_pe]
    p.barrier()


phase_moe.reads = ("x1_bf", "x1_f32", "sw1", "sb1", "sw2", "sb2", "rw1", "rb1",
                   "rw2", "rb2", "centroid", "route_bias", "ident_f")
phase_moe.writes = ("ymoe",)



def phase_l1prep(p, io, cst, scope):
    nc = p.nc
    e = scope.enter_context
    st = e(nc.sbuf_tensor("lp_stat", [128, 32], F32))
    yt = [e(nc.sbuf_tensor(f"lp_yt{j}", [128, D], F32)) for j in range(4)]
    x0t = [e(nc.sbuf_tensor(f"lp_x0{j}", [128, D], F32)) for j in range(4)]
    abt = [e(nc.sbuf_tensor(f"lp_ab{j}", [128, D], F32)) for j in range(4)]
    xm = [e(nc.sbuf_tensor(f"lp_xm{j}", [128, D], F32)) for j in range(4)]
    t1 = [e(nc.sbuf_tensor(f"lp_t1{j}", [128, D], F32)) for j in range(4)]
    t2 = [e(nc.sbuf_tensor(f"lp_t2{j}", [128, D], F32)) for j in range(4)]
    sqd = [e(nc.sbuf_tensor(f"lp_sqd{j}", [128, D], BF16)) for j in range(4)]
    n2g = e(nc.sbuf_tensor("lp_n2g", [128, D], F32))
    n2b = e(nc.sbuf_tensor("lp_n2b", [128, D], F32))
    l1g = e(nc.sbuf_tensor("lp_l1g", [128, D], F32))
    l1b = e(nc.sbuf_tensor("lp_l1b", [128, D], F32))
    ldw = p.sem("ldw")
    for buf, nm in ((n2g, "norm2_w"), (n2b, "norm2_b"), (l1g, "ln1_g"), (l1b, "ln1_b")):
        wv = p.dma("sync", ldw, lambda s, buf=buf, nm=nm: s.dma_start(
            out=buf[:], in_=io[nm].partition_broadcast(128)))
    p.wait("vector", ldw, wv)
    sem_ld = p.sem("lp_ld")
    sem_d = p.sem("lp_d")
    sem_a = p.sem("lp_a")
    sem_st = p.sem("st")
    stg = p.sem("stg")
    for g in range(NG):
        for j in range(4):
            tt = g * 4 + j
            p.dma("sync", sem_ld, lambda s, tt=tt, j=j: s.dma_start(
                out=yt[j][:], in_=io["ymoe"][tt * 128:(tt + 1) * 128, :]))
            p.dma("sync", sem_ld, lambda s, tt=tt, j=j: s.dma_start(
                out=x0t[j][:], in_=io["x"][tt * 128:(tt + 1) * 128, :]))
            c = p.dma("sync", sem_ld, lambda s, tt=tt, j=j: s.dma_start(
                out=abt[j][:], in_=io["attn_br"][tt * 128:(tt + 1) * 128, :]))
        p.wait("vector", sem_ld, c)
        p.wait("scalar", sem_ld, c)
        cout = ln_tiles(p, "lp", st, 0, sqd, xm, yt, t1, D, sem_d, sem_a)
        for j in range(4):
            tt = g * 4 + j
            p.wait("vector", sem_d, cout[j])
            cm_ = p.op("vector", lambda v, j=j: v.tensor_mul(t1[j][:], t1[j][:], n2g[:]),
                       inc=sem_d)
            p.wait("vector", sem_d, cm_)
            cm_ = p.op("vector", lambda v, j=j: v.tensor_add(t1[j][:], t1[j][:], n2b[:]),
                      inc=sem_d)
            p.wait("vector", sem_d, cm_)
            cm_ = p.op("vector", lambda v, j=j: v.tensor_add(t1[j][:], t1[j][:], x0t[j][:]),
                       inc=sem_d)
            p.wait("vector", sem_d, cm_)
            cr = p.op("vector", lambda v, j=j: v.tensor_add(t1[j][:], t1[j][:], abt[j][:]),
                      inc=sem_d, small=True)
            p.wait("sync", sem_d, cr)
            p.dma("sync", sem_st, lambda s, tt=tt, j=j: s.dma_start(
                out=io["xres"][tt * 128:(tt + 1) * 128, :], in_=t1[j][:]))
            p.wait("scalar", sem_d, cr)
        cout = ln_tiles(p, "lp", st, 0, sqd, xm, t1, t2, D, sem_d, sem_a)
        for j in range(4):
            tt = g * 4 + j
            p.wait("gpsimd", sem_d, cout[j])
            p.dma("gpsimd", stg, lambda gp, tt=tt, j=j: gp.dma_start(
                out=io["xn2_bf"][tt * 128:(tt + 1) * 128, :], in_=t2[j][:]))
            p.wait("scalar", sem_d, cout[j])
        p.wait("vector", "st", p.counts["st"])
        cout = ln_tiles(p, "lp", st, 0, sqd, xm, t2, t1, D, sem_d, sem_a)
        for j in range(4):
            tt = g * 4 + j
            p.wait("vector", sem_d, cout[j])
            cm_ = p.op("vector", lambda v, j=j: v.tensor_mul(t1[j][:], t1[j][:], l1g[:]),
                       inc=sem_d)
            p.wait("vector", sem_d, cm_)
            ch = p.op("vector", lambda v, j=j: v.tensor_add(t1[j][:], t1[j][:], l1b[:]),
                      inc=sem_d, small=True)
            p.wait("gpsimd", sem_d, ch)
            p.dma("gpsimd", stg, lambda gp, tt=tt, j=j: gp.dma_start(
                out=io["h1_bf"][tt * 128:(tt + 1) * 128, :], in_=t1[j][:]))
        p.barrier()


phase_l1prep.reads = ("ymoe", "x", "attn_br", "norm2_w", "norm2_b", "ln1_g", "ln1_b")
phase_l1prep.writes = ("xres", "xn2_bf", "h1_bf")



def phase_l1wcast(p, io, cst, scope):
    stg = p.sem("stg")
    for wname, dname in (("Wq", "wqT_src"), ("Wk", "wkT_src"), ("Wv", "wvT_src"),
                         ("Wo", "woT_src"), ):
        p.dma("gpsimd", stg, lambda gp, wname=wname, dname=dname: gp.dma_start(
            out=io[dname], in_=io[wname]))
    p.barrier()


phase_l1wcast.reads = ("Wq", "Wk", "Wv", "Wo")
phase_l1wcast.writes = ("wqT_src", "wkT_src", "wvT_src", "woT_src")



def phase_l1qkv(p, io, cst, scope):
    nc = p.nc
    e = scope.enter_context
    tag = "l1q"
    wsb = [e(nc.sbuf_tensor(f"{tag}_w{dc}", [128, 3 * D], BF16)) for dc in range(8)]
    xfm = [[e(nc.sbuf_tensor(f"{tag}_x{pr}{dc}", [128, 512], BF16)) for dc in range(8)]
           for pr in range(2)]
    stage = [e(nc.sbuf_tensor(f"{tag}_s{i}", [128, 512], BF16)) for i in range(6)]
    pms = [e(nc.psum_tensor(f"{tag}_p{i}", [128, 512], F32)) for i in range(4)]
    banks = Banks(p, pms)
    ldw = p.sem("ldw")
    wldv = None
    for dc in range(8):
        for mi, wn in enumerate(("wqT_src", "wkT_src", "wvT_src")):
            wldv = p.dma("sync", ldw, lambda s, dc=dc, mi=mi, wn=wn: s.dma_start(
                out=wsb[dc][:, mi * D:(mi + 1) * D],
                in_=io[wn][:, dc * 128:(dc + 1) * 128], transpose=True))
    sem_pe = p.sem(tag + "_pe")
    sem_ev = p.sem(tag + "_ev")
    las = [p.sem(tag + "_la"), p.sem(tag + "_lb")]
    slots = StSlots(p, tag, 6)
    pe_group_end = {}
    waited_w = False
    for g in range(NG):
        la = las[g % 2]
        xf = xfm[g % 2]
        if g >= 2:
            p.wait("sync", sem_pe, pe_group_end[g - 2])
        for dc in range(8):
            p.dma("sync", la, lambda s, g=g, dc=dc, xf=xf: s.dma_start(
                out=xf[dc][:], in_=io["xn2_bf"][g * 512:(g + 1) * 512,
                                                dc * 128:(dc + 1) * 128],
                transpose=True))
        ldv = p.counts[la]
        p.wait("tensor", la, ldv)
        if not waited_w:
            p.wait("tensor", ldw, wldv)
            waited_w = True
        si = 0
        for mc in range(16):
            b = banks.next()
            wcol = (0 if mc < 8 else 1) * D + (mc % 8) * 128
            cpe = mm_acc(p, b[0][:],
                         [(wsb[dc][:, wcol:wcol + 128], xf[dc][:])
                          for dc in range(8)], sem_pe)
            sj = si % 6
            si += 1
            slots.guard("scalar", sj)
            p.wait("scalar", sem_pe, cpe)
            cev = p.op("scalar", lambda s, sj=sj, b=b: s.activation(
                stage[sj][:], b[0][:], AF.Copy), inc=sem_ev)
            Banks.release(b, sem_ev, cev)
            dst = io["qf"] if mc < 8 else io["kf"]
            r = (mc % 8) * 128
            p.wait("sync", sem_ev, cev)
            slots.store(sj, lambda s, dst=dst, r=r, g=g, sj=sj: s.dma_start(
                out=dst[r:r + 128, g * 512:(g + 1) * 512], in_=stage[sj][:]))
        for tt in range(4):
            for half in range(2):
                b = banks.next()
                cpe = mm_acc(p, b[0][:],
                             [(xf[dc][:, tt * 128:(tt + 1) * 128],
                               wsb[dc][:, 2 * D + half * 512: 2 * D + (half + 1) * 512])
                              for dc in range(8)], sem_pe)
                sj = si % 6
                si += 1
                slots.guard("scalar", sj)
                p.wait("scalar", sem_pe, cpe)
                cev = p.op("scalar", lambda s, sj=sj, b=b: s.activation(
                    stage[sj][:], b[0][:], AF.Copy), inc=sem_ev)
                Banks.release(b, sem_ev, cev)
                row = (g * 4 + tt) * 128
                p.wait("sync", sem_ev, cev)
                slots.store(sj, lambda s, row=row, half=half, sj=sj: s.dma_start(
                    out=io["vt"][row:row + 128, half * 512:(half + 1) * 512],
                    in_=stage[sj][:]))
        pe_group_end[g] = p.counts[sem_pe]
    p.barrier()


phase_l1qkv.reads = ("xn2_bf", "wqT_src", "wkT_src", "wvT_src")
phase_l1qkv.writes = ("qf", "kf", "vt")



def phase_l1proj(p, io, cst, scope):
    nc = p.nc
    e = scope.enter_context
    tag = "l1p"
    wsb = [e(nc.sbuf_tensor(f"{tag}_w{dc}", [128, D], BF16)) for dc in range(8)]
    ofm = [[e(nc.sbuf_tensor(f"{tag}_o{pr}{dc}", [128, 512], BF16)) for dc in range(8)]
           for pr in range(2)]
    stage = [e(nc.sbuf_tensor(f"{tag}_s{i}", [128, 512], F32)) for i in range(4)]
    pms = [e(nc.psum_tensor(f"{tag}_p{i}", [128, 512], F32)) for i in range(4)]
    banks = Banks(p, pms)
    ldw = p.sem("ldw")
    wldv = None
    for dc in range(8):
        wldv = p.dma("sync", ldw, lambda s, dc=dc: s.dma_start(
            out=wsb[dc][:], in_=io["woT_src"][:, dc * 128:(dc + 1) * 128],
            transpose=True))
    sem_pe = p.sem(tag + "_pe")
    sem_ev = p.sem(tag + "_ev")
    las = [p.sem(tag + "_la"), p.sem(tag + "_lb")]
    slots = StSlots(p, tag, 4)
    pe_group_end = {}
    waited = False
    for g in range(NG):
        la = las[g % 2]
        of = ofm[g % 2]
        if g >= 2:
            p.wait("sync", sem_pe, pe_group_end[g - 2])
        for dc in range(8):
            p.dma("sync", la, lambda s, g=g, dc=dc, of=of: s.dma_start(
                out=of[dc][:], in_=io["of"][dc * 128:(dc + 1) * 128,
                                            g * 512:(g + 1) * 512]))
        ldv = p.counts[la]
        p.wait("tensor", la, ldv)
        if not waited:
            p.wait("tensor", ldw, wldv)
            waited = True
        si = 0
        for tt in range(4):
            for half in range(2):
                b = banks.next()
                cpe = mm_acc(p, b[0][:],
                             [(of[dc][:, tt * 128:(tt + 1) * 128],
                               wsb[dc][:, half * 512:(half + 1) * 512])
                              for dc in range(8)], sem_pe)
                sj = si % 4
                si += 1
                slots.guard("scalar", sj)
                p.wait("scalar", sem_pe, cpe)
                cev = p.op("scalar", lambda s, sj=sj, b=b: s.activation(
                    stage[sj][:], b[0][:], AF.Copy), inc=sem_ev)
                Banks.release(b, sem_ev, cev)
                row = (g * 4 + tt) * 128
                p.wait("sync", sem_ev, cev)
                slots.store(sj, lambda s, row=row, half=half, sj=sj: s.dma_start(
                    out=io["attn2"][row:row + 128, half * 512:(half + 1) * 512],
                    in_=stage[sj][:]))
        pe_group_end[g] = p.counts[sem_pe]
    p.barrier()


phase_l1proj.reads = ("of", "woT_src")
phase_l1proj.writes = ("attn2",)



def emit_quant_stats(p, io, scope, tag, w_name, nrow, ncol, sc1col):
    nc = p.nc
    e = scope.enter_context
    wt = [e(nc.sbuf_tensor(f"{tag}_wt{i}", [128, 512], F32)) for i in range(2)]
    stt = e(nc.sbuf_tensor(f"{tag}_st", [128, 80], F32))
    dmp = [e(nc.sbuf_tensor(f"{tag}_dmp{i}", [128, 512], BF16)) for i in range(2)]
    identf = e(nc.sbuf_tensor(f"{tag}_idf", [128, 128], F32))
    pq = e(nc.psum_tensor(f"{tag}_pq", [128, 512], F32))
    ntile = nrow * ncol // (128 * 512)
    assert ntile <= 64
    lds = [p.sem(tag + "_lda"), p.sem(tag + "_ldb")]
    ldi = p.sem(tag + "_ldi")
    sem_d = p.sem(tag + "_d")
    sem_a = p.sem(tag + "_a")
    sem_pe = p.sem(tag + "_pe")
    sem_mk = p.sem(tag + "_mk")
    cld = p.dma("sync", ldi, lambda s: s.dma_start(out=identf[:], in_=io["ident_f"]))
    w2d = io[w_name].rearrange("(a p) c -> a p c", p=128)
    npc = ncol // 512
    act_free = {}
    for i in range(ntile):
        r, cblk = i // npc, i % npc
        if act_free.get(i % 2) is not None:
            p.wait("sync", sem_a, act_free[i % 2])
        c = p.dma("sync", lds[i % 2], lambda s, r=r, cblk=cblk, i=i: s.dma_start(
            out=wt[i % 2][:], in_=w2d[r, :, cblk * 512:(cblk + 1) * 512]))
        p.wait("scalar", lds[i % 2], c)
        act_free[i % 2] = p.op("scalar", lambda s, i=i: s.activation(
            dmp[i % 2][:], wt[i % 2][:], AF.Abs, accum_out=stt[:, i:i + 1]),
            inc=sem_a, small=(i == ntile - 1))
    p.wait("vector", sem_a, act_free[(ntile - 1) % 2])
    cr = p.op("vector", lambda v: v.reduce_sum(stt[:, 64:65], stt[:, 0:ntile],
                                               axis=AX.X), inc=sem_d, small=True)
    p.wait("tensor", sem_d, cr)
    p.wait("tensor", ldi, cld)
    cpe = p.op("tensor", lambda t: t.transpose(pq[0:1, 0:128], stt[:, 64:65],
                                               identf[:]), inc=sem_pe)
    p.wait("vector", sem_pe, cpe)
    cs = p.op("vector", lambda v: v.reduce_sum(stt[0:1, 66:67], pq[0:1, 0:128],
                                               axis=AX.X), inc=sem_d, small=True)
    p.wait("scalar", sem_d, cs)
    cm = p.op("scalar", lambda s: s.activation(stt[0:1, 67:68], stt[0:1, 66:67],
                                               AF.Copy, scale=1.0 / (nrow * ncol)),
              inc=sem_a, small=True)
    p.wait("vector", sem_a, cm)
    cc = p.op("vector", lambda v: v.tensor_single_scalar(stt[0:1, 68:69],
                                                         stt[0:1, 67:68], 1e-5,
                                                         op=ALU.max),
              inc=sem_d, small=True)
    p.wait("sync", sem_d, cc)
    cs1 = p.dma("sync", sem_mk, lambda s: s.dma_start(
        out=io["sc1"][0:1, sc1col:sc1col + 1], in_=stt[0:1, 68:69]))
    return cs1, sem_mk


def emit_quantize_wT(p, io, scope, tag, w_name, nrow, ncol, sc1col, dst):
    nc = p.nc
    e = scope.enter_context
    wt = [e(nc.sbuf_tensor(f"{tag}_qw{i}", [128, 512], F32)) for i in range(2)]
    g1 = e(nc.sbuf_tensor(f"{tag}_g1", [128, 512], F32))
    g2 = e(nc.sbuf_tensor(f"{tag}_g2", [128, 512], F32))
    qb = [e(nc.sbuf_tensor(f"{tag}_qb{i}", [128, 512], BF16)) for i in range(2)]
    hcol = e(nc.sbuf_tensor(f"{tag}_hcol", [128, 2], F32))
    identb = e(nc.sbuf_tensor(f"{tag}_idb", [128, 128], BF16))
    pqb = e(nc.psum_tensor(f"{tag}_pqb", [128, 1024], BF16))
    lds = [p.sem(tag + "_qlda"), p.sem(tag + "_qldb")]
    ld = p.sem(tag + "_qld")
    sem_d = p.sem(tag + "_qd")
    sem_a = p.sem(tag + "_qa")
    sem_pe = p.sem(tag + "_qpe")
    sem_ev = p.sem(tag + "_qev")
    cld = p.dma("gpsimd", p.sem(tag + "_qldg"),
                lambda g_: g_.dma_start(out=identb[:], in_=io["ident_f"]))
    cmc = p.dma("sync", ld, lambda s: s.dma_start(
        out=hcol[:, 0:1], in_=io["sc1"][0:1, sc1col:sc1col + 1].partition_broadcast(128)))
    p.wait("vector", ld, cmc)
    ch = p.op("vector", lambda v: v.tensor_scalar(hcol[:, 0:1], hcol[:, 0:1], 0.5,
                                                  None, op0=ALU.mult),
              inc=sem_d, small=True)
    p.wait("vector", sem_d, ch)
    cn = p.op("vector", lambda v: v.tensor_scalar(hcol[:, 1:2], hcol[:, 0:1], -1.0,
                                                  None, op0=ALU.mult),
              inc=sem_d, small=True)
    p.wait("vector", sem_d, cn)
    p.wait("tensor", p.sem(tag + "_qldg"), cld)
    w2d = io[w_name].rearrange("(a p) c -> a p c", p=128)
    npc = ncol // 512
    dve_free = {}
    pe_qb_free = {}
    ev_free = None
    for i in range(nrow // 128 * npc):
        r, cblk = i // npc, i % npc
        if dve_free.get(i % 2) is not None:
            p.wait("sync", sem_d, dve_free[i % 2])
        c = p.dma("sync", lds[i % 2], lambda s, r=r, cblk=cblk, i=i: s.dma_start(
            out=wt[i % 2][:], in_=w2d[r, :, cblk * 512:(cblk + 1) * 512]))
        p.wait("vector", lds[i % 2], c)
        ca = p.op("vector", lambda v, i=i: v.tensor_scalar(
            g1[:], wt[i % 2][:], hcol[:, 0:1], None, op0=ALU.is_ge), inc=sem_d)
        p.wait("vector", sem_d, ca)
        cb = p.op("vector", lambda v, i=i: v.tensor_scalar(
            g2[:], wt[i % 2][:], hcol[:, 1:2], None, op0=ALU.is_le), inc=sem_d)
        p.wait("vector", sem_d, cb)
        if pe_qb_free.get(i % 2) is not None:
            p.wait("vector", sem_pe, pe_qb_free[i % 2])
        cq = p.op("vector", lambda v, i=i: v.tensor_sub(qb[i % 2][:], g1[:], g2[:]),
                  inc=sem_d, small=True)
        dve_free[i % 2] = p.counts[sem_d]
        p.wait("tensor", sem_d, cq)
        if ev_free is not None:
            p.wait("tensor", sem_ev, ev_free)
        for t4 in range(4):
            p.op("tensor", lambda t, i=i, t4=t4: t.transpose(
                pqb[:, t4 * 128:(t4 + 1) * 128],
                qb[i % 2][:, t4 * 128:(t4 + 1) * 128], identb[:]),
                inc=(sem_pe if t4 == 3 else None))
        cpe = p.counts[sem_pe]
        pe_qb_free[i % 2] = cpe
        p.wait("scalar", sem_pe, cpe)
        cev = None
        for t4 in range(4):
            cchunk = cblk * 4 + t4
            cev = p.op("scalar", lambda s, r=r, cchunk=cchunk, t4=t4: s.activation(
                dst[:, cchunk * nrow + r * 128: cchunk * nrow + (r + 1) * 128],
                pqb[:, t4 * 128:(t4 + 1) * 128], AF.Copy), inc=sem_ev)
        ev_free = cev
    return p.counts[sem_ev]



def phase_l1ffna(p, io, cst, scope):
    nc = p.nc
    e = scope.enter_context
    c1, s1 = emit_quant_stats(p, io, scope, "qs1", "w1", HID, D, 4)
    c2, s2 = emit_quant_stats(p, io, scope, "qs2", "w2", D, HID, 5)
    p.wait("sync", s1, c1)
    p.wait("sync", s2, c2)
    p.barrier()
    w1q = e(nc.sbuf_tensor("fa_w1q", [128, 8 * HID], BF16))
    emit_quantize_wT(p, io, scope, "fa", "w1", HID, D, 4, w1q)
    p.barrier()
    hfm = [[e(nc.sbuf_tensor(f"fa_h{pr}{dc}", [128, 512], BF16)) for dc in range(8)]
           for pr in range(2)]
    b1bc = e(nc.sbuf_tensor("fa_b1bc", [128, HID], F32))
    g2bc = e(nc.sbuf_tensor("fa_g2bc", [128, HID], F32))
    bb2bc = e(nc.sbuf_tensor("fa_bb2bc", [128, HID], F32))
    mcc = e(nc.sbuf_tensor("fa_mcc", [128, 1], F32))
    htok = e(nc.sbuf_tensor("fa_htok", [128, HID], F32))
    hG = e(nc.sbuf_tensor("fa_hG", [128, HID], F32))
    h2b = e(nc.sbuf_tensor("fa_h2b", [128, HID], BF16))
    hstage = [e(nc.sbuf_tensor(f"fa_hs{i}", [128, 1024], BF16)) for i in range(2)]
    stt = e(nc.sbuf_tensor("fa_st", [128, 32], F32))
    sqd1 = e(nc.sbuf_tensor("fa_sqd", [128, HID], BF16))
    identb = e(nc.sbuf_tensor("fa_idb2", [128, 128], BF16))
    pms = [e(nc.psum_tensor(f"fa_p{i}", [128, 512], F32)) for i in range(2)]
    pqb = e(nc.psum_tensor("fa_pqb2", [128, 1024], BF16))
    ldw = p.sem("ldw")
    ldg = p.sem("fa_ldg")
    p.dma("sync", ldw, lambda s: s.dma_start(
        out=b1bc[:], in_=io["b1"].partition_broadcast(128)))
    p.dma("sync", ldw, lambda s: s.dma_start(
        out=g2bc[:], in_=io["ln2_g"].partition_broadcast(128)))
    p.dma("sync", ldw, lambda s: s.dma_start(
        out=bb2bc[:], in_=io["ln2_b"].partition_broadcast(128)))
    wldv = p.dma("sync", ldw, lambda s: s.dma_start(
        out=mcc[:], in_=io["sc1"][0:1, 4:5].partition_broadcast(128)))
    cldb = p.dma("gpsimd", ldg, lambda g_: g_.dma_start(out=identb[:],
                                                        in_=io["ident_f"]))
    sem_pe = p.sem("fa_pe")
    sem_d = p.sem("fa_d")
    sem_a = p.sem("fa_a")
    sem_ev = p.sem("fa_ev")
    las = [p.sem("fa_la"), p.sem("fa_lb")]
    slots = StSlots(p, "fa", 2)
    pe_group_end = {}
    evh_bank = {}
    tr_free = None
    waited = False
    for g in range(NG):
        la = las[g % 2]
        hf = hfm[g % 2]
        if g >= 2:
            p.wait("sync", sem_pe, pe_group_end[g - 2])
        for dc in range(8):
            p.dma("sync", la, lambda s, g=g, dc=dc, hf=hf: s.dma_start(
                out=hf[dc][:], in_=io["h1_bf"][g * 512:(g + 1) * 512,
                                               dc * 128:(dc + 1) * 128],
                transpose=True))
        ldv = p.counts[la]
        p.wait("tensor", la, ldv)
        if not waited:
            p.wait("tensor", ldw, wldv)
            p.wait("vector", ldw, wldv)
            p.wait("tensor", ldg, cldb)
            waited = True
        for j in range(4):
            tt = g * 4 + j
            for hg in range(8):
                b = pms[hg % 2]
                if evh_bank.get(hg % 2) is not None:
                    p.wait("tensor", sem_d, evh_bank[hg % 2])
                cpe = mm_acc(p, b[:],
                             [(hf[dc][:, j * 128:(j + 1) * 128],
                               w1q[:, dc * HID + hg * 512: dc * HID + (hg + 1) * 512])
                              for dc in range(8)], sem_pe)
                p.wait("vector", sem_pe, cpe)
                evh_bank[hg % 2] = p.op(
                    "vector", lambda v, hg=hg, b=b: v.scalar_tensor_tensor(
                        htok[:, hg * 512:(hg + 1) * 512], b[:], mcc[:, 0:1],
                        b1bc[:, hg * 512:(hg + 1) * 512], op0=ALU.mult, op1=ALU.add),
                    inc=sem_d, small=(hg == 7))
            p.wait("scalar", sem_d, evh_bank[1])
            cgel = p.op("scalar", lambda s: s.activation(hG[:], htok[:], AF.Gelu),
                        inc=sem_a, small=True)
            p.wait("vector", sem_a, cgel)
            p.wait("scalar", sem_a, cgel)
            cout = ln_tiles(p, "fa", stt, 0, [sqd1], [htok], [hG], [hG], HID,
                            sem_d, sem_a)
            p.wait("vector", sem_d, cout[0])
            cm_ = p.op("vector", lambda v: v.tensor_mul(hG[:], hG[:], g2bc[:]),
                       inc=sem_d)
            p.wait("vector", sem_d, cm_)
            ch2 = p.op("vector", lambda v: v.tensor_add(hG[:], hG[:], bb2bc[:]),
                       inc=sem_d)
            p.wait("vector", sem_d, ch2)
            ch2 = p.op("vector", lambda v: v.tensor_copy(h2b[:], hG[:]),
                       inc=sem_d, small=True)
            p.wait("tensor", sem_d, ch2)
            for g8 in range(4):
                if tr_free is not None:
                    p.wait("tensor", sem_ev, tr_free)
                for t8 in range(8):
                    p.op("tensor", lambda t, g8=g8, t8=t8: t.transpose(
                        pqb[:, t8 * 128:(t8 + 1) * 128],
                        h2b[:, (g8 * 8 + t8) * 128:(g8 * 8 + t8 + 1) * 128],
                        identb[:]),
                        inc=(sem_pe if t8 == 7 else None))
                cpe = p.counts[sem_pe]
                sj = g8 % 2
                slots.guard("scalar", sj)
                p.wait("scalar", sem_pe, cpe)
                cev = p.op("scalar", lambda s, sj=sj: s.activation(
                    hstage[sj][:], pqb[:], AF.Copy), inc=sem_ev)
                tr_free = cev
                p.wait("sync", sem_ev, cev)
                slots.store(sj, lambda s, tt=tt, g8=g8, sj=sj: s.dma_start(
                    out=io["h2f"][tt, g8 * 8:(g8 + 1) * 8, :, :].transpose([1, 0, 2]),
                    in_=hstage[sj][:].rearrange("p (a c) -> p a c", c=128)))
        pe_group_end[g] = p.counts[sem_pe]
    p.barrier()


phase_l1ffna.reads = ("w1", "w2", "b1", "ln2_g", "ln2_b", "h1_bf", "ident_f",
                      "sc1")
phase_l1ffna.writes = ("h2f", "sc1")



def phase_l1ffnb(p, io, cst, scope):
    nc = p.nc
    e = scope.enter_context
    w2q = e(nc.sbuf_tensor("fb_w2q", [128, 32 * D], BF16))
    emit_quantize_wT(p, io, scope, "fb", "w2", D, HID, 5, w2q)
    p.barrier()
    h2fm = [e(nc.sbuf_tensor(f"fb_h{pr}", [128, HID], BF16)) for pr in range(2)]
    b2bc = e(nc.sbuf_tensor("fb_b2bc", [128, D], F32))
    mcc = e(nc.sbuf_tensor("fb_mcc", [128, 1], F32))
    xrt = [e(nc.sbuf_tensor(f"fb_xr{i}", [128, D], F32)) for i in range(2)]
    a2t = [e(nc.sbuf_tensor(f"fb_a2{i}", [128, D], F32)) for i in range(2)]
    ost = [e(nc.sbuf_tensor(f"fb_o{i}", [128, D], F32)) for i in range(2)]
    pms = [e(nc.psum_tensor(f"fb_p{i}", [128, 512], F32)) for i in range(4)]
    banks = Banks(p, pms)
    ldw = p.sem("ldw")
    p.dma("sync", ldw, lambda s: s.dma_start(
        out=b2bc[:], in_=io["b2"].partition_broadcast(128)))
    wldv = p.dma("sync", ldw, lambda s: s.dma_start(
        out=mcc[:], in_=io["sc1"][0:1, 5:6].partition_broadcast(128)))
    sem_pe = p.sem("fb_pe")
    sem_d = p.sem("fb_d")
    lh = [p.sem("fb_lha"), p.sem("fb_lhb")]
    lx = [p.sem("fb_lxa"), p.sem("fb_lxb")]
    slots = StSlots(p, "fb", 2)
    pe_tile_end = {}
    dve_tile_end = {}
    p.wait("vector", ldw, wldv)
    for tt in range(NT):
        pr = tt % 2
        if tt >= 2:
            p.wait("sync", sem_pe, pe_tile_end[tt - 2])
        ch = p.dma("sync", lh[pr], lambda s, tt=tt, pr=pr: s.dma_start(
            out=h2fm[pr][:].rearrange("pp (a c) -> pp a c", c=128),
            in_=io["h2f"][tt, :, :, :].transpose([1, 0, 2])))
        if tt >= 2:
            p.wait("sync", sem_d, dve_tile_end[tt - 2])
        p.dma("sync", lx[pr], lambda s, tt=tt, pr=pr: s.dma_start(
            out=xrt[pr][:], in_=io["xres"][tt * 128:(tt + 1) * 128, :]))
        cx = p.dma("sync", lx[pr], lambda s, tt=tt, pr=pr: s.dma_start(
            out=a2t[pr][:], in_=io["attn2"][tt * 128:(tt + 1) * 128, :]))
        p.wait("tensor", lh[pr], ch)
        slots.guard("vector", pr)
        for half in range(2):
            b = banks.next()
            cpe = mm_acc(p, b[0][:],
                         [(h2fm[pr][:, hc * 128:(hc + 1) * 128],
                           w2q[:, hc * D + half * 512: hc * D + (half + 1) * 512])
                          for hc in range(32)], sem_pe)
            p.wait("vector", sem_pe, cpe)
            p.wait("vector", lx[pr], cx)
            co = p.op("vector", lambda v, pr=pr, half=half, b=b: v.scalar_tensor_tensor(
                ost[pr][:, half * 512:(half + 1) * 512], b[0][:], mcc[:, 0:1],
                b2bc[:, half * 512:(half + 1) * 512], op0=ALU.mult, op1=ALU.add),
                inc=sem_d)
            Banks.release(b, sem_d, co)
            p.wait("vector", sem_d, co)
            co = p.op("vector", lambda v, pr=pr, half=half: v.tensor_add(
                ost[pr][:, half * 512:(half + 1) * 512],
                ost[pr][:, half * 512:(half + 1) * 512],
                xrt[pr][:, half * 512:(half + 1) * 512]), inc=sem_d)
            p.wait("vector", sem_d, co)
            co = p.op("vector", lambda v, pr=pr, half=half: v.tensor_add(
                ost[pr][:, half * 512:(half + 1) * 512],
                ost[pr][:, half * 512:(half + 1) * 512],
                a2t[pr][:, half * 512:(half + 1) * 512]),
                inc=sem_d, small=(half == 1))
        cfin = p.counts[sem_d]
        dve_tile_end[tt] = cfin
        p.wait("sync", sem_d, cfin)
        slots.store(pr, lambda s, tt=tt, pr=pr: s.dma_start(
            out=io["out"][tt * 128:(tt + 1) * 128, :], in_=ost[pr][:]))
        pe_tile_end[tt] = p.counts[sem_pe]
    p.barrier()


phase_l1ffnb.reads = ("w2", "b2", "h2f", "xres", "attn2", "ident_f", "sc1")
phase_l1ffnb.writes = ("out",)




PHASES_ALL = [phase_ln0, phase_rqkv, phase_rattn, phase_rproj,
              phase_mqkv, phase_mattn, phase_mproj, phase_norm1, phase_moe,
              phase_l1prep, phase_l1wcast, phase_l1qkv, phase_l1attn,
              phase_l1proj, phase_l1ffna, phase_l1ffnb]
GROUPS = [PHASES_ALL]


def build_group(phases, produced_before):
    nc = bass.Bass("TRN2", target_bir_lowering=False, debug=False)
    reads = set()
    writes = set()
    for ph in phases:
        reads |= set(ph.reads)
        writes |= set(ph.writes)
    io = {}
    ext_in = []
    ext_out = []
    for name in sorted(reads | writes):
        if name in INPUT_SPECS:
            shape, dt = INPUT_SPECS[name]
            io[name] = nc.dram_tensor(name, list(shape), dt, kind="ExternalInput").ap()
            ext_in.append(name)
        else:
            shape, dt = SCRATCH[name]
            if name in reads and name in produced_before and name not in writes:
                kind = "ExternalInput"
                ext_in.append(name)
            elif name in writes:
                kind = "ExternalOutput"
                ext_out.append(name)
            else:
                kind = "Internal"
            io[name] = nc.dram_tensor(name, list(shape), dt, kind=kind).ap()
    p = Prog(nc)
    with ExitStack() as stack:
        p.dmy = {e: stack.enter_context(nc.sbuf_tensor(f"dmy_{e}", [128, 2], F32))
                 for e in ("vector", "scalar", "gpsimd")}
        cst = None
        for ph in phases:
            with ExitStack() as scope:
                ph(p, io, cst, scope)
                p.barrier()
        p.finish()
        sems_needed = [s for s, c in p.counts.items() if c > 0 or s in p.dma_sems]
        with ExitStack() as semstack:
            h = {s: semstack.enter_context(nc.semaphore(s)) for s in sems_needed}
            with nc.Block() as block:
                p.build(block, h)
    return nc, ext_in, ext_out


def kernel(**inputs):
    full = prepare_inputs(inputs)
    consts = make_consts()
    produced = set()
    scratch_vals = [dict() for _ in range(NCORES)]
    final = None
    for phases in GROUPS:
        nc, ext_in, ext_out = build_group(phases, produced)
        in_maps = []
        for c in range(NCORES):
            m = {}
            for name in ext_in:
                if name in consts:
                    m[name] = consts[name]
                elif name in full:
                    m[name] = full[name] if not isinstance(full[name], list) else full[name][c]
                else:
                    m[name] = scratch_vals[c][name]
            in_maps.append(m)
        res = run_bass_kernel_spmd(nc, in_maps, core_ids=list(range(NCORES)))
        for c in range(NCORES):
            for name in ext_out:
                scratch_vals[c][name] = res.results[c][name]
        produced |= set(ext_out)
    out = np.stack([scratch_vals[c][OUTPUT_NAME] for c in range(NCORES)], axis=0)
    return out


OUTPUT_NAME = "out"


def prepare_inputs(inputs):
    f = {}
    f["x"] = [np.ascontiguousarray(inputs["L0_attn_x"][c]) for c in range(NCORES)]
    ren = {
        "Wqkv": "L0_attn_Wqkv", "bqkv": "L0_attn_bqkv", "Wproj": "L0_attn_Wproj",
        "bproj": "L0_attn_bproj", "gamma": "L0_attn_gamma",
        "qkv_w": "L0_ffn_qkv_w", "qkv_b": "L0_ffn_qkv_b", "out_w": "L0_ffn_out_w",
        "out_b": "L0_ffn_out_b", "norm1_w": "L0_ffn_norm1_w", "norm1_b": "L0_ffn_norm1_b",
        "sw1": "L0_ffn_sw1", "sb1": "L0_ffn_sb1", "sw2": "L0_ffn_sw2", "sb2": "L0_ffn_sb2",
        "rw1": "L0_ffn_rw1", "rb1": "L0_ffn_rb1", "rw2": "L0_ffn_rw2", "rb2": "L0_ffn_rb2",
        "centroid": "L0_ffn_centroid", "route_bias": "L0_ffn_route_bias",
        "norm2_w": "L0_ffn_norm2_w", "norm2_b": "L0_ffn_norm2_b",
        "Wq": "L1_attn_Wq", "Wk": "L1_attn_Wk", "Wv": "L1_attn_Wv", "Wo": "L1_attn_Wo",
        "ln1_g": "L1_ffn_ln1_g", "ln1_b": "L1_ffn_ln1_b", "w1": "L1_ffn_w1",
        "b1": "L1_ffn_b1", "ln2_g": "L1_ffn_ln2_g", "ln2_b": "L1_ffn_ln2_b",
        "w2": "L1_ffn_w2", "b2": "L1_ffn_b2",
    }
    for k, src in ren.items():
        a = np.asarray(inputs[src])
        shape = INPUT_SPECS[k][0]
        f[k] = np.ascontiguousarray(a.reshape(shape))
    return f


# version 09 (final)
# speedup vs baseline: 1.3917x; 1.0921x over previous
import numpy as np
from contextlib import ExitStack

import concourse.bass as bass
import concourse.mybir as mybir
from concourse.bass_utils import run_bass_kernel_spmd
import ml_dtypes

F32 = mybir.dt.float32
BF16 = mybir.dt.bfloat16
AF = mybir.ActivationFunctionType
ALU = mybir.AluOpType
AX = mybir.AxisListType

NCORES = 8
S = 4096
D = 1024
NT = S // 128
NG = S // 512
NH, HD = 16, 64
FF = 2048
HID = 4096
EPS = 1e-5

ENGINES = ("sync", "tensor", "vector", "scalar", "gpsimd")


class Prog:

    def __init__(self, nc):
        self.nc = nc
        self.thunks = {e: [] for e in ENGINES}
        self.counts = {}
        self.deferred = {e: [] for e in ENGINES}
        self.ops_since_barrier = {e: 0 for e in ENGINES}
        self.engine_sem = {"tensor": "g_pe", "vector": "g_dve", "scalar": "g_act",
                          "gpsimd": "g_pool", "sync": "g_sy"}
        for s in self.engine_sem.values():
            self.sem(s)
        self.dma_sems = set()
        self.dmy = None

    _SEM_MAP = [
        ("_qlda", "lpA0"), ("_qldb", "lpA1"), ("_qldg", "s_ldg"),
        ("_qla", "lpA0"), ("_qlb", "lpA1"), ("_kla", "lpB0"), ("_klb", "lpB1"),
        ("_vla", "lpC0"), ("_vlb", "lpC1"), ("_khl", "lpB0"), ("_khm", "lpB1"),
        ("_vhl", "lpC0"), ("_vhm", "lpC1"),
        ("_lha", "lpA0"), ("_lhb", "lpA1"), ("_lxa", "lpC0"), ("_lxb", "lpC1"),
        ("_lda", "lpA0"), ("_ldb", "lpA1"), ("_ldi", "s_ldc"), ("_ldc", "s_ldc"),
        ("_ldg", "s_ldg"), ("_qld", "s_ldc"), ("_lx", "lpB0"),
        ("_la", "lpA0"), ("_lb", "lpA1"), ("_ld", "s_ldx"),
        ("_qd", "s_d"), ("_qa", "s_a"), ("_qev", "s_ev"), ("_qpe", "s_pe"),
        ("_pe", "s_pe"), ("_ev", "s_ev"), ("_pl", "s_pl"),
        ("_d", "s_d"), ("_a", "s_a"), ("_y", "s_y"),
    ]

    def _canon(self, name):
        if name.startswith("g_") or name in ("st", "stg", "ldw"):
            return name
        if name == "ldwg":
            return "s_ldg"
        if "_ss" in name:
            return "sl" + name[name.index("_ss"):]
        for suf, c in self._SEM_MAP:
            if name.endswith(suf):
                return c
        return name

    def sem(self, name):
        name = self._canon(name)
        if name not in self.counts:
            self.counts[name] = 0
        return name

    def wait(self, eng, sem, val):
        sem = self._canon(sem)
        assert sem in self.counts, sem
        assert val <= self.counts[sem], (sem, val, self.counts[sem])
        if val > 0:
            self.thunks[eng].append(("wait", sem, val))

    def op(self, eng, fn, inc=None, small=False):
        inc = self._canon(inc) if inc is not None else None
        """Emit one engine op, optionally incrementing `inc` on completion.

        small=True: the op produces a tiny (reduce/stat) output — also emit a
        trailing dummy op carrying a second increment, so hardware consumers
        (which may observe the first inc before writeback lands) get extra
        slack, while the simulator's race model is satisfied by the first.
        Returns the count the consumer should wait for."""
        incs = []
        ret = None
        if inc is not None:
            self.sem(inc)
            self.counts[inc] += 1
            ret = self.counts[inc]
            incs.append((inc, 1))
        self.thunks[eng].append(["op", fn, incs])
        self.ops_since_barrier[eng] += 1
        if small and inc is not None:
            ret = self.op(eng, self._drain_fn(eng), inc=inc)
        return ret

    def dma(self, eng, sem, fn):
        sem = self._canon(sem)
        self.sem(sem)
        self.dma_sems.add(sem)
        self.counts[sem] += 16
        self.thunks[eng].append(["op", fn, [(sem, 16)]])
        self.ops_since_barrier[eng] += 1
        return self.counts[sem]

    def _dummy_fn(self, eng):
        return lambda e: e.nop()

    def _drain_fn(self, eng):
        if eng == "sync":
            return lambda e: e.nop()
        return lambda e: e.drain(fusable=True)

    def flush_deferred(self, eng):
        while self.deferred[eng]:
            self.op(eng, self._dummy_fn(eng))

    def _last_real_op(self, eng):
        for item in reversed(self.thunks[eng]):
            if item[0] == "op":
                return item
        return None

    def barrier(self):
        active = [e for e in ENGINES
                  if self.ops_since_barrier[e] > 0 or self.deferred[e]]
        for e in active:
            gs = self.engine_sem[e]
            self.op(e, self._drain_fn(e), inc=gs)
        targets = [(self.engine_sem[e], self.counts[self.engine_sem[e]]) for e in active]
        dmas = [(s, self.counts[s]) for s in sorted(self.dma_sems) if self.counts[s] > 0]
        for e in ENGINES:
            for s, v in targets:
                if self.engine_sem[e] == s:
                    continue
                self.wait(e, s, v)
            for s, v in dmas:
                self.wait(e, s, v)
        for e in ENGINES:
            self.ops_since_barrier[e] = 0

    def finish(self):
        for e in ENGINES:
            self.flush_deferred(e)
        for s in sorted(self.dma_sems):
            if self.counts[s] > 0:
                self.wait("sync", s, self.counts[s])

    def build(self, block, h):
        for ename in ENGINES:
            thunks = self.thunks[ename]
            if not thunks:
                continue

            def body(engine, thunks=thunks):
                for item in thunks:
                    if item[0] == "wait":
                        engine.wait_ge(h[item[1]], item[2])
                    else:
                        ins = item[1](engine)
                        assert len(item[2]) <= 1, item[2]
                        for s, n in item[2]:
                            ins.then_inc(h[s], n)

            getattr(block, ename)(body)


class Banks:

    def __init__(self, p, psums):
        self.p = p
        self.banks = [[ps, None] for ps in psums]
        self.i = 0

    def next(self):
        b = self.banks[self.i % len(self.banks)]
        self.i += 1
        if b[1] is not None:
            self.p.wait("tensor", b[1][0], b[1][1])
            b[1] = None
        return b

    @staticmethod
    def release(b, sem, count):
        b[1] = (sem, count)


class StSlots:

    def __init__(self, p, tag, n):
        self.p = p
        self.sems = [p.sem(f"{tag}_ss{i}") for i in range(n)]
        self.last = [0] * n

    def guard(self, eng, i):
        if self.last[i] > 0:
            self.p.wait(eng, self.sems[i], self.last[i])

    def store(self, i, fn):
        self.last[i] = self.p.dma("sync", self.sems[i], fn)


def load_bias_cols(p, sem, dst, src2d, ncols):
    last = None
    for c in range(ncols):
        last = p.dma("sync", sem, lambda s, c=c: s.dma_start(
            out=dst[:, c:c + 1],
            in_=src2d[0:1, c * 128:(c + 1) * 128].rearrange("o (p q) -> (o p) q", q=1)))
    return last


def mm_acc(p, out_ap, pairs, sem_pe):
    n = len(pairs)
    for i, (l, r) in enumerate(pairs):
        p.op("tensor",
             lambda t, l=l, r=r, st=(i == 0), sp=(i == n - 1), o=out_ap:
             t.matmul(o, l, r, start=st, stop=sp),
             inc=(sem_pe if i == n - 1 else None))
    return p.counts[sem_pe]



SCRATCH = {
    "xn_f32": ((S, D), F32),
    "xn_bf": ((S, D), BF16),
    "qf": ((D, S), BF16),
    "kf": ((D, S), BF16),
    "vt": ((S, D), BF16),
    "of": ((D, S), BF16),
    "attn_br": ((S, D), F32),
    "x1_f32": ((S, D), F32),
    "x1_bf": ((S, D), BF16),
    "xres": ((S, D), F32),
    "xn2_bf": ((S, D), BF16),
    "h1_bf": ((S, D), BF16),
    "attn2": ((S, D), F32),
    "wqT_src": ((D, D), BF16),
    "wkT_src": ((D, D), BF16),
    "wvT_src": ((D, D), BF16),
    "woT_src": ((D, D), BF16),
    "h2f": ((NT, HID // 128, 128, 128), BF16),
    "ymha": ((S, D), F32),
    "ymoe": ((S, D), F32),
    "out": ((S, D), F32),
    "sc1": ((1, 8), F32),
}

INPUT_SPECS = {
    "x": ((S, D), F32),
    "Wqkv": ((D, 3 * D), F32), "bqkv": ((1, 3 * D), F32),
    "Wproj": ((D, D), F32), "bproj": ((1, D), F32), "gamma": ((1, 1), F32),
    "qkv_w": ((D, 3 * D), F32), "qkv_b": ((1, 3 * D), F32),
    "out_w": ((D, D), F32), "out_b": ((1, D), F32),
    "norm1_w": ((1, D), F32), "norm1_b": ((1, D), F32),
    "sw1": ((D, FF), F32), "sb1": ((1, FF), F32),
    "sw2": ((FF, D), F32), "sb2": ((1, D), F32),
    "rw1": ((8, D, FF), F32), "rb1": ((8, FF), F32),
    "rw2": ((8, FF, D), F32), "rb2": ((8, D), F32),
    "centroid": ((8, D), F32), "route_bias": ((1, 8), F32),
    "norm2_w": ((1, D), F32), "norm2_b": ((1, D), F32),
    "Wq": ((D, D), F32), "Wk": ((D, D), F32), "Wv": ((D, D), F32), "Wo": ((D, D), F32),
    "ln1_g": ((1, D), F32), "ln1_b": ((1, D), F32),
    "w1": ((HID, D), F32), "b1": ((1, HID), F32),
    "ln2_g": ((1, HID), F32), "ln2_b": ((1, HID), F32),
    "w2": ((D, HID), F32), "b2": ((1, D), F32),
    "ident_f": ((128, 128), F32),
    "diffT": ((128, 128), F32), "triT": ((128, 128), F32),
    "ones_f": ((1, 128), F32), "onescol_b": ((128, 1), BF16),
    "triT_b": ((128, 128), BF16), "neg20": ((1, 1), F32),
}


def make_consts():
    i = np.arange(128)
    return {
        "ident_f": np.eye(128, dtype=np.float32),
        "diffT": (i[None, :] - i[:, None]).astype(np.float32),
        "triT": (i[None, :] >= i[:, None]).astype(np.float32),
        "ones_f": np.ones((1, 128), np.float32),
        "onescol_b": np.ones((128, 1)).astype(ml_dtypes.bfloat16),
        "triT_b": (i[None, :] >= i[:, None]).astype(ml_dtypes.bfloat16),
        "neg20": np.full((1, 1), -20.0, np.float32),
    }



def emit_ln_stream(p, io, cst, scope, tag, src_name, dst_f32, dst_bf,
                   extra_ln=None):
    nc = p.nc
    e = scope.enter_context
    st = e(nc.sbuf_tensor(tag + "_stat", [128, 64], F32))
    xt = [e(nc.sbuf_tensor(f"{tag}_xt{j}", [128, D], F32)) for j in range(4)]
    xm = [e(nc.sbuf_tensor(f"{tag}_xm{j}", [128, D], F32)) for j in range(4)]
    xnb = [e(nc.sbuf_tensor(f"{tag}_xnb{j}", [128, D], F32)) for j in range(4)]
    bf = [e(nc.sbuf_tensor(f"{tag}_bf{j}", [128, D], BF16)) for j in range(4)]
    sqd = [e(nc.sbuf_tensor(f"{tag}_sqd{j}", [128, D], BF16)) for j in range(4)]
    if extra_ln is not None:
        g_name, b_name, dst2_bf = extra_ln
        gb = e(nc.sbuf_tensor(tag + "_g", [128, D], F32))
        bb = e(nc.sbuf_tensor(tag + "_b", [128, D], F32))
        h2 = [e(nc.sbuf_tensor(f"{tag}_h2{j}", [128, D], F32)) for j in range(4)]
        bf2 = [e(nc.sbuf_tensor(f"{tag}_bf2{j}", [128, D], BF16)) for j in range(4)]
        ldw = p.sem("ldw")
        p.dma("sync", ldw, lambda s: s.dma_start(
            out=gb[:], in_=io[g_name].partition_broadcast(128)))
        wv = p.dma("sync", ldw, lambda s: s.dma_start(
            out=bb[:], in_=io[b_name].partition_broadcast(128)))
        p.wait("vector", ldw, wv)

    sem_ld = p.sem(tag + "_ld")
    sem_d = p.sem(tag + "_d")
    sem_a = p.sem(tag + "_a")
    sem_st = p.sem("st")
    src = io[src_name] if isinstance(src_name, str) else src_name

    def one_ln(tiles_in, out_tiles, colbase):
        cb = colbase
        cnm, cxm, csq = {}, {}, {}
        for j in range(4):
            cnm[j] = p.op("scalar",
                          lambda s, j=j: s.activation(sqd[j][:], tiles_in[j][:],
                                                      AF.Copy, scale=-1.0 / D,
                                                      accum_out=st[:, cb + j:cb + j + 1]),
                          inc=sem_a, small=True)
        for j in range(4):
            p.wait("vector", sem_a, cnm[j])
            cxm[j] = p.op("vector",
                          lambda v, j=j: v.tensor_scalar(xm[j][:], tiles_in[j][:],
                                                         st[:, cb + j:cb + j + 1],
                                                         None, op0=ALU.add),
                          inc=sem_d, small=True)
        for j in range(4):
            p.wait("scalar", sem_d, cxm[j])
            csq[j] = p.op("scalar",
                          lambda s, j=j: s.activation(sqd[j][:], xm[j][:], AF.Square,
                                                      accum_out=st[:, cb + 8 + j:cb + 9 + j]),
                          inc=sem_a, small=True)
        p.wait("vector", sem_a, csq[3])
        cv = p.op("vector",
                  lambda v: v.tensor_scalar(st[:, cb + 16:cb + 20],
                                            st[:, cb + 8:cb + 12],
                                            1.0 / D, EPS, op0=ALU.mult, op1=ALU.add),
                  inc=sem_d, small=True)
        p.wait("scalar", sem_d, cv)
        cl = p.op("scalar", lambda s: s.activation(st[:, cb + 20:cb + 24],
                                                   st[:, cb + 16:cb + 20], AF.Ln),
                  inc=sem_a, small=True)
        p.wait("vector", sem_a, cl)
        cln = p.op("vector",
                   lambda v: v.tensor_scalar(st[:, cb + 16:cb + 20],
                                             st[:, cb + 20:cb + 24],
                                             -0.5, None, op0=ALU.mult),
                   inc=sem_d, small=True)
        p.wait("scalar", sem_d, cln)
        ca = p.op("scalar", lambda s: s.activation(st[:, cb + 24:cb + 28],
                                                   st[:, cb + 16:cb + 20], AF.Exp),
                  inc=sem_a, small=True)
        cout = {}
        for j in range(4):
            p.wait("vector", sem_a, ca)
            cout[j] = p.op("vector",
                           lambda v, j=j: v.tensor_scalar(out_tiles[j][:], xm[j][:],
                                                          st[:, cb + 24 + j:cb + 25 + j],
                                                          None, op0=ALU.mult),
                           inc=sem_d, small=True)
        return cout

    for g in range(NG):
        for j in range(4):
            tt = g * 4 + j
            c = p.dma("sync", sem_ld, lambda s, tt=tt, j=j: s.dma_start(
                out=xt[j][:], in_=src[tt * 128:(tt + 1) * 128, :]))
        p.wait("vector", sem_ld, c)
        p.wait("scalar", sem_ld, c)
        cout = one_ln(xt, xnb, 0)
        for j in range(4):
            tt = g * 4 + j
            if dst_f32 is not None:
                p.wait("sync", sem_d, cout[j])
                p.dma("sync", sem_st, lambda s, tt=tt, j=j: s.dma_start(
                    out=io[dst_f32][tt * 128:(tt + 1) * 128, :], in_=xnb[j][:]))
            p.wait("scalar", sem_d, cout[j])
            cc = p.op("scalar", lambda s, j=j: s.copy(bf[j][:], xnb[j][:]),
                      inc=sem_a)
            p.wait("sync", sem_a, cc)
            p.dma("sync", sem_st, lambda s, tt=tt, j=j: s.dma_start(
                out=io[dst_bf][tt * 128:(tt + 1) * 128, :], in_=bf[j][:]))
        if extra_ln is not None:
            cout2 = one_ln(xnb, h2, 32)
            for j in range(4):
                tt = g * 4 + j
                p.wait("gpsimd", sem_d, cout2[j])
                cw = p.op("gpsimd", lambda v, j=j: v.tensor_mul(h2[j][:], h2[j][:], gb[:]),
                          inc="g_pool", small=True)
                p.wait("gpsimd", "g_pool", cw)
                cw = p.op("gpsimd", lambda v, j=j: v.tensor_add(h2[j][:], h2[j][:], bb[:]),
                          inc="g_pool", small=True)
                p.wait("scalar", "g_pool", cw)
                cc = p.op("scalar", lambda s, j=j: s.copy(bf2[j][:], h2[j][:]),
                          inc=sem_a)
                p.wait("sync", sem_a, cc)
                p.dma("sync", sem_st, lambda s, tt=tt, j=j: s.dma_start(
                    out=io[dst2_bf][tt * 128:(tt + 1) * 128, :], in_=bf2[j][:]))
        p.barrier()


def phase_ln0(p, io, cst, scope):
    emit_ln_stream(p, io, cst, scope, "ln0", "x", "xn_f32", "xn_bf")


phase_ln0.reads = ("x",)
phase_ln0.writes = ("xn_f32", "xn_bf")



def emit_qkv(p, io, cst, scope, tag, src_bf, w_name, b_name):
    nc = p.nc
    e = scope.enter_context
    wsb = [e(nc.sbuf_tensor(f"{tag}_w{dc}", [128, 3 * D], BF16)) for dc in range(8)]
    bcols = e(nc.sbuf_tensor(f"{tag}_bc", [128, 24], F32))
    bvb = e(nc.sbuf_tensor(f"{tag}_bvb", [128, D], F32))
    xfm = [[e(nc.sbuf_tensor(f"{tag}_x{pr}{dc}", [128, 512], BF16)) for dc in range(8)]
           for pr in range(2)]
    stage = [e(nc.sbuf_tensor(f"{tag}_s{i}", [128, 512], BF16)) for i in range(6)]
    pms = [e(nc.psum_tensor(f"{tag}_p{i}", [128, 512], F32)) for i in range(4)]
    banks = Banks(p, pms)

    ldw = p.sem("ldw")
    ldwg = p.sem("ldwg")
    for dc in range(8):
        wldg = p.dma("gpsimd", ldwg, lambda g, dc=dc: g.dma_start(
            out=wsb[dc][:], in_=io[w_name][dc * 128:(dc + 1) * 128, :]))
    load_bias_cols(p, ldw, bcols, io[b_name], 24)
    wldv = p.dma("sync", ldw, lambda s: s.dma_start(
        out=bvb[:], in_=io[b_name][:, 2 * D:3 * D].partition_broadcast(128)))

    sem_pe = p.sem(tag + "_pe")
    sem_ev = p.sem(tag + "_ev")
    las = [p.sem(tag + "_la"), p.sem(tag + "_lb")]
    slots = StSlots(p, tag, 6)
    pe_group_end = {}
    waited_w = False
    for g in range(NG):
        la = las[g % 2]
        xf = xfm[g % 2]
        if g >= 2:
            p.wait("sync", sem_pe, pe_group_end[g - 2])
        for dc in range(8):
            p.dma("sync", la, lambda s, g=g, dc=dc, xf=xf: s.dma_start(
                out=xf[dc][:], in_=io[src_bf][g * 512:(g + 1) * 512,
                                              dc * 128:(dc + 1) * 128],
                transpose=True))
        ldv = p.counts[la]
        p.wait("tensor", la, ldv)
        if not waited_w:
            p.wait("tensor", ldw, wldv)
            p.wait("tensor", ldwg, wldg)
            p.wait("scalar", ldw, wldv)
            p.wait("vector", ldw, wldv)
            p.wait("vector", ldwg, wldg)
            waited_w = True
        si = 0
        for mc in range(16):
            b = banks.next()
            cpe = mm_acc(p, b[0][:],
                         [(wsb[dc][:, mc * 128:(mc + 1) * 128], xf[dc][:])
                          for dc in range(8)], sem_pe)
            sj = si % 6
            si += 1
            slots.guard("vector", sj)
            p.wait("vector", sem_pe, cpe)
            cev = p.op("vector", lambda v, mc=mc, sj=sj, b=b: v.tensor_scalar(
                stage[sj][:], b[0][:], bcols[:, mc:mc + 1], None, op0=ALU.add),
                inc=sem_ev)
            Banks.release(b, sem_ev, cev)
            dst = io["qf"] if mc < 8 else io["kf"]
            r = (mc % 8) * 128
            p.wait("sync", sem_ev, cev)
            slots.store(sj, lambda s, dst=dst, r=r, g=g, sj=sj: s.dma_start(
                out=dst[r:r + 128, g * 512:(g + 1) * 512], in_=stage[sj][:]))
        for tt in range(4):
            for half in range(2):
                b = banks.next()
                cpe = mm_acc(p, b[0][:],
                             [(xf[dc][:, tt * 128:(tt + 1) * 128],
                               wsb[dc][:, 2 * D + half * 512: 2 * D + (half + 1) * 512])
                              for dc in range(8)], sem_pe)
                sj = si % 6
                si += 1
                slots.guard("vector", sj)
                p.wait("vector", sem_pe, cpe)
                cev = p.op("vector", lambda v, sj=sj, b=b, half=half: v.tensor_tensor(
                    stage[sj][:], b[0][:], bvb[:, half * 512:(half + 1) * 512],
                    op=ALU.add), inc=sem_ev)
                Banks.release(b, sem_ev, cev)
                row = (g * 4 + tt) * 128
                p.wait("sync", sem_ev, cev)
                slots.store(sj, lambda s, row=row, half=half, sj=sj: s.dma_start(
                    out=io["vt"][row:row + 128, half * 512:(half + 1) * 512],
                    in_=stage[sj][:]))
        pe_group_end[g] = p.counts[sem_pe]
    p.barrier()


def phase_rqkv(p, io, cst, scope):
    emit_qkv(p, io, cst, scope, "rqkv", "xn_bf", "Wqkv", "bqkv")


phase_rqkv.reads = ("xn_bf", "Wqkv", "bqkv")
phase_rqkv.writes = ("qf", "kf", "vt")


def phase_mqkv(p, io, cst, scope):
    emit_qkv(p, io, cst, scope, "mqkv", "xn_bf", "qkv_w", "qkv_b")


phase_mqkv.reads = ("xn_bf", "qkv_w", "qkv_b")
phase_mqkv.writes = ("qf", "kf", "vt")



def phase_rattn(p, io, cst, scope):
    nc = p.nc
    e = scope.enter_context
    Pb = e(nc.sbuf_tensor("ra_P", [128, 32 * 512], BF16))
    Sd = [e(nc.sbuf_tensor(f"ra_Sd{i}", [128, 512], F32)) for i in range(2)]
    qg = [[e(nc.sbuf_tensor(f"ra_q{pr}{dc}", [128, 512], BF16)) for dc in range(8)]
          for pr in range(2)]
    kblk = [e(nc.sbuf_tensor(f"ra_k{i}", [128, D], BF16)) for i in range(2)]
    vblk = [e(nc.sbuf_tensor(f"ra_v{i}", [128, D], BF16)) for i in range(2)]
    maskT = e(nc.sbuf_tensor("ra_mask", [128, 128], F32))
    diffT = e(nc.sbuf_tensor("ra_diff", [128, 128], F32))
    triT = e(nc.sbuf_tensor("ra_tri", [128, 128], F32))
    onesf = e(nc.sbuf_tensor("ra_onesf", [1, 128], F32))
    onescb = e(nc.sbuf_tensor("ra_onescb", [128, 1], BF16))
    gt = e(nc.sbuf_tensor("ra_gt", [1, 8], F32))
    lngb = e(nc.sbuf_tensor("ra_lngb", [128, 1], F32))
    m0 = e(nc.sbuf_tensor("ra_m0", [128, 128], F32))
    n20 = e(nc.sbuf_tensor("ra_n20", [128, 1], F32))
    rowr = e(nc.sbuf_tensor("ra_rowr", [1, 512], F32))
    rbc = e(nc.sbuf_tensor("ra_rbc", [128, 512], F32))
    ost = [e(nc.sbuf_tensor(f"ra_o{i}", [128, 512], BF16)) for i in range(4)]
    pms = [e(nc.psum_tensor(f"ra_p{i}", [128, 512], F32)) for i in range(8)]
    ptr = None

    ldc = p.sem("ra_ldc")
    sem_pe = p.sem("ra_pe")
    sem_ev = p.sem("ra_ev")
    sem_d = p.sem("ra_d")
    sem_a = p.sem("ra_a")
    sem_pl = p.sem("ra_pl")
    sem_st = p.sem("st")
    sem_mk = p.sem("ra_mk")

    for buf, src in ((diffT, "diffT"), (triT, "triT"),
                     (onesf, "ones_f"), (onescb, "onescol_b")):
        p.dma("sync", ldc, lambda s, buf=buf, src=src: s.dma_start(out=buf[:], in_=io[src]))
    p.dma("sync", ldc, lambda s: s.dma_start(
        out=n20[:], in_=io["neg20"].partition_broadcast(128)))
    cg = p.dma("sync", ldc, lambda s: s.dma_start(out=gt[0:1, 0:1], in_=io["gamma"]))

    p.wait("scalar", ldc, cg)
    ca = p.op("scalar", lambda s: s.activation(gt[0:1, 1:2], gt[0:1, 0:1], AF.Ln),
              inc=sem_a, small=True)
    p.wait("sync", sem_a, ca)
    cs1 = p.dma("sync", sem_mk, lambda s: s.dma_start(out=io["sc1"][0:1, 0:1],
                                                      in_=gt[0:1, 1:2]))
    p.wait("sync", sem_mk, cs1)
    cs2 = p.dma("sync", sem_mk, lambda s: s.dma_start(
        out=lngb[:], in_=io["sc1"][0:1, 0:1].partition_broadcast(128)))
    p.wait("vector", sem_mk, cs2)
    p.wait("vector", ldc, p.counts[ldc])
    cd = p.op("vector", lambda v: v.tensor_scalar(m0[:], diffT[:], lngb[:, 0:1],
                                                  None, op0=ALU.mult),
              inc=sem_d, small=True)
    p.wait("scalar", sem_d, cd)
    ca = p.op("scalar", lambda s: s.activation(m0[:], m0[:], AF.Exp),
              inc=sem_a, small=True)
    p.wait("vector", sem_a, ca)
    cm = p.op("vector", lambda v: v.tensor_mul(maskT[:], m0[:], triT[:]),
              inc=sem_d, small=True)
    mask_ready = cm

    qlas = [p.sem("ra_qla"), p.sem("ra_qlb")]
    klas = [p.sem("ra_kla"), p.sem("ra_klb")]
    vlas = [p.sem("ra_vla"), p.sem("ra_vlb")]
    kf_r = io["kf"].rearrange("(a p) s -> p a s", p=128)
    pe_group_end = {}
    pe_kc = {}
    pe_pv_kc = {}
    ev_group_end = {}
    dve_pv_end = {}
    exp_count = {}
    ev_of_bank = {}
    oslots = StSlots(p, "ra_o", 4)
    waited_mask = False
    for g in range(NG):
        qla = qlas[g % 2]
        qf = qg[g % 2]
        if g >= 2:
            p.wait("sync", sem_pe, pe_group_end[g - 2])
        if g > 0:
            p.wait("tensor", sem_d, dve_pv_end[g - 1])
        for dc in range(8):
            p.dma("sync", qla, lambda s, g=g, dc=dc, qf=qf: s.dma_start(
                out=qf[dc][:], in_=io["qf"][dc * 128:(dc + 1) * 128,
                                            g * 512:(g + 1) * 512]))
        qldv = p.counts[qla]
        p.wait("tensor", qla, qldv)
        if not waited_mask:
            p.wait("gpsimd", sem_d, mask_ready)
            waited_mask = True
        if g > 0:
            p.wait("scalar", sem_pe, pe_group_end[g - 1])
        cexp = {}
        for kc in range(32):
            kla = klas[kc % 2]
            kb = kblk[kc % 2]
            guard = pe_kc.get((g, kc - 2)) or (pe_group_end.get(g - 1) if kc < 2 else None)
            if guard is not None:
                p.wait("sync", sem_pe, guard)
            p.dma("sync", kla, lambda s, kc=kc, kb=kb: s.dma_start(
                out=kb[:].rearrange("p (a f) -> p a f", f=128),
                in_=kf_r[:, :, kc * 128:(kc + 1) * 128]))
            p.wait("tensor", kla, p.counts[kla])
            b2 = pms[kc % 2]
            if (kc % 2) in ev_of_bank:
                p.wait("tensor", sem_a, ev_of_bank[kc % 2])
            cpe = mm_acc(p, b2[:],
                         [(kb[:, dc * 128:(dc + 1) * 128], qf[dc][:])
                          for dc in range(8)], sem_pe)
            pe_kc[(g, kc)] = cpe
            p.wait("scalar", sem_pe, cpe)
            if 4 * g <= kc < 4 * g + 4:
                tt = kc - 4 * g
                sdj = kc % 2
                cev = p.op("scalar", lambda s, b2=b2, sdj=sdj: s.activation(
                    Sd[sdj][:], b2[:], AF.Copy, scale=1.0 / 32.0), inc=sem_a)
                ev_of_bank[kc % 2] = cev
                p.wait("gpsimd", sem_a, cev)
                cpl = p.op("gpsimd", lambda gp, tt=tt, sdj=sdj: gp.tensor_mul(
                    Sd[sdj][:, tt * 128:(tt + 1) * 128],
                    Sd[sdj][:, tt * 128:(tt + 1) * 128],
                    maskT[:]), inc=sem_pl, small=True)
                p.wait("scalar", sem_pl, cpl)
                cexp[kc] = p.op("scalar", lambda s, kc=kc, sdj=sdj: s.activation(
                    Pb[:, kc * 512:(kc + 1) * 512], Sd[sdj][:], AF.Exp,
                    bias=n20[:, 0:1]), inc=sem_a)
            else:
                cexp[kc] = p.op("scalar", lambda s, kc=kc, b2=b2: s.activation(
                    Pb[:, kc * 512:(kc + 1) * 512], b2[:], AF.Exp,
                    bias=n20[:, 0:1], scale=1.0 / 32.0), inc=sem_a)
                ev_of_bank[kc % 2] = cexp[kc]
        exp_count[g] = cexp[31]
        for kc in range(32):
            p.wait("tensor", sem_a, cexp[kc])
            p.op("tensor", lambda t, kc=kc: t.matmul(
                pms[2][0:1, :], onescb[:], Pb[:, kc * 512:(kc + 1) * 512],
                start=(kc == 0), stop=(kc == 31)),
                inc=(sem_pe if kc == 31 else None))
        cpe = p.counts[sem_pe]
        p.wait("vector", sem_pe, cpe)
        cd = p.op("vector", lambda v: v.reciprocal(rowr[0:1, :], pms[2][0:1, :]),
                  inc=sem_d, small=True)
        p.wait("tensor", sem_d, cd)
        cpe = p.op("tensor", lambda t: t.matmul(pms[3][:], onesf[:], rowr[0:1, :],
                                                start=True, stop=True), inc=sem_pe)
        p.wait("scalar", sem_pe, cpe)
        crbc = p.op("scalar", lambda s: s.activation(rbc[:], pms[3][:], AF.Copy),
                    inc=sem_a)
        p.wait("tensor", sem_a, exp_count[g])
        p.wait("tensor", sem_d, cd)
        p.wait("tensor", sem_a, crbc)
        for kc in range(32):
            vla = vlas[kc % 2]
            vb = vblk[kc % 2]
            guard = pe_pv_kc.get((g, kc - 2)) or (pe_group_end.get(g - 1) if kc < 2 else None)
            if guard is not None:
                p.wait("sync", sem_pe, guard)
            row = kc * 128
            p.dma("sync", vla, lambda s, row=row, vb=vb: s.dma_start(
                out=vb[:], in_=io["vt"][row:row + 128, :]))
            p.wait("tensor", vla, p.counts[vla])
            for dc in range(8):
                p.op("tensor", lambda t, kc=kc, dc=dc, vb=vb: t.matmul(
                    pms[dc][:], vb[:, dc * 128:(dc + 1) * 128],
                    Pb[:, kc * 512:(kc + 1) * 512],
                    start=(kc == 0), stop=(kc == 31)),
                    inc=(sem_pe if dc == 7 else None))
            pe_pv_kc[(g, kc)] = p.counts[sem_pe]
        cpe = p.counts[sem_pe]
        for dc in range(8):
            sj = dc % 4
            p.wait("vector", sem_pe, cpe)
            oslots.guard("vector", sj)
            cev = p.op("vector", lambda v, dc=dc, sj=sj: v.tensor_tensor(
                ost[sj][:], pms[dc][:], rbc[:], op=ALU.mult), inc=sem_d)
            p.wait("sync", sem_d, cev)
            oslots.store(sj, lambda s, dc=dc, g=g, sj=sj: s.dma_start(
                out=io["of"][dc * 128:(dc + 1) * 128, g * 512:(g + 1) * 512],
                in_=ost[sj][:]))
        dve_pv_end[g] = p.counts[sem_d]
        pe_group_end[g] = p.counts[sem_pe]
    p.barrier()


phase_rattn.reads = ("qf", "kf", "vt", "gamma", "diffT", "triT",
                     "ones_f", "onescol_b", "sc1", "neg20")
phase_rattn.writes = ("of", "sc1")



def emit_proj(p, io, cst, scope, tag, w_name, b_name, dst, src="of",
              add_src=None):
    nc = p.nc
    e = scope.enter_context
    wsb = [e(nc.sbuf_tensor(f"{tag}_w{dc}", [128, D], BF16)) for dc in range(8)]
    bpb = e(nc.sbuf_tensor(f"{tag}_b", [128, D], F32))
    ofm = [[e(nc.sbuf_tensor(f"{tag}_o{pr}{dc}", [128, 512], BF16)) for dc in range(8)]
           for pr in range(2)]
    stage = [e(nc.sbuf_tensor(f"{tag}_s{i}", [128, 512], F32)) for i in range(4)]
    pms = [e(nc.psum_tensor(f"{tag}_p{i}", [128, 512], F32)) for i in range(4)]
    banks = Banks(p, pms)

    ldw = p.sem("ldw")
    ldwg = p.sem("ldwg")
    for dc in range(8):
        wldg = p.dma("gpsimd", ldwg, lambda g, dc=dc: g.dma_start(
            out=wsb[dc][:], in_=io[w_name][dc * 128:(dc + 1) * 128, :]))
    wldv = p.dma("sync", ldw, lambda s: s.dma_start(
        out=bpb[:], in_=io[b_name].partition_broadcast(128)))

    sem_pe = p.sem(tag + "_pe")
    sem_ev = p.sem(tag + "_ev")
    las = [p.sem(tag + "_la"), p.sem(tag + "_lb")]
    slots = StSlots(p, tag, 4)
    pe_group_end = {}
    waited = False
    for g in range(NG):
        la = las[g % 2]
        of = ofm[g % 2]
        if g >= 2:
            p.wait("sync", sem_pe, pe_group_end[g - 2])
        for dc in range(8):
            p.dma("sync", la, lambda s, g=g, dc=dc, of=of: s.dma_start(
                out=of[dc][:], in_=io[src][dc * 128:(dc + 1) * 128,
                                           g * 512:(g + 1) * 512]))
        ldv = p.counts[la]
        p.wait("tensor", la, ldv)
        if not waited:
            p.wait("tensor", ldw, wldv)
            p.wait("tensor", ldwg, wldg)
            p.wait("vector", ldw, wldv)
            waited = True
        si = 0
        for tt in range(4):
            for half in range(2):
                b = banks.next()
                cpe = mm_acc(p, b[0][:],
                             [(of[dc][:, tt * 128:(tt + 1) * 128],
                               wsb[dc][:, half * 512:(half + 1) * 512])
                              for dc in range(8)], sem_pe)
                sj = si % 4
                si += 1
                slots.guard("vector", sj)
                p.wait("vector", sem_pe, cpe)
                cev = p.op("vector", lambda v, sj=sj, b=b, half=half: v.tensor_tensor(
                    stage[sj][:], b[0][:], bpb[:, half * 512:(half + 1) * 512],
                    op=ALU.add), inc=sem_ev)
                Banks.release(b, sem_ev, cev)
                row = (g * 4 + tt) * 128
                p.wait("sync", sem_ev, cev)
                slots.store(sj, lambda s, row=row, half=half, sj=sj: s.dma_start(
                    out=io[dst][row:row + 128, half * 512:(half + 1) * 512],
                    in_=stage[sj][:]))
        pe_group_end[g] = p.counts[sem_pe]
    p.barrier()


def phase_rproj(p, io, cst, scope):
    emit_proj(p, io, cst, scope, "rproj", "Wproj", "bproj", "attn_br")


phase_rproj.reads = ("of", "Wproj", "bproj")
phase_rproj.writes = ("attn_br",)


def phase_mproj(p, io, cst, scope):
    emit_proj(p, io, cst, scope, "mproj", "out_w", "out_b", "ymha")


phase_mproj.reads = ("of", "out_w", "out_b")
phase_mproj.writes = ("ymha",)



def ln_tiles(p, tag, st, cb, sqd, xm, tiles_in, out_tiles, width, sem_d, sem_a):
    n = len(tiles_in)
    cnm, cxm, csq = {}, {}, {}
    for j in range(n):
        cnm[j] = p.op("scalar",
                      lambda s, j=j: s.activation(sqd[j][:, 0:width],
                                                  tiles_in[j][:, 0:width],
                                                  AF.Copy, scale=-1.0 / width,
                                                  accum_out=st[:, cb + j:cb + j + 1]),
                      inc=sem_a, small=True)
    for j in range(n):
        p.wait("vector", sem_a, cnm[j])
        cxm[j] = p.op("vector",
                      lambda v, j=j: v.tensor_scalar(xm[j][:, 0:width],
                                                     tiles_in[j][:, 0:width],
                                                     st[:, cb + j:cb + j + 1],
                                                     None, op0=ALU.add),
                      inc=sem_d, small=True)
    for j in range(n):
        p.wait("scalar", sem_d, cxm[j])
        csq[j] = p.op("scalar",
                      lambda s, j=j: s.activation(sqd[j][:, 0:width], xm[j][:, 0:width],
                                                  AF.Square,
                                                  accum_out=st[:, cb + 8 + j:cb + 9 + j]),
                      inc=sem_a, small=True)
    p.wait("vector", sem_a, csq[n - 1])
    cv = p.op("vector",
              lambda v: v.tensor_scalar(st[:, cb + 16:cb + 16 + n],
                                        st[:, cb + 8:cb + 8 + n],
                                        1.0 / width, EPS, op0=ALU.mult, op1=ALU.add),
              inc=sem_d, small=True)
    p.wait("scalar", sem_d, cv)
    cl = p.op("scalar", lambda s: s.activation(st[:, cb + 20:cb + 20 + n],
                                               st[:, cb + 16:cb + 16 + n], AF.Ln),
              inc=sem_a, small=True)
    p.wait("vector", sem_a, cl)
    cln = p.op("vector",
               lambda v: v.tensor_scalar(st[:, cb + 16:cb + 16 + n],
                                         st[:, cb + 20:cb + 20 + n],
                                         -0.5, None, op0=ALU.mult),
               inc=sem_d, small=True)
    p.wait("scalar", sem_d, cln)
    ca = p.op("scalar", lambda s: s.activation(st[:, cb + 24:cb + 24 + n],
                                               st[:, cb + 16:cb + 16 + n], AF.Exp),
              inc=sem_a, small=True)
    cout = {}
    for j in range(n):
        p.wait("vector", sem_a, ca)
        cout[j] = p.op("vector",
                       lambda v, j=j: v.tensor_scalar(out_tiles[j][:, 0:width],
                                                      xm[j][:, 0:width],
                                                      st[:, cb + 24 + j:cb + 25 + j],
                                                      None, op0=ALU.mult),
                       inc=sem_d, small=True)
    return cout



def phase_norm1(p, io, cst, scope):
    nc = p.nc
    e = scope.enter_context
    st = e(nc.sbuf_tensor("n1_stat", [128, 32], F32))
    xt = [e(nc.sbuf_tensor(f"n1_xt{j}", [128, D], F32)) for j in range(4)]
    yt = [e(nc.sbuf_tensor(f"n1_yt{j}", [128, D], F32)) for j in range(4)]
    xm = [e(nc.sbuf_tensor(f"n1_xm{j}", [128, D], F32)) for j in range(4)]
    xnb = [e(nc.sbuf_tensor(f"n1_xnb{j}", [128, D], F32)) for j in range(4)]
    sqd = [e(nc.sbuf_tensor(f"n1_sqd{j}", [128, D], BF16)) for j in range(4)]
    gb = e(nc.sbuf_tensor("n1_g", [128, D], F32))
    bb = e(nc.sbuf_tensor("n1_b", [128, D], F32))
    ldw = p.sem("ldw")
    p.dma("sync", ldw, lambda s: s.dma_start(
        out=gb[:], in_=io["norm1_w"].partition_broadcast(128)))
    wv = p.dma("sync", ldw, lambda s: s.dma_start(
        out=bb[:], in_=io["norm1_b"].partition_broadcast(128)))
    p.wait("vector", ldw, wv)
    sem_ld = p.sem("n1_ld")
    sem_d = p.sem("n1_d")
    sem_a = p.sem("n1_a")
    sem_st = p.sem("st")
    for g in range(NG):
        for j in range(4):
            tt = g * 4 + j
            p.dma("sync", sem_ld, lambda s, tt=tt, j=j: s.dma_start(
                out=xt[j][:], in_=io["xn_f32"][tt * 128:(tt + 1) * 128, :]))
            c = p.dma("sync", sem_ld, lambda s, tt=tt, j=j: s.dma_start(
                out=yt[j][:], in_=io["ymha"][tt * 128:(tt + 1) * 128, :]))
        p.wait("vector", sem_ld, c)
        p.wait("scalar", sem_ld, c)
        cs = {}
        for j in range(4):
            cs[j] = p.op("vector", lambda v, j=j: v.tensor_add(xt[j][:], xt[j][:],
                                                               yt[j][:]),
                         inc=sem_d, small=True)
            p.wait("scalar", sem_d, cs[j])
        cout = ln_tiles(p, "n1", st, 0, sqd, xm, xt, xnb, D, sem_d, sem_a)
        for j in range(4):
            tt = g * 4 + j
            p.wait("vector", sem_d, cout[j])
            cm_ = p.op("vector", lambda v, j=j: v.tensor_mul(xnb[j][:], xnb[j][:], gb[:]),
                       inc=sem_d)
            p.wait("vector", sem_d, cm_)
            cw = p.op("vector", lambda v, j=j: v.tensor_add(xnb[j][:], xnb[j][:], bb[:]),
                      inc=sem_d, small=True)
            p.wait("sync", sem_d, cw)
            p.dma("sync", sem_st, lambda s, tt=tt, j=j: s.dma_start(
                out=io["x1_f32"][tt * 128:(tt + 1) * 128, :], in_=xnb[j][:]))
            p.wait("gpsimd", sem_d, cw)
            p.dma("gpsimd", p.sem("stg"), lambda gp, tt=tt, j=j: gp.dma_start(
                out=io["x1_bf"][tt * 128:(tt + 1) * 128, :], in_=xnb[j][:]))
        p.barrier()


phase_norm1.reads = ("xn_f32", "ymha", "norm1_w", "norm1_b")
phase_norm1.writes = ("x1_f32", "x1_bf")



def emit_mha(p, io, cst, scope, tag, causal, sc1col):
    nc = p.nc
    e = scope.enter_context
    Pb2 = [e(nc.sbuf_tensor(f"{tag}_P{i}", [128, 32 * 512], BF16)) for i in range(2)]
    khb = [e(nc.sbuf_tensor(f"{tag}_kh{i}", [128, S], BF16)) for i in range(2)]
    vhb = [e(nc.sbuf_tensor(f"{tag}_vh{i}", [128, 32 * 65], BF16)) for i in range(2)]
    qh = [e(nc.sbuf_tensor(f"{tag}_qh{i}", [128, 512], BF16)) for i in range(2)]
    onesf = e(nc.sbuf_tensor(tag + "_onesf", [1, 128], F32))
    onescb = e(nc.sbuf_tensor(tag + "_onescb", [128, 1], BF16))
    trib = e(nc.sbuf_tensor(tag + "_trib", [128, 128], BF16))
    n20 = e(nc.sbuf_tensor(tag + "_n20", [128, 1], F32))
    rowr = e(nc.sbuf_tensor(tag + "_rowr", [1, 512], F32))
    rbc = e(nc.sbuf_tensor(tag + "_rbc", [64, 512], F32))
    ost = [e(nc.sbuf_tensor(f"{tag}_o{i}", [64, 512], BF16)) for i in range(2)]
    pms = [e(nc.psum_tensor(f"{tag}_p{i}", [128, 512], F32)) for i in range(5)]

    ldc = p.sem(tag + "_ldc")
    sem_pe = p.sem(tag + "_pe")
    sem_d = p.sem(tag + "_d")
    sem_a = p.sem(tag + "_a")
    sem_pl = p.sem(tag + "_pl")
    khla = [p.sem(tag + "_khl"), p.sem(tag + "_khm")]
    vhla = [p.sem(tag + "_vhl"), p.sem(tag + "_vhm")]
    qla = [p.sem(tag + "_qla"), p.sem(tag + "_qlb")]
    oslots = StSlots(p, tag + "o", 2)

    for buf, srcn in ((onesf, "ones_f"), (onescb, "onescol_b"), (trib, "triT_b")):
        cld = p.dma("sync", ldc, lambda s, buf=buf, srcn=srcn: s.dma_start(
            out=buf[:], in_=io[srcn]))
    cld = p.dma("sync", ldc, lambda s: s.dma_start(
        out=n20[:], in_=io["neg20"].partition_broadcast(128)))

    c_init = None
    for i in range(2):
        p.op("gpsimd", lambda gp, i=i: gp.memset(khb[i][64:128, :], 0.0),
             inc=sem_pl, small=True)
        p.op("gpsimd", lambda gp, i=i: gp.memset(qh[i][64:128, :], 0.0),
             inc=sem_pl, small=True)
        c_init = p.op("gpsimd", lambda gp, i=i: gp.memset(
            vhb[i][:].rearrange("pp (kc c) -> pp kc c", c=65)[:, :, 64:65], 1.0),
            inc=sem_pl, small=True)

    vt_r = io["vt"].rearrange("(kc pp) c -> pp kc c", pp=128)
    iters = [(h, g) for h in range(16) for g in range(NG)]
    s_done = {}
    pv_done = {}
    exp_done = {}
    exp_of_bank = {}
    dve_pc = {}
    kv_loaded = {}
    state = {}
    qld = {}
    waited_c = False

    def load_q(j):
        if j >= 2:
            p.wait("sync", sem_pe, s_done[j - 2])
        jh, jg = iters[j]
        qld[j] = p.dma("sync", qla[j % 2], lambda s, jh=jh, jg=jg, j=j: s.dma_start(
            out=qh[j % 2][0:64, :],
            in_=io["qf"][jh * 64:(jh + 1) * 64, jg * 512:(jg + 1) * 512]))

    def load_kv(h):
        hp = h % 2
        if h >= 2:
            last_idx = (h - 2) * NG + NG - 1
            p.wait("sync", sem_pe, s_done[last_idx])
            p.wait("sync", sem_pe, pv_done[last_idx])
        ck = p.dma("sync", khla[hp], lambda s, h=h, hp=hp: s.dma_start(
            out=khb[hp][0:64, :], in_=io["kf"][h * 64:(h + 1) * 64, :]))
        cv_ = p.dma("sync", vhla[hp], lambda s, h=h, hp=hp: s.dma_start(
            out=vhb[hp][:].rearrange("pp (kc c) -> pp kc c", c=65)[:, :, 0:64],
            in_=vt_r[:, :, h * 64:(h + 1) * 64]))
        kv_loaded[h] = (ck, cv_)

    def emit_iter(idx):
        nonlocal waited_c
        h, g = iters[idx]
        pi = idx % 2
        kmax = 4 * g + 4 if causal else 32
        kh = khb[h % 2]
        qb = qh[pi]
        if idx == 0:
            p.wait("tensor", sem_pl, c_init)
            load_kv(0)
            load_kv(1)
            load_q(0)
            load_q(1)
        if g == 1 and 1 <= h < 15:
            load_kv(h + 1)
        p.wait("tensor", qla[pi], qld[idx])
        p.wait("tensor", khla[h % 2], kv_loaded[h][0])
        if not waited_c:
            p.wait("scalar", ldc, cld)
            p.wait("gpsimd", ldc, cld)
            waited_c = True
        if idx >= 2:
            p.wait("scalar", sem_pe, pv_done[idx - 2])
        Pb = Pb2[pi]
        cexp = {}
        cmsk = {}
        prev = state.pop(idx - 1, None)
        if prev is not None:
            pkmax, pcexp, pcmsk, ph, pg, ppi = prev
            pPb = Pb2[ppi]
            pvh = vhb[ph % 2]
            pvb = pms[2 + (idx - 1) % 2]
            p.wait("tensor", vhla[ph % 2], kv_loaded[ph][1])
            if idx > 2:
                p.wait("tensor", sem_d, dve_pc[idx - 3])
        else:
            pkmax = 0
        for kc in range(max(kmax, pkmax)):
            if kc < kmax:
                b = pms[kc % 2]
                if (kc % 2) in exp_of_bank:
                    p.wait("tensor", sem_a, exp_of_bank[kc % 2])
                cpe = p.op("tensor", lambda t, kc=kc, b=b, qb=qb, kh=kh: t.matmul(
                    b[:], kh[:, kc * 128:(kc + 1) * 128], qb[:],
                    start=True, stop=True), inc=sem_pe)
                p.wait("scalar", sem_pe, cpe)
                cexp[kc] = p.op("scalar", lambda s, kc=kc, b=b, Pb=Pb: s.activation(
                    Pb[:, kc * 512:(kc + 1) * 512], b[:], AF.Exp,
                    bias=n20[:, 0:1], scale=0.125), inc=sem_a)
                exp_of_bank[kc % 2] = cexp[kc]
                if causal and kc >= 4 * g:
                    tt = kc - 4 * g
                    p.wait("gpsimd", sem_a, cexp[kc])
                    cm = p.op("gpsimd", lambda gp, kc=kc, tt=tt, Pb=Pb: gp.tensor_mul(
                        Pb[:, kc * 512 + tt * 128: kc * 512 + (tt + 1) * 128],
                        Pb[:, kc * 512 + tt * 128: kc * 512 + (tt + 1) * 128],
                        trib[:]), inc=sem_pl, small=True)
                    for tt2 in range(tt):
                        cm = p.op("gpsimd", lambda gp, kc=kc, tt2=tt2, Pb=Pb: gp.memset(
                            Pb[:, kc * 512 + tt2 * 128: kc * 512 + (tt2 + 1) * 128],
                            0.0), inc=sem_pl, small=True)
                    cmsk[kc] = cm
            if prev is not None and kc < pkmax:
                p.wait("tensor", sem_a, pcexp[kc])
                if kc in pcmsk:
                    p.wait("tensor", sem_pl, pcmsk[kc])
                p.op("tensor", lambda t, kc=kc, last=(kc == pkmax - 1), vh=pvh,
                     Pb=pPb, pvb=pvb: t.matmul(
                    pvb[0:65, :], vh[:, kc * 65:kc * 65 + 65],
                    Pb[:, kc * 512:(kc + 1) * 512],
                    start=(kc == 0), stop=last),
                    inc=(sem_pe if kc == pkmax - 1 else None))
        s_done[idx] = p.counts[sem_pe]
        state[idx] = (kmax, cexp, cmsk, h, g, pi)
        if idx + 1 < len(iters) and idx >= 1:
            load_q(idx + 1)
        if prev is not None:
            emit_tail(idx - 1, prev)

    def emit_tail(idx, st):
        kmax, cexp, cmsk, h, g, pi = st
        pvb = pms[2 + idx % 2]
        pv_done[idx] = p.counts[sem_pe]
        p.wait("vector", sem_pe, pv_done[idx])
        cd = p.op("vector", lambda v, pvb=pvb: v.reciprocal(
            rowr[0:1, :], pvb[64:65, :]), inc=sem_d, small=True)
        p.wait("tensor", sem_d, cd)
        cpe = p.op("tensor", lambda t: t.matmul(pms[4][0:64, :],
                                                onesf[0:1, 0:64], rowr[0:1, :],
                                                start=True, stop=True),
                   inc=sem_pe)
        p.wait("scalar", sem_pe, cpe)
        crbc = p.op("scalar", lambda s: s.activation(rbc[:], pms[4][0:64, :],
                                                     AF.Copy), inc=sem_a)
        sj = idx % 2
        oslots.guard("vector", sj)
        p.wait("vector", sem_a, crbc)
        cev = p.op("vector", lambda v, sj=sj, pvb=pvb: v.tensor_tensor(
            ost[sj][:], pvb[0:64, :], rbc[:], op=ALU.mult), inc=sem_d)
        dve_pc[idx] = cev
        p.wait("sync", sem_d, cev)
        oslots.store(sj, lambda s, h=h, g=g, sj=sj: s.dma_start(
            out=io["of"][h * 64:(h + 1) * 64, g * 512:(g + 1) * 512],
            in_=ost[sj][:]))

    for idx in range(len(iters)):
        emit_iter(idx)
    last = len(iters) - 1
    st = state.pop(last)
    pkmax, pcexp, pcmsk, ph, pg, ppi = st
    p.wait("tensor", vhla[ph % 2], kv_loaded[ph][1])
    p.wait("tensor", sem_d, dve_pc[last - 1])
    pvb_l = pms[2 + last % 2]
    for kc in range(pkmax):
        p.wait("tensor", sem_a, pcexp[kc])
        if kc in pcmsk:
            p.wait("tensor", sem_pl, pcmsk[kc])
        p.op("tensor", lambda t, kc=kc, last_=(kc == pkmax - 1): t.matmul(
            pvb_l[0:65, :], vhb[ph % 2][:, kc * 65:kc * 65 + 65],
            Pb2[ppi][:, kc * 512:(kc + 1) * 512],
            start=(kc == 0), stop=last_),
            inc=(sem_pe if kc == pkmax - 1 else None))
    emit_tail(last, st)
    p.barrier()


def phase_mattn(p, io, cst, scope):
    emit_mha(p, io, cst, scope, "ma", causal=True, sc1col=2)


phase_mattn.reads = ("qf", "kf", "vt", "ones_f", "onescol_b", "triT_b", "neg20")
phase_mattn.writes = ("of",)


def phase_l1attn(p, io, cst, scope):
    emit_mha(p, io, cst, scope, "la", causal=False, sc1col=3)


phase_l1attn.reads = ("qf", "kf", "vt", "ones_f", "onescol_b", "triT_b", "neg20")
phase_l1attn.writes = ("of",)



def phase_moe(p, io, cst, scope):
    nc = p.nc
    e = scope.enter_context
    xfm = [[e(nc.sbuf_tensor(f"moe_x{pr}{dc}", [128, 512], BF16)) for dc in range(8)]
           for pr in range(2)]
    x1t = [e(nc.sbuf_tensor(f"moe_x1t{j}", [128, D], F32)) for j in range(4)]
    yb = [e(nc.sbuf_tensor(f"moe_y{j}", [128, D], F32)) for j in range(4)]
    w1s = [[e(nc.sbuf_tensor(f"moe_w1{sl}{dc}", [128, FF], BF16)) for dc in range(8)]
           for sl in range(2)]
    w2s = [e(nc.sbuf_tensor(f"moe_w2{hc}", [128, D], BF16)) for hc in range(16)]
    b1c = [e(nc.sbuf_tensor(f"moe_b1c{sl}", [128, 16], F32)) for sl in range(2)]
    hsb = [e(nc.sbuf_tensor(f"moe_h{hc}", [128, 512], BF16)) for hc in range(16)]
    cent = e(nc.sbuf_tensor("moe_cent", [8, D], F32))
    centT = e(nc.sbuf_tensor("moe_centT", [128, 64], BF16))
    bias9 = e(nc.sbuf_tensor("moe_b9", [9, D], F32))
    rbb = e(nc.sbuf_tensor("moe_rbb", [128, 8], F32))
    gst = e(nc.sbuf_tensor("moe_gst", [128, 76], F32))
    gating = e(nc.sbuf_tensor("moe_gat", [128, 36], F32))
    gT = e(nc.sbuf_tensor("moe_gT", [9, 512], F32))
    identf = e(nc.sbuf_tensor("moe_idf", [128, 128], F32))
    pms = [e(nc.psum_tensor(f"moe_p{i}", [128, 512], F32)) for i in range(7)]

    ldc = p.sem("moe_ldc")
    ldg = p.sem("moe_ldg")
    sem_pe = p.sem("moe_pe")
    sem_d = p.sem("moe_d")
    sem_a = p.sem("moe_a")
    sem_y = p.sem("moe_y")
    slots = StSlots(p, "moe", 4)
    las = [p.sem("moe_la"), p.sem("moe_lb")]
    lx = p.sem("moe_lx")

    g3 = lambda ap, w=8: ap.rearrange("p (a b) -> p a b", b=w)
    gcol3 = lambda ap: ap.rearrange("p (a o) -> p a o", o=1)

    p.dma("sync", ldc, lambda s: s.dma_start(out=identf[:], in_=io["ident_f"]))
    p.dma("sync", ldc, lambda s: s.dma_start(
        out=rbb[:], in_=io["route_bias"].partition_broadcast(128)))
    p.dma("sync", ldc, lambda s: s.dma_start(out=bias9[0:1, :], in_=io["sb2"]))
    p.dma("sync", ldc, lambda s: s.dma_start(out=bias9[1:9, :], in_=io["rb2"]))
    cld = p.dma("sync", ldc, lambda s: s.dma_start(out=cent[:], in_=io["centroid"]))
    p.wait("tensor", ldc, cld)
    for dc in range(8):
        p.op("tensor", lambda t, dc=dc: t.transpose(
            pms[6][0:128, dc * 8:(dc + 1) * 8], cent[:, dc * 128:(dc + 1) * 128],
            identf[0:8, 0:8]),
            inc=(sem_pe if dc == 7 else None))
    cpe = p.counts[sem_pe]
    p.wait("scalar", sem_pe, cpe)
    ccT = p.op("scalar", lambda s: s.activation(centT[:], pms[6][:, 0:64], AF.Copy),
               inc=sem_a)

    def w1_src(ei):
        return io["sw1"] if ei == 0 else io["rw1"][ei - 1:ei, :, :].squeeze(0)

    def w2_src(ei):
        return io["sw2"] if ei == 0 else io["rw2"][ei - 1:ei, :, :].squeeze(0)

    def b1_src(ei):
        return io["sb1"] if ei == 0 else io["rb1"][ei - 1:ei, :]

    wload_w1 = {}
    wload_b1 = {}
    wload_w2 = {}
    pe_w1_free = {}
    act_b1_free = {}
    pe_w2_free = [None]
    w1sem = [p.sem("moe_w1a"), p.sem("moe_w1b")]
    b1sem = [p.sem("moe_b1a"), p.sem("moe_b1b")]
    w2sem = p.sem("moe_w2s")

    def load_w1(eseq):
        ei = eseq % 9
        sl = eseq % 2
        if pe_w1_free.get(sl) is not None:
            p.wait("gpsimd", sem_pe, pe_w1_free[sl])
        if act_b1_free.get(sl) is not None:
            p.wait("sync", sem_a, act_b1_free[sl])
        c = None
        for dc in range(8):
            c = p.dma("gpsimd", w1sem[sl], lambda g_, ei=ei, sl=sl, dc=dc: g_.dma_start(
                out=w1s[sl][dc][:], in_=w1_src(ei)[dc * 128:(dc + 1) * 128, :]))
        wload_w1[eseq] = c
        wload_b1[eseq] = load_bias_cols(p, b1sem[sl], b1c[sl], b1_src(ei), 16)

    def load_w2(eseq):
        ei = eseq % 9
        if pe_w2_free[0] is not None:
            p.wait("gpsimd", sem_pe, pe_w2_free[0])
        c = None
        for hc in range(16):
            c = p.dma("gpsimd", w2sem, lambda g_, ei=ei, hc=hc: g_.dma_start(
                out=w2s[hc][:], in_=w2_src(ei)[hc * 128:(hc + 1) * 128, :]))
        wload_w2[eseq] = c

    load_w1(0)
    load_w2(0)
    y_count = {}
    pe_group_end = {}
    gat_ready = {}
    hev = {}
    opev = {}
    for g in range(NG):
        la = las[g % 2]
        xf = xfm[g % 2]
        if g >= 2:
            p.wait("sync", sem_pe, pe_group_end[g - 2])
        for dc in range(8):
            p.dma("sync", la, lambda s, g=g, dc=dc, xf=xf: s.dma_start(
                out=xf[dc][:], in_=io["x1_bf"][g * 512:(g + 1) * 512,
                                               dc * 128:(dc + 1) * 128],
                transpose=True))
        ldv = p.counts[la]
        if g > 0:
            p.wait("sync", sem_y, y_count[(g - 1, 3, 1)])
        for j in range(4):
            tt = g * 4 + j
            p.dma("sync", lx, lambda s, tt=tt, j=j: s.dma_start(
                out=x1t[j][:], in_=io["x1_f32"][tt * 128:(tt + 1) * 128, :]))
        clx = p.counts[lx]
        p.wait("tensor", la, ldv)
        if g == 0:
            p.wait("tensor", sem_a, ccT)
        if g > 0:
            p.wait("tensor", sem_d, gat_ready[g - 1])
        for tt in range(4):
            for dc in range(8):
                p.op("tensor", lambda t, tt=tt, dc=dc, xf=xf: t.matmul(
                    pms[5][:, tt * 8:(tt + 1) * 8],
                    xf[dc][:, tt * 128:(tt + 1) * 128], centT[:, dc * 8:(dc + 1) * 8],
                    start=(dc == 0), stop=(dc == 7)),
                    inc=(sem_pe if (tt == 3 and dc == 7) else None))
        cpe = p.counts[sem_pe]
        p.wait("scalar", sem_pe, cpe)
        css = p.op("scalar", lambda s: s.activation(gst[:, 0:32], pms[5][:, 0:32],
                                                    AF.Copy), inc=sem_a, small=True)
        p.wait("vector", sem_a, css)
        c1 = p.op("vector", lambda v: v.reduce_max(gcol3(gst[:, 32:36]),
                                                   g3(gst[:, 0:32]),
                                                   axis=AX.X, negate=True),
                  inc=sem_d, small=True)
        p.wait("scalar", sem_d, c1)
        ce = None
        for tt in range(4):
            ce = p.op("scalar", lambda s, tt=tt: s.activation(
                gst[:, tt * 8:(tt + 1) * 8], gst[:, tt * 8:(tt + 1) * 8],
                AF.Exp, bias=gst[:, 32 + tt:33 + tt]), inc=sem_a,
                small=(tt == 3))
        p.wait("vector", sem_a, ce)
        c2 = p.op("vector", lambda v: v.reduce_sum(gcol3(gst[:, 36:40]),
                                                   g3(gst[:, 0:32]), axis=AX.X),
                  inc=sem_d, small=True)
        p.wait("vector", sem_d, c2)
        c3 = p.op("vector", lambda v: v.reciprocal(gst[:, 40:44], gst[:, 36:40]),
                  inc=sem_d, small=True)
        p.wait("vector", sem_d, c3)
        csim = None
        for tt in range(4):
            csim = p.op("vector", lambda v, tt=tt: v.tensor_scalar(
                gating[:, tt * 9 + 1:tt * 9 + 9], gst[:, tt * 8:(tt + 1) * 8],
                gst[:, 40 + tt:41 + tt], None, op0=ALU.mult),
                inc=sem_d, small=(tt == 3))
        p.wait("vector", sem_d, csim)
        ct = None
        for tt in range(4):
            ct = p.op("vector", lambda v, tt=tt: v.tensor_add(
                gst[:, tt * 8:(tt + 1) * 8], gating[:, tt * 9 + 1:tt * 9 + 9],
                rbb[:]), inc=sem_d, small=(tt == 3))
        p.wait("vector", sem_d, ct)
        c4 = p.op("vector", lambda v: v.reduce_max(gcol3(gst[:, 32:36]),
                                                   g3(gst[:, 0:32]), axis=AX.X),
                  inc=sem_d, small=True)
        p.wait("vector", sem_d, c4)
        cge1 = None
        for tt in range(4):
            cge1 = p.op("vector", lambda v, tt=tt: v.tensor_scalar(
                gst[:, 44 + tt * 8:52 + tt * 8], gst[:, tt * 8:(tt + 1) * 8],
                gst[:, 32 + tt:33 + tt], None, op0=ALU.is_ge),
                inc=sem_d, small=(tt == 3))
        p.wait("vector", sem_d, cge1)
        c5 = p.op("vector", lambda v: v.scalar_tensor_tensor(
            gst[:, 0:32], gst[:, 44:76], -1e30, gst[:, 0:32],
            op0=ALU.mult, op1=ALU.add), inc=sem_d, small=True)
        p.wait("vector", sem_d, c5)
        c6 = p.op("vector", lambda v: v.reduce_max(gcol3(gst[:, 36:40]),
                                                   g3(gst[:, 0:32]), axis=AX.X),
                  inc=sem_d, small=True)
        p.wait("vector", sem_d, c6)
        cge2 = None
        for tt in range(4):
            cge2 = p.op("vector", lambda v, tt=tt: v.tensor_scalar(
                gst[:, tt * 8:(tt + 1) * 8], gst[:, tt * 8:(tt + 1) * 8],
                gst[:, 36 + tt:37 + tt], None, op0=ALU.is_ge),
                inc=sem_d, small=(tt == 3))
        p.wait("vector", sem_d, cge2)
        c7 = p.op("vector", lambda v: v.tensor_add(gst[:, 44:76], gst[:, 44:76],
                                                   gst[:, 0:32]),
                  inc=sem_d, small=True)
        p.wait("vector", sem_d, c7)
        c8 = p.op("vector", lambda v: v.tensor_mul(
            g3(gst[:, 0:32]),
            gating[:, 0:36].rearrange("p (a b) -> p a b", b=9)[:, :, 1:9],
            g3(gst[:, 44:76])), inc=sem_d, small=True)
        p.wait("vector", sem_d, c8)
        c9 = p.op("vector", lambda v: v.reduce_sum(gcol3(gst[:, 36:40]),
                                                   g3(gst[:, 0:32]), axis=AX.X),
                  inc=sem_d, small=True)
        p.wait("vector", sem_d, c9)
        c10 = p.op("vector", lambda v: v.reciprocal(gst[:, 40:44], gst[:, 36:40]),
                   inc=sem_d, small=True)
        p.wait("vector", sem_d, c10)
        cgat = None
        for tt in range(4):
            p.op("vector", lambda v, tt=tt: v.memset(gating[:, tt * 9:tt * 9 + 1], 1.0))
            cgat = p.op("vector", lambda v, tt=tt: v.tensor_scalar(
                gating[:, tt * 9 + 1:tt * 9 + 9], gst[:, tt * 8:(tt + 1) * 8],
                gst[:, 40 + tt:41 + tt], None, op0=ALU.mult),
                inc=sem_d, small=(tt == 3))
        gat_ready[g] = cgat
        p.wait("vector", sem_d, cgat)
        def expert_h(ei, g=g, xf=xf):
            eseq = g * 9 + ei
            sl = eseq % 2
            if eseq + 1 < NG * 9:
                load_w1(eseq + 1)
            p.wait("tensor", w1sem[sl], wload_w1[eseq])
            p.wait("scalar", b1sem[sl], wload_b1[eseq])
            for hc in range(16):
                b = pms[hc % 2]
                guard = hev.get(hc % 2)
                if guard is not None:
                    p.wait("tensor", sem_a, guard)
                cpe = mm_acc(p, b[:],
                             [(w1s[sl][dc][:, hc * 128:(hc + 1) * 128], xf[dc][:])
                              for dc in range(8)], sem_pe)
                p.wait("scalar", sem_pe, cpe)
                hev[hc % 2] = p.op("scalar", lambda s, hc=hc, b=b, sl=sl: s.activation(
                    hsb[hc][:], b[:], AF.Gelu, bias=b1c[sl][:, hc:hc + 1]),
                    inc=sem_a)
            return p.counts[sem_a], p.counts[sem_pe]

        def expert_out(ei, hdone, pe_h_end, g=g):
            eseq = g * 9 + ei
            sl = eseq % 2
            pe_w1_free[sl] = pe_h_end
            act_b1_free[sl] = hdone
            p.wait("tensor", w2sem, wload_w2[eseq])
            p.wait("tensor", sem_a, hdone)
            for j in range(4):
                for half in range(2):
                    b = pms[2 + (j * 2 + half) % 2]
                    guard = opev.get((j * 2 + half) % 2)
                    if guard is not None:
                        p.wait("tensor", sem_y, guard)
                    cpe = mm_acc(p, b[:],
                                 [(hsb[hc][:, j * 128:(j + 1) * 128],
                                   w2s[hc][:, half * 512:(half + 1) * 512])
                                  for hc in range(16)], sem_pe)
                    p.wait("vector", sem_pe, cpe)
                    p.wait("vector", sem_y, y_count[(g, j, half)])
                    gci = ei
                    cy = p.op("vector", lambda v, j=j, half=half, b=b,
                              gci=gci: v.scalar_tensor_tensor(
                        yb[j][:, half * 512:(half + 1) * 512], b[:],
                        gating[:, j * 9 + gci:j * 9 + gci + 1],
                        yb[j][:, half * 512:(half + 1) * 512],
                        op0=ALU.mult, op1=ALU.add),
                        inc=sem_y, small=(ei == 8))
                    y_count[(g, j, half)] = cy
                    opev[(j * 2 + half) % 2] = cy
            pe_w2_free[0] = p.counts[sem_pe]
            if eseq + 1 < NG * 9:
                load_w2(eseq + 1)

        h0, pe0 = expert_h(0)
        p.wait("tensor", sem_d, cgat)
        for tt in range(4):
            p.op("tensor", lambda t, tt=tt: t.transpose(
                pms[6][0:9, tt * 128:(tt + 1) * 128],
                gating[:, tt * 9:(tt + 1) * 9], identf[:]),
                inc=(sem_pe if tt == 3 else None))
        cpe = p.counts[sem_pe]
        p.wait("scalar", sem_pe, cpe)
        cgT = p.op("scalar", lambda s: s.activation(gT[:], pms[6][0:9, :], AF.Copy),
                   inc=sem_a)
        p.wait("tensor", sem_a, cgT)
        p.wait("tensor", ldc, p.counts[ldc])
        p.wait("vector", lx, clx)
        py_free = None
        for j in range(4):
            slots.guard("vector", j)
            for half in range(2):
                if py_free is not None:
                    p.wait("tensor", sem_y, py_free)
                cpe = p.op("tensor", lambda t, j=j, half=half: t.matmul(
                    pms[4][:], gT[:, j * 128:(j + 1) * 128],
                    bias9[:, half * 512:(half + 1) * 512],
                    start=True, stop=True), inc=sem_pe)
                p.wait("vector", sem_pe, cpe)
                cy = p.op("vector", lambda v, j=j, half=half: v.tensor_add(
                    yb[j][:, half * 512:(half + 1) * 512],
                    x1t[j][:, half * 512:(half + 1) * 512], pms[4][:]),
                    inc=sem_y)
                y_count[(g, j, half)] = cy
                py_free = cy
        expert_out(0, h0, pe0)
        for ei in range(1, 9):
            hd, ped = expert_h(ei)
            expert_out(ei, hd, ped)
        for j in range(4):
            tt = g * 4 + j
            p.wait("sync", sem_y, y_count[(g, j, 1)])
            slots.store(j, lambda s, tt=tt, j=j: s.dma_start(
                out=io["ymoe"][tt * 128:(tt + 1) * 128, :], in_=yb[j][:]))
        pe_group_end[g] = p.counts[sem_pe]
    p.barrier()


phase_moe.reads = ("x1_bf", "x1_f32", "sw1", "sb1", "sw2", "sb2", "rw1", "rb1",
                   "rw2", "rb2", "centroid", "route_bias", "ident_f")
phase_moe.writes = ("ymoe",)



def phase_l1prep(p, io, cst, scope):
    nc = p.nc
    e = scope.enter_context
    st = e(nc.sbuf_tensor("lp_stat", [128, 32], F32))
    yt = [e(nc.sbuf_tensor(f"lp_yt{j}", [128, D], F32)) for j in range(4)]
    x0t = [e(nc.sbuf_tensor(f"lp_x0{j}", [128, D], F32)) for j in range(4)]
    abt = [e(nc.sbuf_tensor(f"lp_ab{j}", [128, D], F32)) for j in range(4)]
    xm = [e(nc.sbuf_tensor(f"lp_xm{j}", [128, D], F32)) for j in range(4)]
    t1 = [e(nc.sbuf_tensor(f"lp_t1{j}", [128, D], F32)) for j in range(4)]
    t2 = [e(nc.sbuf_tensor(f"lp_t2{j}", [128, D], F32)) for j in range(4)]
    sqd = [e(nc.sbuf_tensor(f"lp_sqd{j}", [128, D], BF16)) for j in range(4)]
    n2g = e(nc.sbuf_tensor("lp_n2g", [128, D], F32))
    n2b = e(nc.sbuf_tensor("lp_n2b", [128, D], F32))
    l1g = e(nc.sbuf_tensor("lp_l1g", [128, D], F32))
    l1b = e(nc.sbuf_tensor("lp_l1b", [128, D], F32))
    ldw = p.sem("ldw")
    for buf, nm in ((n2g, "norm2_w"), (n2b, "norm2_b"), (l1g, "ln1_g"), (l1b, "ln1_b")):
        wv = p.dma("sync", ldw, lambda s, buf=buf, nm=nm: s.dma_start(
            out=buf[:], in_=io[nm].partition_broadcast(128)))
    p.wait("vector", ldw, wv)
    sem_ld = p.sem("lp_ld")
    sem_d = p.sem("lp_d")
    sem_a = p.sem("lp_a")
    sem_st = p.sem("st")
    stg = p.sem("stg")
    for g in range(NG):
        for j in range(4):
            tt = g * 4 + j
            p.dma("sync", sem_ld, lambda s, tt=tt, j=j: s.dma_start(
                out=yt[j][:], in_=io["ymoe"][tt * 128:(tt + 1) * 128, :]))
            p.dma("sync", sem_ld, lambda s, tt=tt, j=j: s.dma_start(
                out=x0t[j][:], in_=io["x"][tt * 128:(tt + 1) * 128, :]))
            c = p.dma("sync", sem_ld, lambda s, tt=tt, j=j: s.dma_start(
                out=abt[j][:], in_=io["attn_br"][tt * 128:(tt + 1) * 128, :]))
        p.wait("vector", sem_ld, c)
        p.wait("scalar", sem_ld, c)
        cout = ln_tiles(p, "lp", st, 0, sqd, xm, yt, t1, D, sem_d, sem_a)
        for j in range(4):
            tt = g * 4 + j
            p.wait("vector", sem_d, cout[j])
            cm_ = p.op("vector", lambda v, j=j: v.tensor_mul(t1[j][:], t1[j][:], n2g[:]),
                       inc=sem_d)
            p.wait("vector", sem_d, cm_)
            cm_ = p.op("vector", lambda v, j=j: v.tensor_add(t1[j][:], t1[j][:], n2b[:]),
                      inc=sem_d)
            p.wait("vector", sem_d, cm_)
            cm_ = p.op("vector", lambda v, j=j: v.tensor_add(t1[j][:], t1[j][:], x0t[j][:]),
                       inc=sem_d)
            p.wait("vector", sem_d, cm_)
            cr = p.op("vector", lambda v, j=j: v.tensor_add(t1[j][:], t1[j][:], abt[j][:]),
                      inc=sem_d, small=True)
            p.wait("sync", sem_d, cr)
            p.dma("sync", sem_st, lambda s, tt=tt, j=j: s.dma_start(
                out=io["xres"][tt * 128:(tt + 1) * 128, :], in_=t1[j][:]))
            p.wait("scalar", sem_d, cr)
        cout = ln_tiles(p, "lp", st, 0, sqd, xm, t1, t2, D, sem_d, sem_a)
        for j in range(4):
            tt = g * 4 + j
            p.wait("gpsimd", sem_d, cout[j])
            p.dma("gpsimd", stg, lambda gp, tt=tt, j=j: gp.dma_start(
                out=io["xn2_bf"][tt * 128:(tt + 1) * 128, :], in_=t2[j][:]))
            p.wait("scalar", sem_d, cout[j])
        p.wait("vector", "st", p.counts["st"])
        cout = ln_tiles(p, "lp", st, 0, sqd, xm, t2, t1, D, sem_d, sem_a)
        for j in range(4):
            tt = g * 4 + j
            p.wait("vector", sem_d, cout[j])
            cm_ = p.op("vector", lambda v, j=j: v.tensor_mul(t1[j][:], t1[j][:], l1g[:]),
                       inc=sem_d)
            p.wait("vector", sem_d, cm_)
            ch = p.op("vector", lambda v, j=j: v.tensor_add(t1[j][:], t1[j][:], l1b[:]),
                      inc=sem_d, small=True)
            p.wait("gpsimd", sem_d, ch)
            p.dma("gpsimd", stg, lambda gp, tt=tt, j=j: gp.dma_start(
                out=io["h1_bf"][tt * 128:(tt + 1) * 128, :], in_=t1[j][:]))
        p.barrier()


phase_l1prep.reads = ("ymoe", "x", "attn_br", "norm2_w", "norm2_b", "ln1_g", "ln1_b")
phase_l1prep.writes = ("xres", "xn2_bf", "h1_bf")



def phase_l1wcast(p, io, cst, scope):
    stg = p.sem("stg")
    for wname, dname in (("Wq", "wqT_src"), ("Wk", "wkT_src"), ("Wv", "wvT_src"),
                         ("Wo", "woT_src"), ):
        p.dma("gpsimd", stg, lambda gp, wname=wname, dname=dname: gp.dma_start(
            out=io[dname], in_=io[wname]))
    p.barrier()


phase_l1wcast.reads = ("Wq", "Wk", "Wv", "Wo")
phase_l1wcast.writes = ("wqT_src", "wkT_src", "wvT_src", "woT_src")



def phase_l1qkv(p, io, cst, scope):
    nc = p.nc
    e = scope.enter_context
    tag = "l1q"
    wsb = [e(nc.sbuf_tensor(f"{tag}_w{dc}", [128, 3 * D], BF16)) for dc in range(8)]
    xfm = [[e(nc.sbuf_tensor(f"{tag}_x{pr}{dc}", [128, 512], BF16)) for dc in range(8)]
           for pr in range(2)]
    stage = [e(nc.sbuf_tensor(f"{tag}_s{i}", [128, 512], BF16)) for i in range(6)]
    pms = [e(nc.psum_tensor(f"{tag}_p{i}", [128, 512], F32)) for i in range(4)]
    banks = Banks(p, pms)
    ldw = p.sem("ldw")
    wldv = None
    for dc in range(8):
        for mi, wn in enumerate(("wqT_src", "wkT_src", "wvT_src")):
            wldv = p.dma("sync", ldw, lambda s, dc=dc, mi=mi, wn=wn: s.dma_start(
                out=wsb[dc][:, mi * D:(mi + 1) * D],
                in_=io[wn][:, dc * 128:(dc + 1) * 128], transpose=True))
    sem_pe = p.sem(tag + "_pe")
    sem_ev = p.sem(tag + "_ev")
    las = [p.sem(tag + "_la"), p.sem(tag + "_lb")]
    slots = StSlots(p, tag, 6)
    pe_group_end = {}
    waited_w = False
    for g in range(NG):
        la = las[g % 2]
        xf = xfm[g % 2]
        if g >= 2:
            p.wait("sync", sem_pe, pe_group_end[g - 2])
        for dc in range(8):
            p.dma("sync", la, lambda s, g=g, dc=dc, xf=xf: s.dma_start(
                out=xf[dc][:], in_=io["xn2_bf"][g * 512:(g + 1) * 512,
                                                dc * 128:(dc + 1) * 128],
                transpose=True))
        ldv = p.counts[la]
        p.wait("tensor", la, ldv)
        if not waited_w:
            p.wait("tensor", ldw, wldv)
            waited_w = True
        si = 0
        for mc in range(16):
            b = banks.next()
            wcol = (0 if mc < 8 else 1) * D + (mc % 8) * 128
            cpe = mm_acc(p, b[0][:],
                         [(wsb[dc][:, wcol:wcol + 128], xf[dc][:])
                          for dc in range(8)], sem_pe)
            sj = si % 6
            si += 1
            slots.guard("scalar", sj)
            p.wait("scalar", sem_pe, cpe)
            cev = p.op("scalar", lambda s, sj=sj, b=b: s.activation(
                stage[sj][:], b[0][:], AF.Copy), inc=sem_ev)
            Banks.release(b, sem_ev, cev)
            dst = io["qf"] if mc < 8 else io["kf"]
            r = (mc % 8) * 128
            p.wait("sync", sem_ev, cev)
            slots.store(sj, lambda s, dst=dst, r=r, g=g, sj=sj: s.dma_start(
                out=dst[r:r + 128, g * 512:(g + 1) * 512], in_=stage[sj][:]))
        for tt in range(4):
            for half in range(2):
                b = banks.next()
                cpe = mm_acc(p, b[0][:],
                             [(xf[dc][:, tt * 128:(tt + 1) * 128],
                               wsb[dc][:, 2 * D + half * 512: 2 * D + (half + 1) * 512])
                              for dc in range(8)], sem_pe)
                sj = si % 6
                si += 1
                slots.guard("scalar", sj)
                p.wait("scalar", sem_pe, cpe)
                cev = p.op("scalar", lambda s, sj=sj, b=b: s.activation(
                    stage[sj][:], b[0][:], AF.Copy), inc=sem_ev)
                Banks.release(b, sem_ev, cev)
                row = (g * 4 + tt) * 128
                p.wait("sync", sem_ev, cev)
                slots.store(sj, lambda s, row=row, half=half, sj=sj: s.dma_start(
                    out=io["vt"][row:row + 128, half * 512:(half + 1) * 512],
                    in_=stage[sj][:]))
        pe_group_end[g] = p.counts[sem_pe]
    p.barrier()


phase_l1qkv.reads = ("xn2_bf", "wqT_src", "wkT_src", "wvT_src")
phase_l1qkv.writes = ("qf", "kf", "vt")



def phase_l1proj(p, io, cst, scope):
    nc = p.nc
    e = scope.enter_context
    tag = "l1p"
    wsb = [e(nc.sbuf_tensor(f"{tag}_w{dc}", [128, D], BF16)) for dc in range(8)]
    ofm = [[e(nc.sbuf_tensor(f"{tag}_o{pr}{dc}", [128, 512], BF16)) for dc in range(8)]
           for pr in range(2)]
    stage = [e(nc.sbuf_tensor(f"{tag}_s{i}", [128, 512], F32)) for i in range(4)]
    pms = [e(nc.psum_tensor(f"{tag}_p{i}", [128, 512], F32)) for i in range(4)]
    banks = Banks(p, pms)
    ldw = p.sem("ldw")
    wldv = None
    for dc in range(8):
        wldv = p.dma("sync", ldw, lambda s, dc=dc: s.dma_start(
            out=wsb[dc][:], in_=io["woT_src"][:, dc * 128:(dc + 1) * 128],
            transpose=True))
    sem_pe = p.sem(tag + "_pe")
    sem_ev = p.sem(tag + "_ev")
    las = [p.sem(tag + "_la"), p.sem(tag + "_lb")]
    slots = StSlots(p, tag, 4)
    pe_group_end = {}
    waited = False
    for g in range(NG):
        la = las[g % 2]
        of = ofm[g % 2]
        if g >= 2:
            p.wait("sync", sem_pe, pe_group_end[g - 2])
        for dc in range(8):
            p.dma("sync", la, lambda s, g=g, dc=dc, of=of: s.dma_start(
                out=of[dc][:], in_=io["of"][dc * 128:(dc + 1) * 128,
                                            g * 512:(g + 1) * 512]))
        ldv = p.counts[la]
        p.wait("tensor", la, ldv)
        if not waited:
            p.wait("tensor", ldw, wldv)
            waited = True
        si = 0
        for tt in range(4):
            for half in range(2):
                b = banks.next()
                cpe = mm_acc(p, b[0][:],
                             [(of[dc][:, tt * 128:(tt + 1) * 128],
                               wsb[dc][:, half * 512:(half + 1) * 512])
                              for dc in range(8)], sem_pe)
                sj = si % 4
                si += 1
                slots.guard("scalar", sj)
                p.wait("scalar", sem_pe, cpe)
                cev = p.op("scalar", lambda s, sj=sj, b=b: s.activation(
                    stage[sj][:], b[0][:], AF.Copy), inc=sem_ev)
                Banks.release(b, sem_ev, cev)
                row = (g * 4 + tt) * 128
                p.wait("sync", sem_ev, cev)
                slots.store(sj, lambda s, row=row, half=half, sj=sj: s.dma_start(
                    out=io["attn2"][row:row + 128, half * 512:(half + 1) * 512],
                    in_=stage[sj][:]))
        pe_group_end[g] = p.counts[sem_pe]
    p.barrier()


phase_l1proj.reads = ("of", "woT_src")
phase_l1proj.writes = ("attn2",)



def emit_quant_stats(p, io, scope, tag, w_name, nrow, ncol, sc1col):
    nc = p.nc
    e = scope.enter_context
    wt = [e(nc.sbuf_tensor(f"{tag}_wt{i}", [128, 512], F32)) for i in range(2)]
    stt = e(nc.sbuf_tensor(f"{tag}_st", [128, 80], F32))
    dmp = [e(nc.sbuf_tensor(f"{tag}_dmp{i}", [128, 512], BF16)) for i in range(2)]
    identf = e(nc.sbuf_tensor(f"{tag}_idf", [128, 128], F32))
    pq = e(nc.psum_tensor(f"{tag}_pq", [128, 512], F32))
    ntile = nrow * ncol // (128 * 512)
    assert ntile <= 64
    lds = [p.sem(tag + "_lda"), p.sem(tag + "_ldb")]
    ldi = p.sem(tag + "_ldi")
    sem_d = p.sem(tag + "_d")
    sem_a = p.sem(tag + "_a")
    sem_pe = p.sem(tag + "_pe")
    sem_mk = p.sem(tag + "_mk")
    cld = p.dma("sync", ldi, lambda s: s.dma_start(out=identf[:], in_=io["ident_f"]))
    w2d = io[w_name].rearrange("(a p) c -> a p c", p=128)
    npc = ncol // 512
    act_free = {}
    for i in range(ntile):
        r, cblk = i // npc, i % npc
        if act_free.get(i % 2) is not None:
            p.wait("sync", sem_a, act_free[i % 2])
        c = p.dma("sync", lds[i % 2], lambda s, r=r, cblk=cblk, i=i: s.dma_start(
            out=wt[i % 2][:], in_=w2d[r, :, cblk * 512:(cblk + 1) * 512]))
        p.wait("scalar", lds[i % 2], c)
        act_free[i % 2] = p.op("scalar", lambda s, i=i: s.activation(
            dmp[i % 2][:], wt[i % 2][:], AF.Abs, accum_out=stt[:, i:i + 1]),
            inc=sem_a, small=(i == ntile - 1))
    p.wait("vector", sem_a, act_free[(ntile - 1) % 2])
    cr = p.op("vector", lambda v: v.reduce_sum(stt[:, 64:65], stt[:, 0:ntile],
                                               axis=AX.X), inc=sem_d, small=True)
    p.wait("tensor", sem_d, cr)
    p.wait("tensor", ldi, cld)
    cpe = p.op("tensor", lambda t: t.transpose(pq[0:1, 0:128], stt[:, 64:65],
                                               identf[:]), inc=sem_pe)
    p.wait("vector", sem_pe, cpe)
    cs = p.op("vector", lambda v: v.reduce_sum(stt[0:1, 66:67], pq[0:1, 0:128],
                                               axis=AX.X), inc=sem_d, small=True)
    p.wait("scalar", sem_d, cs)
    cm = p.op("scalar", lambda s: s.activation(stt[0:1, 67:68], stt[0:1, 66:67],
                                               AF.Copy, scale=1.0 / (nrow * ncol)),
              inc=sem_a, small=True)
    p.wait("vector", sem_a, cm)
    cc = p.op("vector", lambda v: v.tensor_single_scalar(stt[0:1, 68:69],
                                                         stt[0:1, 67:68], 1e-5,
                                                         op=ALU.max),
              inc=sem_d, small=True)
    p.wait("sync", sem_d, cc)
    cs1 = p.dma("sync", sem_mk, lambda s: s.dma_start(
        out=io["sc1"][0:1, sc1col:sc1col + 1], in_=stt[0:1, 68:69]))
    return cs1, sem_mk


def emit_quantize_wT(p, io, scope, tag, w_name, nrow, ncol, sc1col, dst):
    nc = p.nc
    e = scope.enter_context
    wt = [e(nc.sbuf_tensor(f"{tag}_qw{i}", [128, 512], F32)) for i in range(2)]
    g1 = e(nc.sbuf_tensor(f"{tag}_g1", [128, 512], F32))
    g2 = e(nc.sbuf_tensor(f"{tag}_g2", [128, 512], F32))
    qb = [e(nc.sbuf_tensor(f"{tag}_qb{i}", [128, 512], BF16)) for i in range(2)]
    hcol = e(nc.sbuf_tensor(f"{tag}_hcol", [128, 2], F32))
    identb = e(nc.sbuf_tensor(f"{tag}_idb", [128, 128], BF16))
    pqb = e(nc.psum_tensor(f"{tag}_pqb", [128, 1024], BF16))
    lds = [p.sem(tag + "_qlda"), p.sem(tag + "_qldb")]
    ld = p.sem(tag + "_qld")
    sem_d = p.sem(tag + "_qd")
    sem_a = p.sem(tag + "_qa")
    sem_pe = p.sem(tag + "_qpe")
    sem_ev = p.sem(tag + "_qev")
    cld = p.dma("gpsimd", p.sem(tag + "_qldg"),
                lambda g_: g_.dma_start(out=identb[:], in_=io["ident_f"]))
    cmc = p.dma("sync", ld, lambda s: s.dma_start(
        out=hcol[:, 0:1], in_=io["sc1"][0:1, sc1col:sc1col + 1].partition_broadcast(128)))
    p.wait("vector", ld, cmc)
    ch = p.op("vector", lambda v: v.tensor_scalar(hcol[:, 0:1], hcol[:, 0:1], 0.5,
                                                  None, op0=ALU.mult),
              inc=sem_d, small=True)
    p.wait("vector", sem_d, ch)
    cn = p.op("vector", lambda v: v.tensor_scalar(hcol[:, 1:2], hcol[:, 0:1], -1.0,
                                                  None, op0=ALU.mult),
              inc=sem_d, small=True)
    p.wait("vector", sem_d, cn)
    p.wait("tensor", p.sem(tag + "_qldg"), cld)
    w2d = io[w_name].rearrange("(a p) c -> a p c", p=128)
    npc = ncol // 512
    dve_free = {}
    pe_qb_free = {}
    ev_free = None
    for i in range(nrow // 128 * npc):
        r, cblk = i // npc, i % npc
        if dve_free.get(i % 2) is not None:
            p.wait("sync", sem_d, dve_free[i % 2])
        c = p.dma("sync", lds[i % 2], lambda s, r=r, cblk=cblk, i=i: s.dma_start(
            out=wt[i % 2][:], in_=w2d[r, :, cblk * 512:(cblk + 1) * 512]))
        p.wait("vector", lds[i % 2], c)
        ca = p.op("vector", lambda v, i=i: v.tensor_scalar(
            g1[:], wt[i % 2][:], hcol[:, 0:1], None, op0=ALU.is_ge), inc=sem_d)
        p.wait("vector", sem_d, ca)
        cb = p.op("vector", lambda v, i=i: v.tensor_scalar(
            g2[:], wt[i % 2][:], hcol[:, 1:2], None, op0=ALU.is_le), inc=sem_d)
        p.wait("vector", sem_d, cb)
        if pe_qb_free.get(i % 2) is not None:
            p.wait("vector", sem_pe, pe_qb_free[i % 2])
        cq = p.op("vector", lambda v, i=i: v.tensor_sub(qb[i % 2][:], g1[:], g2[:]),
                  inc=sem_d, small=True)
        dve_free[i % 2] = p.counts[sem_d]
        p.wait("tensor", sem_d, cq)
        if ev_free is not None:
            p.wait("tensor", sem_ev, ev_free)
        for t4 in range(4):
            p.op("tensor", lambda t, i=i, t4=t4: t.transpose(
                pqb[:, t4 * 128:(t4 + 1) * 128],
                qb[i % 2][:, t4 * 128:(t4 + 1) * 128], identb[:]),
                inc=(sem_pe if t4 == 3 else None))
        cpe = p.counts[sem_pe]
        pe_qb_free[i % 2] = cpe
        p.wait("scalar", sem_pe, cpe)
        cev = None
        for t4 in range(4):
            cchunk = cblk * 4 + t4
            cev = p.op("scalar", lambda s, r=r, cchunk=cchunk, t4=t4: s.activation(
                dst[:, cchunk * nrow + r * 128: cchunk * nrow + (r + 1) * 128],
                pqb[:, t4 * 128:(t4 + 1) * 128], AF.Copy), inc=sem_ev)
        ev_free = cev
    return p.counts[sem_ev]



def phase_l1ffna(p, io, cst, scope):
    nc = p.nc
    e = scope.enter_context
    c1, s1 = emit_quant_stats(p, io, scope, "qs1", "w1", HID, D, 4)
    c2, s2 = emit_quant_stats(p, io, scope, "qs2", "w2", D, HID, 5)
    p.wait("sync", s1, c1)
    p.wait("sync", s2, c2)
    p.barrier()
    w1q = e(nc.sbuf_tensor("fa_w1q", [128, 8 * HID], BF16))
    emit_quantize_wT(p, io, scope, "fa", "w1", HID, D, 4, w1q)
    p.barrier()
    hfm = [[e(nc.sbuf_tensor(f"fa_h{pr}{dc}", [128, 512], BF16)) for dc in range(8)]
           for pr in range(2)]
    b1bc = e(nc.sbuf_tensor("fa_b1bc", [128, HID], F32))
    g2bc = e(nc.sbuf_tensor("fa_g2bc", [128, HID], F32))
    bb2bc = e(nc.sbuf_tensor("fa_bb2bc", [128, HID], F32))
    mcc = e(nc.sbuf_tensor("fa_mcc", [128, 1], F32))
    htok = e(nc.sbuf_tensor("fa_htok", [128, HID], F32))
    hG = e(nc.sbuf_tensor("fa_hG", [128, HID], F32))
    h2b = e(nc.sbuf_tensor("fa_h2b", [128, HID], BF16))
    hstage = [e(nc.sbuf_tensor(f"fa_hs{i}", [128, 1024], BF16)) for i in range(2)]
    stt = e(nc.sbuf_tensor("fa_st", [128, 32], F32))
    sqd1 = e(nc.sbuf_tensor("fa_sqd", [128, HID], BF16))
    identb = e(nc.sbuf_tensor("fa_idb2", [128, 128], BF16))
    pms = [e(nc.psum_tensor(f"fa_p{i}", [128, 512], F32)) for i in range(2)]
    pqb = e(nc.psum_tensor("fa_pqb2", [128, 1024], BF16))
    ldw = p.sem("ldw")
    ldg = p.sem("fa_ldg")
    p.dma("sync", ldw, lambda s: s.dma_start(
        out=b1bc[:], in_=io["b1"].partition_broadcast(128)))
    p.dma("sync", ldw, lambda s: s.dma_start(
        out=g2bc[:], in_=io["ln2_g"].partition_broadcast(128)))
    p.dma("sync", ldw, lambda s: s.dma_start(
        out=bb2bc[:], in_=io["ln2_b"].partition_broadcast(128)))
    wldv = p.dma("sync", ldw, lambda s: s.dma_start(
        out=mcc[:], in_=io["sc1"][0:1, 4:5].partition_broadcast(128)))
    cldb = p.dma("gpsimd", ldg, lambda g_: g_.dma_start(out=identb[:],
                                                        in_=io["ident_f"]))
    sem_pe = p.sem("fa_pe")
    sem_d = p.sem("fa_d")
    sem_a = p.sem("fa_a")
    sem_ev = p.sem("fa_ev")
    las = [p.sem("fa_la"), p.sem("fa_lb")]
    slots = StSlots(p, "fa", 2)
    pe_group_end = {}
    evh_bank = {}
    tr_free = None
    waited = False
    for g in range(NG):
        la = las[g % 2]
        hf = hfm[g % 2]
        if g >= 2:
            p.wait("sync", sem_pe, pe_group_end[g - 2])
        for dc in range(8):
            p.dma("sync", la, lambda s, g=g, dc=dc, hf=hf: s.dma_start(
                out=hf[dc][:], in_=io["h1_bf"][g * 512:(g + 1) * 512,
                                               dc * 128:(dc + 1) * 128],
                transpose=True))
        ldv = p.counts[la]
        p.wait("tensor", la, ldv)
        if not waited:
            p.wait("tensor", ldw, wldv)
            p.wait("vector", ldw, wldv)
            p.wait("tensor", ldg, cldb)
            waited = True
        for j in range(4):
            tt = g * 4 + j
            for hg in range(8):
                b = pms[hg % 2]
                if evh_bank.get(hg % 2) is not None:
                    p.wait("tensor", sem_d, evh_bank[hg % 2])
                cpe = mm_acc(p, b[:],
                             [(hf[dc][:, j * 128:(j + 1) * 128],
                               w1q[:, dc * HID + hg * 512: dc * HID + (hg + 1) * 512])
                              for dc in range(8)], sem_pe)
                p.wait("vector", sem_pe, cpe)
                evh_bank[hg % 2] = p.op(
                    "vector", lambda v, hg=hg, b=b: v.scalar_tensor_tensor(
                        htok[:, hg * 512:(hg + 1) * 512], b[:], mcc[:, 0:1],
                        b1bc[:, hg * 512:(hg + 1) * 512], op0=ALU.mult, op1=ALU.add),
                    inc=sem_d, small=(hg == 7))
            p.wait("scalar", sem_d, evh_bank[1])
            cgel = p.op("scalar", lambda s: s.activation(hG[:], htok[:], AF.Gelu),
                        inc=sem_a, small=True)
            p.wait("vector", sem_a, cgel)
            p.wait("scalar", sem_a, cgel)
            cout = ln_tiles(p, "fa", stt, 0, [sqd1], [htok], [hG], [hG], HID,
                            sem_d, sem_a)
            p.wait("vector", sem_d, cout[0])
            cm_ = p.op("vector", lambda v: v.tensor_mul(hG[:], hG[:], g2bc[:]),
                       inc=sem_d)
            p.wait("vector", sem_d, cm_)
            ch2 = p.op("vector", lambda v: v.tensor_add(hG[:], hG[:], bb2bc[:]),
                       inc=sem_d)
            p.wait("vector", sem_d, ch2)
            ch2 = p.op("vector", lambda v: v.tensor_copy(h2b[:], hG[:]),
                       inc=sem_d, small=True)
            p.wait("tensor", sem_d, ch2)
            for g8 in range(4):
                if tr_free is not None:
                    p.wait("tensor", sem_ev, tr_free)
                for t8 in range(8):
                    p.op("tensor", lambda t, g8=g8, t8=t8: t.transpose(
                        pqb[:, t8 * 128:(t8 + 1) * 128],
                        h2b[:, (g8 * 8 + t8) * 128:(g8 * 8 + t8 + 1) * 128],
                        identb[:]),
                        inc=(sem_pe if t8 == 7 else None))
                cpe = p.counts[sem_pe]
                sj = g8 % 2
                slots.guard("scalar", sj)
                p.wait("scalar", sem_pe, cpe)
                cev = p.op("scalar", lambda s, sj=sj: s.activation(
                    hstage[sj][:], pqb[:], AF.Copy), inc=sem_ev)
                tr_free = cev
                p.wait("sync", sem_ev, cev)
                slots.store(sj, lambda s, tt=tt, g8=g8, sj=sj: s.dma_start(
                    out=io["h2f"][tt, g8 * 8:(g8 + 1) * 8, :, :].transpose([1, 0, 2]),
                    in_=hstage[sj][:].rearrange("p (a c) -> p a c", c=128)))
        pe_group_end[g] = p.counts[sem_pe]
    p.barrier()


phase_l1ffna.reads = ("w1", "w2", "b1", "ln2_g", "ln2_b", "h1_bf", "ident_f",
                      "sc1")
phase_l1ffna.writes = ("h2f", "sc1")



def phase_l1ffnb(p, io, cst, scope):
    nc = p.nc
    e = scope.enter_context
    w2q = e(nc.sbuf_tensor("fb_w2q", [128, 32 * D], BF16))
    emit_quantize_wT(p, io, scope, "fb", "w2", D, HID, 5, w2q)
    p.barrier()
    h2fm = [e(nc.sbuf_tensor(f"fb_h{pr}", [128, HID], BF16)) for pr in range(2)]
    b2bc = e(nc.sbuf_tensor("fb_b2bc", [128, D], F32))
    mcc = e(nc.sbuf_tensor("fb_mcc", [128, 1], F32))
    xrt = [e(nc.sbuf_tensor(f"fb_xr{i}", [128, D], F32)) for i in range(2)]
    a2t = [e(nc.sbuf_tensor(f"fb_a2{i}", [128, D], F32)) for i in range(2)]
    ost = [e(nc.sbuf_tensor(f"fb_o{i}", [128, D], F32)) for i in range(2)]
    pms = [e(nc.psum_tensor(f"fb_p{i}", [128, 512], F32)) for i in range(4)]
    banks = Banks(p, pms)
    ldw = p.sem("ldw")
    p.dma("sync", ldw, lambda s: s.dma_start(
        out=b2bc[:], in_=io["b2"].partition_broadcast(128)))
    wldv = p.dma("sync", ldw, lambda s: s.dma_start(
        out=mcc[:], in_=io["sc1"][0:1, 5:6].partition_broadcast(128)))
    sem_pe = p.sem("fb_pe")
    sem_d = p.sem("fb_d")
    lh = [p.sem("fb_lha"), p.sem("fb_lhb")]
    lx = [p.sem("fb_lxa"), p.sem("fb_lxb")]
    slots = StSlots(p, "fb", 2)
    pe_tile_end = {}
    dve_tile_end = {}
    p.wait("vector", ldw, wldv)
    for tt in range(NT):
        pr = tt % 2
        if tt >= 2:
            p.wait("sync", sem_pe, pe_tile_end[tt - 2])
        ch = p.dma("sync", lh[pr], lambda s, tt=tt, pr=pr: s.dma_start(
            out=h2fm[pr][:].rearrange("pp (a c) -> pp a c", c=128),
            in_=io["h2f"][tt, :, :, :].transpose([1, 0, 2])))
        if tt >= 2:
            p.wait("sync", sem_d, dve_tile_end[tt - 2])
        p.dma("sync", lx[pr], lambda s, tt=tt, pr=pr: s.dma_start(
            out=xrt[pr][:], in_=io["xres"][tt * 128:(tt + 1) * 128, :]))
        cx = p.dma("sync", lx[pr], lambda s, tt=tt, pr=pr: s.dma_start(
            out=a2t[pr][:], in_=io["attn2"][tt * 128:(tt + 1) * 128, :]))
        p.wait("tensor", lh[pr], ch)
        slots.guard("vector", pr)
        for half in range(2):
            b = banks.next()
            cpe = mm_acc(p, b[0][:],
                         [(h2fm[pr][:, hc * 128:(hc + 1) * 128],
                           w2q[:, hc * D + half * 512: hc * D + (half + 1) * 512])
                          for hc in range(32)], sem_pe)
            p.wait("vector", sem_pe, cpe)
            p.wait("vector", lx[pr], cx)
            co = p.op("vector", lambda v, pr=pr, half=half, b=b: v.scalar_tensor_tensor(
                ost[pr][:, half * 512:(half + 1) * 512], b[0][:], mcc[:, 0:1],
                b2bc[:, half * 512:(half + 1) * 512], op0=ALU.mult, op1=ALU.add),
                inc=sem_d)
            Banks.release(b, sem_d, co)
            p.wait("vector", sem_d, co)
            co = p.op("vector", lambda v, pr=pr, half=half: v.tensor_add(
                ost[pr][:, half * 512:(half + 1) * 512],
                ost[pr][:, half * 512:(half + 1) * 512],
                xrt[pr][:, half * 512:(half + 1) * 512]), inc=sem_d)
            p.wait("vector", sem_d, co)
            co = p.op("vector", lambda v, pr=pr, half=half: v.tensor_add(
                ost[pr][:, half * 512:(half + 1) * 512],
                ost[pr][:, half * 512:(half + 1) * 512],
                a2t[pr][:, half * 512:(half + 1) * 512]),
                inc=sem_d, small=(half == 1))
        cfin = p.counts[sem_d]
        dve_tile_end[tt] = cfin
        p.wait("sync", sem_d, cfin)
        slots.store(pr, lambda s, tt=tt, pr=pr: s.dma_start(
            out=io["out"][tt * 128:(tt + 1) * 128, :], in_=ost[pr][:]))
        pe_tile_end[tt] = p.counts[sem_pe]
    p.barrier()


phase_l1ffnb.reads = ("w2", "b2", "h2f", "xres", "attn2", "ident_f", "sc1")
phase_l1ffnb.writes = ("out",)




PHASES_ALL = [phase_ln0, phase_rqkv, phase_rattn, phase_rproj,
              phase_mqkv, phase_mattn, phase_mproj, phase_norm1, phase_moe,
              phase_l1prep, phase_l1wcast, phase_l1qkv, phase_l1attn,
              phase_l1proj, phase_l1ffna, phase_l1ffnb]
GROUPS = [PHASES_ALL]


def build_group(phases, produced_before):
    nc = bass.Bass("TRN2", target_bir_lowering=False, debug=False)
    reads = set()
    writes = set()
    for ph in phases:
        reads |= set(ph.reads)
        writes |= set(ph.writes)
    io = {}
    ext_in = []
    ext_out = []
    for name in sorted(reads | writes):
        if name in INPUT_SPECS:
            shape, dt = INPUT_SPECS[name]
            io[name] = nc.dram_tensor(name, list(shape), dt, kind="ExternalInput").ap()
            ext_in.append(name)
        else:
            shape, dt = SCRATCH[name]
            if name in reads and name in produced_before and name not in writes:
                kind = "ExternalInput"
                ext_in.append(name)
            elif name in writes:
                kind = "ExternalOutput"
                ext_out.append(name)
            else:
                kind = "Internal"
            io[name] = nc.dram_tensor(name, list(shape), dt, kind=kind).ap()
    p = Prog(nc)
    with ExitStack() as stack:
        p.dmy = {e: stack.enter_context(nc.sbuf_tensor(f"dmy_{e}", [128, 2], F32))
                 for e in ("vector", "scalar", "gpsimd")}
        cst = None
        for ph in phases:
            with ExitStack() as scope:
                ph(p, io, cst, scope)
                p.barrier()
        p.finish()
        sems_needed = [s for s, c in p.counts.items() if c > 0 or s in p.dma_sems]
        with ExitStack() as semstack:
            h = {s: semstack.enter_context(nc.semaphore(s)) for s in sems_needed}
            with nc.Block() as block:
                p.build(block, h)
    return nc, ext_in, ext_out


def kernel(**inputs):
    full = prepare_inputs(inputs)
    consts = make_consts()
    produced = set()
    scratch_vals = [dict() for _ in range(NCORES)]
    final = None
    for phases in GROUPS:
        nc, ext_in, ext_out = build_group(phases, produced)
        in_maps = []
        for c in range(NCORES):
            m = {}
            for name in ext_in:
                if name in consts:
                    m[name] = consts[name]
                elif name in full:
                    m[name] = full[name] if not isinstance(full[name], list) else full[name][c]
                else:
                    m[name] = scratch_vals[c][name]
            in_maps.append(m)
        res = run_bass_kernel_spmd(nc, in_maps, core_ids=list(range(NCORES)))
        for c in range(NCORES):
            for name in ext_out:
                scratch_vals[c][name] = res.results[c][name]
        produced |= set(ext_out)
    out = np.stack([scratch_vals[c][OUTPUT_NAME] for c in range(NCORES)], axis=0)
    return out


OUTPUT_NAME = "out"


def prepare_inputs(inputs):
    f = {}
    f["x"] = [np.ascontiguousarray(inputs["L0_attn_x"][c]) for c in range(NCORES)]
    ren = {
        "Wqkv": "L0_attn_Wqkv", "bqkv": "L0_attn_bqkv", "Wproj": "L0_attn_Wproj",
        "bproj": "L0_attn_bproj", "gamma": "L0_attn_gamma",
        "qkv_w": "L0_ffn_qkv_w", "qkv_b": "L0_ffn_qkv_b", "out_w": "L0_ffn_out_w",
        "out_b": "L0_ffn_out_b", "norm1_w": "L0_ffn_norm1_w", "norm1_b": "L0_ffn_norm1_b",
        "sw1": "L0_ffn_sw1", "sb1": "L0_ffn_sb1", "sw2": "L0_ffn_sw2", "sb2": "L0_ffn_sb2",
        "rw1": "L0_ffn_rw1", "rb1": "L0_ffn_rb1", "rw2": "L0_ffn_rw2", "rb2": "L0_ffn_rb2",
        "centroid": "L0_ffn_centroid", "route_bias": "L0_ffn_route_bias",
        "norm2_w": "L0_ffn_norm2_w", "norm2_b": "L0_ffn_norm2_b",
        "Wq": "L1_attn_Wq", "Wk": "L1_attn_Wk", "Wv": "L1_attn_Wv", "Wo": "L1_attn_Wo",
        "ln1_g": "L1_ffn_ln1_g", "ln1_b": "L1_ffn_ln1_b", "w1": "L1_ffn_w1",
        "b1": "L1_ffn_b1", "ln2_g": "L1_ffn_ln2_g", "ln2_b": "L1_ffn_ln2_b",
        "w2": "L1_ffn_w2", "b2": "L1_ffn_b2",
    }
    for k, src in ren.items():
        a = np.asarray(inputs[src])
        shape = INPUT_SPECS[k][0]
        f[k] = np.ascontiguousarray(a.reshape(shape))
    return f
